# version 11
# baseline (speedup 1.0000x reference)
import os
import numpy as np
import concourse.bass as bass
import concourse.mybir as mybir
from concourse.bass_utils import run_bass_kernel_spmd
from contextlib import ExitStack

F32 = mybir.dt.float32
BF16 = mybir.dt.bfloat16
U8 = mybir.dt.uint8
AF = mybir.ActivationFunctionType
ALU = mybir.AluOpType
AX = mybir.AxisListType

STAGE = int(os.environ.get("MK_STAGE", "3"))
NEXP = int(os.environ.get("MK_NEXP", "8"))
NOGATE = int(os.environ.get("MK_NOGATE", "0"))
STOP = os.environ.get("MK_STOP", "")
P6N = int(os.environ.get("MK_P6N", "32"))

NCORE = 8
D = 1024
NCH = 8
TOK = 2048
HALO = 128
TTOK = TOK + HALO
G = 17
NE = 8
INV_D = 1.0 / D
MASKVAL = -8.0e9
NLNBUF = 4


class Sem:
    def __init__(self, ctx, nc, name):
        self.h = ctx.enter_context(nc.semaphore(name))
        self.n = 0
        self.em = 0

    def inc(self, inst, k=1):
        inst.then_inc(self.h, k)
        self.n += k
        self.em = max(self.em, self.n)
        return self.n

    def dma(self, eng, out, in_, **kw):
        eng.wait_ge(self.h, self.n)
        return self.inc(eng.dma_start(out=out, in_=in_, **kw), 16)

    def reserve(self, k):
        self.n += k
        return self.n


def build():
    nc = bass.Bass("TRN2", target_bir_lowering=False, debug=False,
                   detect_race_conditions=False)

    xin = nc.dram_tensor("xin", [TTOK, D], F32, kind="ExternalInput").ap()
    wq_d = nc.dram_tensor("wq", [D, D], F32, kind="ExternalInput").ap()
    wk_d = nc.dram_tensor("wk", [D, D], F32, kind="ExternalInput").ap()
    wv_d = nc.dram_tensor("wv", [D, D], F32, kind="ExternalInput").ap()
    wo_d = nc.dram_tensor("wo", [D, D], F32, kind="ExternalInput").ap()
    expT_d = nc.dram_tensor("expT", [NE, D, D], F32, kind="ExternalInput").ap()
    gT_d = nc.dram_tensor("gT", [D, NE], F32, kind="ExternalInput").ap()
    bo_d = nc.dram_tensor("bo", [1, D], F32, kind="ExternalInput").ap()
    ident_d = nc.dram_tensor("ident", [128, 128], F32, kind="ExternalInput").ap()
    ones_d = nc.dram_tensor("onesc", [128, 1], F32, kind="ExternalInput").ap()
    onesr_d = nc.dram_tensor("onesr", [1, 128], F32, kind="ExternalInput").ap()
    onese_d = nc.dram_tensor("onese", [128, 1], F32, kind="ExternalInput").ap()
    masks_d = nc.dram_tensor("masks", [2, 128, 256], F32, kind="ExternalInput").ap()
    pref_d = nc.dram_tensor("prefm", [128, NE], F32, kind="ExternalInput").ap()
    out_d = nc.dram_tensor("out", [TOK, D], F32, kind="ExternalOutput").ap()
    dbg_d = nc.dram_tensor("dbg", [128, 1024], F32, kind="ExternalOutput").ap()
    x1sp = nc.dram_tensor("x1spill", [TOK, D], F32).ap()
    ag_in = nc.dram_tensor("ag_in", [128, 390], F32)
    ag_out = nc.dram_tensor("ag_out", [128 * NCORE, 390], F32)

    ctx = ExitStack()
    with ctx:
        E = ctx.enter_context

        R1 = E(nc.sbuf_tensor("R1", [128, 73728], U8))
        r1b = R1.ap().bitcast(BF16)
        r1f = R1.ap().bitcast(F32)
        x_res = r1f[:, 0:17408].rearrange("p (g d) -> p g d", d=D)
        kg = r1b[:, 0:12288].rearrange("p (g d) -> p g d", d=768)
        vg = r1b[:, 12288:24576].rearrange("p (g d) -> p g d", d=768)
        qs_fm = r1b[:, 24576:36864].rearrange("p (j t) -> p j t", t=TOK)
        x2 = r1f[:, 0:16384].rearrange("p (g d) -> p g d", d=D)

        A = E(nc.sbuf_tensor("A", [128, 34816], U8))
        ab = A.ap().bitcast(BF16)
        xn17 = ab.rearrange("p (c t) -> p c t", t=TTOK)
        afm = ab[:, 0:16384].rearrange("p (c t) -> p c t", t=TOK)

        wbuf = [E(nc.sbuf_tensor(f"wb{i}", [128, NCH, 512], BF16))
                for i in range(3)]
        qloc = E(nc.sbuf_tensor("qloc", [128, 2, TOK], BF16))
        kloc = E(nc.sbuf_tensor("kloc", [128, 2, TTOK], BF16))
        vloc = E(nc.sbuf_tensor("vloc", [128, G, 256], BF16))
        xn_tm = E(nc.sbuf_tensor("xn_tm", [128, 4, D], BF16))
        sqscr = E(nc.sbuf_tensor("sqscr", [128, D], F32))
        scr_a = E(nc.sbuf_tensor("scr_a", [128, 2, 768], F32))

        qstm = E(nc.sbuf_tensor("qstm", [128, 2, 768], BF16))
        sqdmp = qstm.ap().rearrange("p a b -> p (a b)")[:, 0:1024]
        vstage = E(nc.sbuf_tensor("vstage", [128, 768], BF16))
        x_roll = E(nc.sbuf_tensor("x_roll", [128, 2, D], F32))
        scr_b = x_roll.ap()[:, :, 0:768]
        ctxs2 = E(nc.sbuf_tensor("ctxs2", [128, 2, 6, 128], BF16))
        ctxf = E(nc.sbuf_tensor("ctxf", [128, 6, 64], F32))
        w_full = E(nc.sbuf_tensor("w_full", [128, G, NE], F32))
        stats = E(nc.sbuf_tensor("stats", [128, 16], F32))
        gstat = E(nc.sbuf_tensor("gstat", [128, 8], F32))
        masks = E(nc.sbuf_tensor("masks_sb", [128, 2, 256], F32))
        probs_f = E(nc.sbuf_tensor("probs_f", [128, 2, 256], F32))
        probs_b = E(nc.sbuf_tensor("probs_b", [128, 2, 256], BF16))
        pts = E(nc.sbuf_tensor("pts", [128, 2, 128], BF16))
        ident = E(nc.sbuf_tensor("identb", [128, 128], BF16))
        onesc = E(nc.sbuf_tensor("onescb", [128, 1], BF16))
        onesr = E(nc.sbuf_tensor("onesrb", [1, 128], BF16))
        onese = E(nc.sbuf_tensor("oneseb", [128, 1], BF16))
        gT_sb = E(nc.sbuf_tensor("gTsb", [128, NCH, NE], BF16))
        bo_sb = E(nc.sbuf_tensor("bosb", [1, D], BF16))
        pref = E(nc.sbuf_tensor("prefsb", [128, NE], F32))
        tsb = E(nc.sbuf_tensor("tsb", [128, 6, 65], F32))
        tbuf = E(nc.sbuf_tensor("tbuf", [128, 6, 65], F32))
        carry = E(nc.sbuf_tensor("carry", [128, 6, 65], F32))
        rsg = E(nc.sbuf_tensor("rsg", [128, 12], F32))
        zrow = E(nc.sbuf_tensor("zrow", [128, 8], F32))

        sd = Sem(ctx, nc, "sd")
        gw = Sem(ctx, nc, "gw")
        cc = Sem(ctx, nc, "cc")
        tS = Sem(ctx, nc, "tS")
        vS = Sem(ctx, nc, "vS")
        sS = Sem(ctx, nc, "sS")

        block = E(nc.Block())

        wload_plans = []
        wslot_rr = [0]
        wslot_tenant = {}

        def plan_wloads(loads, slots=(0, 1, 2)):
            res = {}
            plan = []
            for key, ap in loads:
                slot = slots[wslot_rr[0] % len(slots)]
                wslot_rr[0] += 1
                free_ref = wslot_tenant.get(slot)
                tgt = gw.reserve(16)
                myref = [None, None]
                wslot_tenant[slot] = myref
                plan.append((key, ap, slot, tgt, free_ref))
                res[key] = (slot, tgt, myref)
            wload_plans.append(plan)
            return res

        def emit_wload_body(plan):
            @block.gpsimd
            def _(g):
                for key, ap, slot, tgt, free_ref in plan:
                    if free_ref is not None:
                        assert free_ref[0] is not None, f"free target unset {key}"
                        g.wait_ge(free_ref[0].h, free_ref[1])
                    g.wait_ge(gw.h, tgt - 16)
                    g.dma_start(
                        out=wbuf[slot][:, :, :],
                        in_=ap.rearrange("(c p) d -> p c d", p=128)
                    ).then_inc(gw.h, 16)
                    gw.em = max(gw.em, tgt)

        xload_tgt = []

        @block.sync
        def _(sy):
            for i in range(5):
                g0, g1 = i * 4, min((i + 1) * 4, G)
                if g0 >= g1:
                    break
                t = sd.dma(sy, x_res[:, g0:g1, :],
                           xin[g0 * 128:g1 * 128, :].rearrange(
                               "(g p) d -> p g d", p=128))
                xload_tgt.extend([t] * (g1 - g0))
            sd.dma(sy, masks[:, :, :], masks_d.rearrange("m p k -> p m k"))
            sd.dma(sy, pref[:, :], pref_d)
        misc_ld = sd.n

        @block.gpsimd
        def _(g):
            gw.dma(g, ident[:, :], ident_d)
            gw.dma(g, onesc[:, :], ones_d)
            gw.dma(g, onesr[:, :], onesr_d)
            gw.dma(g, onese[:, :], onese_d)
            gw.dma(g, gT_sb[:, :, :], gT_d.rearrange("(c p) e -> p c e", p=128))
            gw.dma(g, bo_sb[:, :], bo_d)
        consts_tgt = gw.n

        eps_tgt = [None]

        @block.vector
        def _(v):
            v.memset(ctxs2.ap().bitcast(U8), 0)
            v.memset(vstage.ap().bitcast(U8), 0)
            vS.inc(v.memset(stats[:, 6:7], 1e-5))
            eps_tgt[0] = vS.n

        @block.scalar
        def _(s):
            from concourse.hw_specs import get_activation_tables
            need = {AF.Exp, AF.Ln, AF.Square, AF.Identity}
            for idx, funcs in enumerate(get_activation_tables(nc.m.arch).values()):
                if need.issubset(set(funcs)):
                    s.add_instruction(mybir.InstLoadActFuncSet(
                        name=nc.get_next_instruction_name(),
                        act_func_set_id=idx, ins=[], outs=[]))
                    break
            else:
                raise RuntimeError("no act table set with all needed funcs")

        wslot_rr[0] = 0
        ffn1_loads = plan_wloads(
            [((e, h), expT_d[e, :, 512 * h:512 * h + 512]) for e in range(NE)
             for h in range(2)])

        def emit_sln(g, src, extra_s_wait, buf):
            s = nc.scalar
            s.wait_ge(vS.h, eps_tgt[0])
            if extra_s_wait is not None:
                s.wait_ge(extra_s_wait[0].h, extra_s_wait[1])
            tv = sS.inc(s.activation(sqdmp[:, :], src, AF.Identity,
                                     accum_out=stats[:, 0:1]))
            s.wait_ge(sS.h, tv)
            tv = sS.inc(s.activation(stats[:, 1:2], stats[:, 0:1], AF.Identity,
                                     scale=-INV_D))
            s.wait_ge(sS.h, tv)
            tv = sS.inc(s.activation(sqscr[:, :], src, AF.Identity,
                                     bias=stats[:, 1:2]))
            s.wait_ge(sS.h, tv)
            tv = sS.inc(s.activation(sqdmp[:, :], sqscr[:, :], AF.Square,
                                     accum_out=stats[:, 2:3]))
            s.wait_ge(sS.h, tv)
            tv = sS.inc(s.activation(stats[:, 3:4], stats[:, 2:3], AF.Identity,
                                     scale=INV_D, bias=stats[:, 6:7]))
            s.wait_ge(sS.h, tv)
            tv = sS.inc(s.activation(stats[:, 4:5], stats[:, 3:4], AF.Ln))
            s.wait_ge(sS.h, tv)
            tv = sS.inc(s.activation(stats[:, 5:6], stats[:, 4:5], AF.Exp,
                                     scale=-0.5))
            s.wait_ge(sS.h, tv)
            tv = sS.inc(s.activation(xn_tm[:, buf, :], sqscr[:, :],
                                     AF.Identity, scale=stats[:, 5:6]))
            return tv

        def emit_ln(groups, src_fn, xn_view, tok_off, pT, extra_v_wait=None,
                    extra_s_wait=None, spill=False):
            v0, s0, t0 = vS.n, sS.n, tS.n
            n = len(groups)
            ln_s = {}

            @block.scalar
            def _(s):
                for i, g in enumerate(groups):
                    w = extra_s_wait(g) if extra_s_wait is not None else None
                    if i >= NLNBUF:
                        s.wait_ge(tS.h, t0 + (i - NLNBUF) + 1)
                    ln_s[i] = emit_sln(g, src_fn(g), w, i % NLNBUF)

            @block.tensor
            def _(t):
                t.wait_ge(gw.h, consts_tgt)
                for i, g in enumerate(groups):
                    t.wait_ge(sS.h, s0 + 8 * i + 8)
                    if i >= 2:
                        t.wait_ge(vS.h, v0 + (i - 2) + 1)
                    for c in range(NCH):
                        ins = t.transpose(pT[:, i % 2, c, :],
                                          xn_tm[:, i % NLNBUF,
                                                c * 128:(c + 1) * 128],
                                          ident[:, :])
                    tS.inc(ins)

            @block.vector
            def _(v):
                for i, g in enumerate(groups):
                    v.wait_ge(tS.h, t0 + i + 1)
                    vS.inc(v.tensor_copy(
                        xn_view[:, :, tok_off(g):tok_off(g) + 128],
                        pT[:, i % 2, :, :]))

            if spill:
                @block.sync
                def _(sy):
                    for i, g in enumerate(groups):
                        if g == 0:
                            continue
                        sy.wait_ge(sS.h, ln_s[i])
                        sd.dma(sy, x1sp[(g - 1) * 128:g * 128, :], src_fn(g))
            ln_t = {i: t0 + i + 1 for i in range(n)}
            ln_v = {i: v0 + i + 1 for i in range(n)}
            return ln_s, ln_t, ln_v

        def emit_lnffn(groups, src_fn, xn_view, tok_off, res_fn, loads,
                       pY, pL, pT, extra_s_wait=None, spill=False):
            n = len(groups)
            ln_s = {}
            ln_t = {}
            ln_v = {}
            f0_t = {}
            f0_v = {}
            fvg = {}
            sg = {}
            sl00, sl01 = loads[(0, 0)], loads[(0, 1)]

            def e0_mms(t, j):
                g = groups[j]
                if j == 0:
                    t.wait_ge(gw.h, max(sl00[1], sl01[1]))
                t.wait_ge(vS.h, ln_v[j])
                if j >= 2:
                    t.wait_ge(vS.h, f0_v[j - 2])
                par = j % 2
                to = tok_off(g)
                for c in range(NCH):
                    lhs = xn_view[:, c, to:to + 128]
                    t.matmul(pY[:, par, 0, :], lhs, wbuf[sl00[0]][:, c, :],
                             start=(c == 0), stop=(c == NCH - 1))
                    t.matmul(pY[:, par, 1, :], lhs, wbuf[sl01[0]][:, c, :],
                             start=(c == 0), stop=(c == NCH - 1))
                    ins = t.matmul(pL[:, par, 0:8], lhs, gT_sb[:, c, :],
                                   start=(c == 0), stop=(c == NCH - 1))
                tS.inc(ins)
                f0_t[j] = tS.n

            def gating_part1(v, j):
                par = j % 2
                pl = pL[:, par, 0:8]
                v.wait_ge(tS.h, f0_t[j])
                tv = vS.inc(v.reduce_max(stats[:, 8:9], pl, axis=AX.X))
                v.wait_ge(vS.h, tv)
                tv = vS.inc(v.tensor_tensor(
                    zrow[:, :], pl, stats[:, 8:9].broadcast_to([128, NE]),
                    op=ALU.is_ge))
                v.wait_ge(vS.h, tv)
                tv = vS.inc(v.scalar_tensor_tensor(
                    w_full[:, groups[j], :], zrow[:, :], -1e30, pl,
                    op0=ALU.mult, op1=ALU.add))
                v.wait_ge(vS.h, tv)
                tv = vS.inc(v.reduce_max(stats[:, 9:10], w_full[:, groups[j], :],
                                         axis=AX.X))
                v.wait_ge(vS.h, tv)
                vS.inc(v.tensor_scalar_mul(stats[:, 10:11], stats[:, 8:9],
                                           -1.0))
                fvg[j] = vS.n

            def gating_exps(s, j):
                s.wait_ge(vS.h, fvg[j])
                par = j % 2
                sS.inc(s.activation(gstat[:, 0:8], pL[:, par, 0:8], AF.Exp,
                                    bias=stats[:, 10:11]))
                sS.inc(s.activation(stats[:, 11:12], stats[:, 9:10], AF.Exp,
                                    bias=stats[:, 10:11]))
                sg[j] = sS.n

            def gating_part2_combine(v, j):
                par = j % 2
                pl = pL[:, par, 0:8]
                g = groups[j]
                v.wait_ge(sS.h, sg[j])
                tv = vS.inc(v.tensor_scalar(
                    stats[:, 12:13], stats[:, 11:12], 2.0, 2.0,
                    op0=ALU.mult, op1=ALU.add))
                v.wait_ge(vS.h, tv)
                tv = vS.inc(v.reciprocal(stats[:, 13:14], stats[:, 12:13]))
                v.wait_ge(vS.h, tv)
                tv = vS.inc(v.tensor_tensor(
                    zrow[:, :], pl, stats[:, 9:10].broadcast_to([128, NE]),
                    op=ALU.is_ge))
                v.wait_ge(vS.h, tv)
                tv = vS.inc(v.scalar_tensor_tensor(
                    w_full[:, g, :], gstat[:, 0:8], stats[:, 13:14],
                    zrow[:, :], op0=ALU.mult, op1=ALU.mult))
                v.wait_ge(vS.h, tv)
                wcol = w_full[:, g, 0:1]
                v.scalar_tensor_tensor(
                    res_fn(g, 0), pY[:, par, 0, :], wcol, res_fn(g, 0),
                    op0=ALU.mult, op1=ALU.add)
                vS.inc(v.scalar_tensor_tensor(
                    res_fn(g, 1), pY[:, par, 1, :], wcol, res_fn(g, 1),
                    op0=ALU.mult, op1=ALU.add))
                f0_v[j] = vS.n

            for i, g in enumerate(groups):
                @block.scalar
                def _(s, i=i, g=g):
                    if i >= NLNBUF:
                        s.wait_ge(tS.h, ln_t[i - NLNBUF])
                    w = extra_s_wait(g) if extra_s_wait is not None else None
                    ln_s[i] = emit_sln(g, src_fn(g), w, i % NLNBUF)
                    if i >= 2:
                        gating_exps(s, i - 2)

                @block.tensor
                def _(t, i=i):
                    if i == 0:
                        t.wait_ge(gw.h, consts_tgt)
                    t.wait_ge(sS.h, ln_s[i])
                    if i >= 2:
                        t.wait_ge(vS.h, ln_v[i - 2])
                    for c in range(NCH):
                        ins = t.transpose(pT[:, i % 2, c, :],
                                          xn_tm[:, i % NLNBUF,
                                                c * 128:(c + 1) * 128],
                                          ident[:, :])
                    tS.inc(ins)
                    ln_t[i] = tS.n
                    if i >= 1:
                        e0_mms(t, i - 1)

                @block.vector
                def _(v, i=i, g=g):
                    v.wait_ge(tS.h, ln_t[i])
                    vS.inc(v.tensor_copy(
                        xn_view[:, :, tok_off(g):tok_off(g) + 128],
                        pT[:, i % 2, :, :]))
                    ln_v[i] = vS.n
                    if i >= 2:
                        gating_part2_combine(v, i - 2)
                    if i >= 1:
                        gating_part1(v, i - 1)

            @block.tensor
            def _(t):
                e0_mms(t, n - 1)

            @block.scalar
            def _(s):
                gating_exps(s, n - 2)

            @block.vector
            def _(v):
                gating_part2_combine(v, n - 2)
                gating_part1(v, n - 1)

            @block.scalar
            def _(s):
                gating_exps(s, n - 1)

            @block.vector
            def _(v):
                gating_part2_combine(v, n - 1)

            t_base = tS.n
            v_base = vS.n

            @block.tensor
            def _(t):
                for e in range(1, NEXP):
                    for h in range(2):
                        sl = loads[(e, h)]
                        t.wait_ge(gw.h, sl[1])
                        for i, g in enumerate(groups):
                            m = (2 * (e - 1) + h) * n + i
                            if m >= 2:
                                t.wait_ge(vS.h, v_base + m - 1)
                            else:
                                t.wait_ge(vS.h, f0_v[n - 2 + m])
                            par = i % 2
                            to = tok_off(g)
                            for c in range(NCH):
                                ins = t.matmul(pY[:, par, h, :],
                                               xn_view[:, c, to:to + 128],
                                               wbuf[sl[0]][:, c, :],
                                               start=(c == 0),
                                               stop=(c == NCH - 1))
                            tS.inc(ins)

            @block.vector
            def _(v):
                for e in range(1, NEXP):
                    for h in range(2):
                        for i, g in enumerate(groups):
                            m = (2 * (e - 1) + h) * n + i
                            v.wait_ge(tS.h, t_base + m + 1)
                            wcol = w_full[:, g, e:e + 1]
                            vS.inc(v.scalar_tensor_tensor(
                                res_fn(g, h), pY[:, i % 2, h, :], wcol,
                                res_fn(g, h), op0=ALU.mult, op1=ALU.add))

            loads[(0, 0)][2][0] = tS
            loads[(0, 0)][2][1] = f0_t[n - 1]
            loads[(0, 1)][2][0] = tS
            loads[(0, 1)][2][1] = f0_t[n - 1]
            for e in range(1, NEXP):
                for h in range(2):
                    ref = loads[(e, h)][2]
                    ref[0] = tS
                    ref[1] = t_base + (2 * (e - 1) + h) * n + n
            xdone = {i: v_base + 13 * n + i + 1 for i in range(n)}
            if spill:
                @block.sync
                def _(sy):
                    for i, g in enumerate(groups):
                        if g == 0:
                            continue
                        sy.wait_ge(vS.h, xdone[i])
                        sd.dma(sy, x1sp[(g - 1) * 128:g * 128, :], src_fn(g))
            return ln_v, xdone

        with nc.psum_tensor("pT1", [128, 2, NCH, 128], BF16) as pT1, \
             nc.psum_tensor("pY1", [128, 2, 2, 512], F32) as pY1, \
             nc.psum_tensor("pL1", [128, 2, 512], F32) as pL1:

            groups17 = list(range(G))
            _, xdone1 = emit_lnffn(
                groups17, lambda g: x_res[:, g, :], xn17, lambda g: g * 128,
                lambda g, h: x_res[:, g, 512 * h:512 * h + 512],
                ffn1_loads, pY1, pL1, pT1,
                extra_s_wait=lambda g: (sd, xload_tgt[g]))
            emit_wload_body(wload_plans[0])

        def x1_done(g):
            return xdone1[groups17.index(g)]

        if STAGE == 1:
            @block.gpsimd
            def _(g):
                g.wait_ge(vS.h, x1_done(16))
                gw.dma(g, dbg_d[:, 0:136], w_full[:, :, :].rearrange("p a b -> p (a b)"))

            @block.sync
            def _(sy):
                for g in range(1, G):
                    sy.wait_ge(vS.h, x1_done(g))
                    sd.dma(sy, out_d[(g - 1) * 128:g * 128, :], x_res[:, g, :])
                sy.wait_ge(sd.h, sd.n)
                sy.wait_ge(gw.h, gw.em)
            return nc

        def finish_early(tag):
            if STOP != tag:
                return False
            dbgsel = os.environ.get("MK_DBG", "")
            if dbgsel:
                @block.vector
                def _(v):
                    v.wait_ge(tS.h, tS.n)
                    v.wait_ge(sS.h, sS.n)
                    if dbgsel == "kv":
                        v.tensor_copy(scr_a[:, 0, :], kg[:, 0, :])
                        vS.inc(v.tensor_copy(scr_b[:, 0, 0:256],
                                             vg[:, 0, 0:256]))
                    elif dbgsel == "qs":
                        vS.inc(v.tensor_copy(
                            scr_a[:, 0, :].rearrange("p (j t) -> p j t", t=128),
                            qs_fm[:, :, 0:128]))
                    elif dbgsel == "afm":
                        vS.inc(v.tensor_copy(
                            scr_a[:, 0, :].rearrange("p (c t) -> p c t", t=128),
                            afm[:, 2:8, 64:192]))
                dbg_v = vS.n

                @block.gpsimd
                def _(g):
                    g.wait_ge(vS.h, dbg_v)
                    gw.dma(g, dbg_d[:, 0:768], scr_a[:, 0, :])
                    if dbgsel == "kv":
                        gw.dma(g, dbg_d[:, 768:1024], scr_b[:, 0, 0:256])

            @block.sync
            def _(sy):
                sy.wait_ge(tS.h, tS.n)
                sy.wait_ge(vS.h, vS.n)
                sy.wait_ge(sS.h, sS.n)
                sy.wait_ge(gw.h, gw.em)
                for g in range(1, G):
                    sd.dma(sy, out_d[(g - 1) * 128:g * 128, :], x_res[:, g, :])
                sy.wait_ge(sd.h, sd.n)
            return True

        _scopes = []

        def _phase_scope(tag):
            if _scopes:
                nm, sid = _scopes.pop()
                nc.leave_named_scope(nm, sid, False)
            sid, _ = nc.enter_named_scope(tag, False)
            _scopes.append((tag, sid))

        def barrier_all():
            tT, tV, tSS = tS.n, vS.n, sS.n

            @block.tensor
            def _(t):
                t.wait_ge(vS.h, tV)
                t.wait_ge(sS.h, tSS)

            @block.vector
            def _(v):
                v.wait_ge(tS.h, tT)
                v.wait_ge(sS.h, tSS)

            @block.scalar
            def _(s):
                s.wait_ge(tS.h, tT)
                s.wait_ge(vS.h, tV)

        wslot_rr[0] = 0
        proj_loads = plan_wloads([
            (("q", 0), wq_d[:, 0:512]), (("q", 1), wq_d[:, 512:1024]),
            (("k", 0), wk_d[:, 0:512]), (("k", 1), wk_d[:, 512:1024]),
            (("v", 0), wv_d[:, 0:512]), (("v", 1), wv_d[:, 512:1024]),
            (("o", 0), wo_d[:, 0:512]), (("o", 1), wo_d[:, 512:1024]),
        ])
        proj_plan = wload_plans[-1]
        emit_wload_body(proj_plan[0:3])

        _phase_scope("ph_p3")
        barrier_all()
        with nc.psum_tensor("pT3", [128, 2, NCH, 128], BF16) as pT3:
            ln2 = emit_ln(groups17, lambda g: x_res[:, g, :], xn17,
                          lambda g: g * 128, pT3,
                          extra_s_wait=lambda g: (vS, x1_done(g)),
                          spill=True)

        spill_tgt = sd.n

        @block.vector
        def _(v):
            v.wait_ge(sd.h, spill_tgt)

        @block.scalar
        def _(s):
            s.wait_ge(sd.h, spill_tgt)

        @block.tensor
        def _(t):
            t.wait_ge(sd.h, spill_tgt)



        GOWN = list(range(1, G))

        def emit_fm_proj(wkey, dst, dst_ncols, tok_lo, tok_n, loads):
            sl0 = loads[(wkey, 0)]
            tiles = []
            t_ = tok_lo
            while t_ < tok_lo + tok_n:
                w = min(512, tok_lo + tok_n - t_)
                tiles.append((t_, w))
                t_ += w
            ev_tgt = {}
            mm_tgt = {}
            for ti, (t0c, w) in enumerate(tiles):
                @block.tensor
                def _(t, ti=ti, t0c=t0c, w=w):
                    if ti == 0:
                        t.wait_ge(gw.h, sl0[1])
                    if ti >= 2:
                        t.wait_ge(vS.h, ev_tgt[ti - 2])
                    par = ti % 2
                    for dch in range(dst_ncols // 128):
                        for c in range(NCH):
                            ins = t.matmul(
                                pql[:, par, dch, 0:w],
                                wbuf[sl0[0]][:, c, dch * 128:(dch + 1) * 128],
                                xn17[:, c, t0c:t0c + w],
                                start=(c == 0), stop=(c == NCH - 1))
                        tS.inc(ins)
                    mm_tgt[ti] = tS.n

                @block.vector
                def _(v, ti=ti, t0c=t0c, w=w):
                    v.wait_ge(tS.h, mm_tgt[ti])
                    par = ti % 2
                    vS.inc(v.tensor_copy(
                        dst[:, :, t0c - tok_lo:t0c - tok_lo + w],
                        pql[:, par, 0:dst_ncols // 128, 0:w]))
                    ev_tgt[ti] = vS.n
            return ev_tgt

        _phase_scope("ph_p4a")
        barrier_all()
        with nc.psum_tensor("pql", [128, 2, 2, 512], F32) as pql:
            qloc_done = emit_fm_proj("q", qloc, 256, 128, TOK, proj_loads)

        if finish_early('p4a'):
            return nc

        _phase_scope("ph_p4b")
        barrier_all()
        qg_v2 = {}
        with nc.psum_tensor("pqg", [128, 2, 2, 512], F32) as pqg, \
             nc.psum_tensor("ptrq", [128, 2, 6, 128], BF16) as ptrq:
            qg_t = {}
            qg_s = {}
            qg_v1 = {}
            qg_t2 = {}
            sl0, sl1 = proj_loads[("q", 0)], proj_loads[("q", 1)]
            for i, g in enumerate(GOWN):
                par = i % 2

                @block.tensor
                def _(t, i=i, g=g, par=par):
                    if i == 0:
                        t.wait_ge(gw.h, max(sl0[1], sl1[1]))
                    if i >= 2:
                        t.wait_ge(sS.h, qg_s[i - 2])
                        t.wait_ge(vS.h, qg_v1[i - 2])
                    to = g * 128
                    for c in range(NCH):
                        t.matmul(pqg[:, par, 0, 0:256],
                                 xn17[:, c, to:to + 128],
                                 wbuf[sl0[0]][:, c, 256:512],
                                 start=(c == 0), stop=(c == NCH - 1))
                        ins = t.matmul(pqg[:, par, 1, :],
                                       xn17[:, c, to:to + 128],
                                       wbuf[sl1[0]][:, c, :],
                                       start=(c == 0), stop=(c == NCH - 1))
                    tS.inc(ins)
                    qg_t[i] = tS.n

                @block.scalar
                def _(s, i=i, par=par):
                    s.wait_ge(tS.h, qg_t[i])
                    sS.inc(s.activation(scr_a[:, par, 0:256],
                                        pqg[:, par, 0, 0:256], AF.Exp))
                    sS.inc(s.activation(scr_a[:, par, 256:768],
                                        pqg[:, par, 1, :], AF.Exp))
                    qg_s[i] = sS.n

                @block.vector
                def _(v, i=i, par=par):
                    v.wait_ge(sS.h, qg_s[i])
                    sa = scr_a[:, par, :].rearrange("p (h e) -> p h e", e=64)
                    tv = vS.inc(v.reduce_sum(rsg[:, :], sa, axis=AX.X))
                    v.wait_ge(vS.h, tv)
                    tv = vS.inc(v.reciprocal(rsg[:, :], rsg[:, :]))
                    v.wait_ge(vS.h, tv)
                    vS.inc(v.tensor_tensor(
                        qstm[:, par, :].rearrange("p (h e) -> p h e", e=64),
                        sa,
                        rsg[:, :].unsqueeze(2).broadcast_to([128, 12, 64]),
                        op=ALU.mult))
                    qg_v1[i] = vS.n

                @block.tensor
                def _(t, i=i, par=par):
                    t.wait_ge(vS.h, qg_v1[i])
                    if i >= 2:
                        t.wait_ge(vS.h, qg_v2[i - 2])
                    for j in range(6):
                        ins = t.transpose(ptrq[:, par, j, :],
                                          qstm[:, par, j * 128:(j + 1) * 128],
                                          ident[:, :])
                    tS.inc(ins)
                    qg_t2[i] = tS.n

                @block.vector
                def _(v, i=i, g=g, par=par):
                    v.wait_ge(tS.h, qg_t2[i])
                    vS.inc(v.tensor_copy(
                        qs_fm[:, :, (g - 1) * 128:g * 128],
                        ptrq[:, par, :, :]))
                    qg_v2[i] = vS.n

        if finish_early('p4b'):
            return nc

        for h in range(2):
            proj_loads[("q", h)][2][0] = tS
            proj_loads[("q", h)][2][1] = tS.n
        emit_wload_body(proj_plan[3:5])

        _phase_scope("ph_p4c")
        barrier_all()
        with nc.psum_tensor("pkl", [128, 2, 2, 512], F32) as pql:
            kloc_done = emit_fm_proj("k", kloc, 256, 0, TTOK, proj_loads)

        if finish_early('p4c'):
            return nc

        _phase_scope("ph_p4d")
        barrier_all()
        with nc.psum_tensor("pkg", [128, 2, 2, 512], F32) as pkg:
            kg_t = {}
            kg_s = {}
            kg_vm = {}
            kg_v = {}
            sl0, sl1 = proj_loads[("k", 0)], proj_loads[("k", 1)]
            for i, g in enumerate(GOWN):
                par = i % 2

                @block.tensor
                def _(t, i=i, g=g, par=par):
                    if i == 0:
                        t.wait_ge(gw.h, max(sl0[1], sl1[1]))
                    if i >= 2:
                        t.wait_ge(sS.h, kg_s[i - 2])
                        t.wait_ge(vS.h, kg_vm[i - 2])
                    to = g * 128
                    for c in range(NCH):
                        t.matmul(pkg[:, par, 0, 0:256],
                                 xn17[:, c, to:to + 128],
                                 wbuf[sl0[0]][:, c, 256:512],
                                 start=(c == 0), stop=(c == NCH - 1))
                        ins = t.matmul(pkg[:, par, 1, :],
                                       xn17[:, c, to:to + 128],
                                       wbuf[sl1[0]][:, c, :],
                                       start=(c == 0), stop=(c == NCH - 1))
                    tS.inc(ins)
                    kg_t[i] = tS.n

                @block.scalar
                def _(s, i=i, par=par):
                    s.wait_ge(tS.h, kg_t[i])
                    if i >= 1:
                        s.wait_ge(vS.h, kg_v[i - 1])
                    s.activation(scr_b[:, par, 0:256], pkg[:, par, 0, 0:256],
                                 AF.Relu)
                    sS.inc(s.activation(scr_b[:, par, 256:768],
                                        pkg[:, par, 1, :], AF.Relu))
                    kg_s[i] = sS.n

                @block.vector
                def _(v, i=i, par=par):
                    v.wait_ge(tS.h, kg_t[i])
                    v.wait_ge(sS.h, kg_s[i])
                    v.tensor_scalar_min(scr_a[:, par, 0:256],
                                        pkg[:, par, 0, 0:256], 0.0)
                    vS.inc(v.tensor_scalar_min(scr_a[:, par, 256:768],
                                               pkg[:, par, 1, :], 0.0))
                    kg_vm[i] = vS.n

                @block.scalar
                def _(s, i=i, par=par):
                    s.wait_ge(vS.h, kg_vm[i])
                    sS.inc(s.activation(sqscr[:, 0:768], scr_a[:, par, :],
                                        AF.Exp))
                    kg_s2 = sS.n
                    kg_s[i] = max(kg_s[i], kg_s2)

                @block.vector
                def _(v, i=i, g=g, par=par):
                    v.wait_ge(sS.h, kg_s[i])
                    vS.inc(v.tensor_tensor(kg[:, g - 1, :], sqscr[:, 0:768],
                                           scr_b[:, par, :], op=ALU.add))
                    kg_v[i] = vS.n

        if finish_early('p4d'):
            return nc

        for h in range(2):
            proj_loads[("k", h)][2][0] = tS
            proj_loads[("k", h)][2][1] = tS.n
        emit_wload_body(proj_plan[5:7])

        _phase_scope("ph_p4e")
        barrier_all()
        with nc.psum_tensor("pvl", [128, 512], F32) as pvl, \
             nc.psum_tensor("pvg", [128, 2, 512], F32) as pvg, \
             nc.psum_tensor("pSt", [128, 8, 128], F32) as pSt:
            vl_t = {}
            vl_v = {}
            vg_t = {}
            vg_s = {}
            st_t = {}
            st_v = {}
            slv0, slv1 = proj_loads[("v", 0)], proj_loads[("v", 1)]
            for g in range(G):
                i = g - 1

                @block.tensor
                def _(t, g=g, i=i):
                    if g == 0:
                        t.wait_ge(gw.h, max(slv0[1], slv1[1]))
                    if g >= 1:
                        t.wait_ge(vS.h, vl_v[g - 1])
                    to = g * 128
                    for c in range(NCH):
                        ins = t.matmul(pvl[:, 0:256],
                                       xn17[:, c, to:to + 128],
                                       wbuf[slv0[0]][:, c, 0:256],
                                       start=(c == 0), stop=(c == NCH - 1))
                    tS.inc(ins)
                    vl_t[g] = tS.n

                @block.vector
                def _(v, g=g):
                    v.wait_ge(tS.h, vl_t[g])
                    vS.inc(v.tensor_copy(vloc[:, g, :], pvl[:, 0:256]))
                    vl_v[g] = vS.n

                if g == 0:
                    continue
                par = i % 2

                @block.tensor
                def _(t, g=g, i=i, par=par):
                    if i >= 1:
                        t.wait_ge(sS.h, vg_s[i - 1])
                    to = g * 128
                    for c in range(NCH):
                        t.matmul(pvg[:, 0, 0:256],
                                 xn17[:, c, to:to + 128],
                                 wbuf[slv0[0]][:, c, 256:512],
                                 start=(c == 0), stop=(c == NCH - 1))
                        ins = t.matmul(pvg[:, 1, :],
                                       xn17[:, c, to:to + 128],
                                       wbuf[slv1[0]][:, c, :],
                                       start=(c == 0), stop=(c == NCH - 1))
                    tS.inc(ins)
                    vg_t[i] = tS.n

                @block.scalar
                def _(s, i=i, g=g, par=par):
                    s.wait_ge(tS.h, vg_t[i])
                    s.copy(vg[:, g - 1, 0:256], pvg[:, 0, 0:256])
                    sS.inc(s.copy(vg[:, g - 1, 256:768], pvg[:, 1, :]))
                    vg_s[i] = sS.n

                @block.tensor
                def _(t, i=i, g=g):
                    t.wait_ge(sS.h, vg_s[i])
                    t.wait_ge(vS.h, kg_v[i])
                    if i >= 1:
                        t.wait_ge(vS.h, st_v[i - 1])
                    for j in range(6):
                        t.matmul(pSt[:, j, :],
                                 kg[:, i, j * 128:(j + 1) * 128],
                                 vg[:, i, j * 128:(j + 1) * 128],
                                 start=True, stop=True)
                        ins = t.matmul(pSt[:, 6, j:j + 1],
                                       kg[:, i, j * 128:(j + 1) * 128],
                                       onesc[:, 0:1],
                                       start=True, stop=True)
                    tS.inc(ins)
                    st_t[i] = tS.n

                @block.vector
                def _(v, i=i):
                    v.wait_ge(tS.h, st_t[i])
                    if i == 0:
                        v.tensor_copy(tsb[0:64, :, 0:64], pSt[0:64, 0:6, 0:64])
                        v.tensor_copy(tsb[64:128, :, 0:64],
                                      pSt[64:128, 0:6, 64:128])
                        vS.inc(v.tensor_copy(tsb[:, :, 64:65].squeeze(2),
                                             pSt[:, 6, 0:6]))
                    else:
                        v.scalar_tensor_tensor(
                            tsb[0:64, :, 0:64], pSt[0:64, 0:6, 0:64], 1.0,
                            tsb[0:64, :, 0:64], op0=ALU.mult, op1=ALU.add)
                        v.scalar_tensor_tensor(
                            tsb[64:128, :, 0:64], pSt[64:128, 0:6, 64:128], 1.0,
                            tsb[64:128, :, 0:64], op0=ALU.mult, op1=ALU.add)
                        vS.inc(v.scalar_tensor_tensor(
                            tsb[:, :, 64:65].squeeze(2), pSt[:, 6, 0:6], 1.0,
                            tsb[:, :, 64:65].squeeze(2),
                            op0=ALU.mult, op1=ALU.add))
                    st_v[i] = vS.n

            pub_s = [sS.n]
            pub_v = [st_v[15]]

        if finish_early('p4e'):
            return nc

        for h in range(2):
            proj_loads[("v", h)][2][0] = tS
            proj_loads[("v", h)][2][1] = tS.n
        emit_wload_body(proj_plan[7:8])

        _phase_scope("ph_p5")
        barrier_all()
        with nc.psum_tensor("psc", [128, 2, 512], F32) as psc, \
             nc.psum_tensor("ppt", [128, 2, 128], BF16) as ppt, \
             nc.psum_tensor("pout", [128, 2, 512], F32) as pout:
            sc_t = {}
            v_mask = {}
            s_exp = {}
            v_pb = {}
            t_pt = {}
            s_pt = {}
            t_out = {}
            v_out = {}
            bh = -1
            for b in range(16):
                for h in range(4):
                    bh += 1
                    par = bh % 2
                    mrow = 0 if b == 0 else 1

                    @block.tensor
                    def _(t, b=b, h=h, bh=bh, par=par):
                        if bh == 0:
                            t.wait_ge(sd.h, misc_ld)
                            t.wait_ge(vS.h, qloc_done[3])
                            t.wait_ge(vS.h, kloc_done[4])
                        if bh >= 2:
                            t.wait_ge(vS.h, v_mask[bh - 2])
                        r0 = (h % 2) * 64
                        ins = t.matmul(
                            psc[:, par, 0:256],
                            qloc[r0:r0 + 64, h // 2, b * 128:(b + 1) * 128],
                            kloc[r0:r0 + 64, h // 2, b * 128:b * 128 + 256],
                            start=True, stop=True)
                        tS.inc(ins)
                        sc_t[bh] = tS.n

                    @block.vector
                    def _(v, bh=bh, par=par, mrow=mrow):
                        v.wait_ge(tS.h, sc_t[bh])
                        if bh >= 2:
                            v.wait_ge(sS.h, s_exp[bh - 2])
                        tv = vS.inc(v.tensor_tensor(
                            probs_f[:, par, :], psc[:, par, 0:256],
                            masks[:, mrow, :], op=ALU.add))
                        v.wait_ge(vS.h, tv)
                        tv = vS.inc(v.reduce_max(stats[:, 8 + par:9 + par],
                                                 probs_f[:, par, :], axis=AX.X))
                        v.wait_ge(vS.h, tv)
                        vS.inc(v.tensor_scalar_mul(
                            stats[:, 10 + par:11 + par],
                            stats[:, 8 + par:9 + par], -0.125))
                        v_mask[bh] = vS.n

                    @block.scalar
                    def _(s, bh=bh, par=par):
                        s.wait_ge(vS.h, v_mask[bh])
                        if bh >= 2:
                            s.wait_ge(vS.h, v_pb[bh - 2])
                        sS.inc(s.activation(
                            sqscr[:, par * 256:(par + 1) * 256],
                            probs_f[:, par, :], AF.Exp,
                            bias=stats[:, 10 + par:11 + par], scale=0.125,
                            accum_out=stats[:, 12 + par:13 + par]))
                        s_exp[bh] = sS.n

                    @block.vector
                    def _(v, bh=bh, par=par):
                        v.wait_ge(sS.h, s_exp[bh])
                        tv = vS.inc(v.reciprocal(stats[:, 14 + par:15 + par],
                                                 stats[:, 12 + par:13 + par]))
                        v.wait_ge(vS.h, tv)
                        vS.inc(v.tensor_scalar_mul(
                            probs_b[:, par, :],
                            sqscr[:, par * 256:(par + 1) * 256],
                            stats[:, 14 + par:15 + par]))
                        v_pb[bh] = vS.n

                    @block.tensor
                    def _(t, bh=bh, par=par):
                        t.wait_ge(vS.h, v_pb[bh])
                        if bh >= 1:
                            t.wait_ge(sS.h, s_pt[bh - 1])
                        t.transpose(ppt[:, 0, :], probs_b[:, par, 0:128],
                                    ident[:, :])
                        ins = t.transpose(ppt[:, 1, :], probs_b[:, par, 128:256],
                                         ident[:, :])
                        tS.inc(ins)
                        t_pt[bh] = tS.n

                    @block.scalar
                    def _(s, bh=bh):
                        s.wait_ge(tS.h, t_pt[bh])
                        sS.inc(s.copy(pts[:, :, :], ppt[:, :, :]))
                        s_pt[bh] = sS.n

                    @block.tensor
                    def _(t, b=b, h=h, bh=bh, par=par):
                        t.wait_ge(sS.h, s_pt[bh])
                        t.wait_ge(vS.h, vl_v[b + 1])
                        if bh >= 2:
                            t.wait_ge(vS.h, v_out[bh - 2])
                        hp = h // 2
                        t.matmul(pout[:, par, 0:128],
                                 vloc[:, b, hp * 128:(hp + 1) * 128],
                                 pts[:, 0, :], start=True, stop=False)
                        ins = t.matmul(pout[:, par, 0:128],
                                       vloc[:, b + 1, hp * 128:(hp + 1) * 128],
                                       pts[:, 1, :], start=False, stop=True)
                        tS.inc(ins)
                        t_out[bh] = tS.n

                    @block.vector
                    def _(v, b=b, h=h, bh=bh, par=par):
                        v.wait_ge(tS.h, t_out[bh])
                        r0 = (h % 2) * 64
                        vS.inc(v.tensor_copy(
                            afm[r0:r0 + 64, h // 2, b * 128:(b + 1) * 128],
                            pout[r0:r0 + 64, par, 0:128]))
                        v_out[bh] = vS.n

        tbuf_tgt = []
        carry_v = []
        cv0 = [None]

        @block.gpsimd
        def _(g):
            g.wait_ge(sS.h, pub_s[0])
            g.wait_ge(vS.h, pub_v[0])
            gw.dma(g, ag_in.ap(), tsb[:, :, :].rearrange("p j f -> p (j f)"))
            g.collective_compute(
                "AllGather", ALU.bypass,
                replica_groups=[list(range(NCORE))],
                ins=[ag_in.ap().opt()],
                outs=[ag_out.ap().opt()],
            ).then_inc(cc.h, 1)
            cc.n += 1
            g.wait_ge(cc.h, cc.n)
            vbase_c = vS.n
            for j in range(NCORE):
                if j >= 1:
                    g.wait_ge(vS.h, vbase_c + j)
                tbuf_tgt.append(gw.dma(g, tbuf[:, :, :].rearrange('p j f -> p (j f)'), ag_out.ap()[j * 128:(j + 1) * 128, :]))

        @block.vector
        def _(v):
            for j in range(NCORE):
                v.wait_ge(gw.h, tbuf_tgt[j])
                if j == 0:
                    tv = vS.inc(v.tensor_scalar_mul(
                        carry[:, :, :], tbuf[:, :, :], pref[:, 0:1]))
                else:
                    tv = vS.inc(v.scalar_tensor_tensor(
                        carry[:, :, :], tbuf[:, :, :],
                        pref[:, j:j + 1], carry[:, :, :],
                        op0=ALU.mult, op1=ALU.add))
                v.wait_ge(vS.h, tv)
                carry_v.append(tv)

        if finish_early('exch'):
            return nc

        if finish_early('p5'):
            return nc

        _phase_scope("ph_p6")
        barrier_all()
        v_ctx = {}
        t_outg = {}
        t_updA = {}
        t_updC = {}
        s_outg = {}
        s_ctx = {}
        asb = tsb
        with nc.psum_tensor("pC", [128, 8, 128], F32) as pC, \
             nc.psum_tensor("pG", [128, 8, 128], F32) as pG, \
             nc.psum_tensor("pog", [128, 2, 8, 64], F32) as pog:
            av0 = [None]

            @block.vector
            def _(v):
                v.wait_ge(vS.h, carry_v[7])
                tv = vS.inc(v.tensor_copy(asb[:, :, :], carry[:, :, :]))
                v.wait_ge(vS.h, tv)
                av0[0] = vS.n
            for u in range(P6N):
                par = u % 2
                i = u // 2
                odd = u % 2 == 1

                @block.vector
                def _(v, u=u, par=par, i=i, odd=odd):
                    if u == 0:
                        v.wait_ge(vS.h, av0[0])
                    if odd:
                        v.wait_ge(tS.h, t_updC[i])
                    if u >= 2:
                        v.wait_ge(tS.h, t_outg[u - 2])
                    if not odd and i >= 1:
                        v.wait_ge(tS.h, t_updC[i - 1])
                    if not odd:
                        vS.inc(v.tensor_copy(vstage[0:64, :], vg[0:64, i, :]))
                    if odd:
                        v.scalar_tensor_tensor(
                            ctxf[0:64, :, :], pC[0:64, 0:6, 0:64], 1.0,
                            asb[0:64, :, 0:64], op0=ALU.mult, op1=ALU.add)
                        v.scalar_tensor_tensor(
                            ctxf[64:128, :, :], pC[64:128, 0:6, 64:128], 1.0,
                            asb[64:128, :, 0:64], op0=ALU.mult, op1=ALU.add)
                        tv = vS.inc(v.scalar_tensor_tensor(
                            zrow[:, 0:6], pC[:, 6, 0:6], 1.0,
                            asb[:, :, 64:65].squeeze(2),
                            op0=ALU.mult, op1=ALU.add))
                        v.wait_ge(vS.h, tv)
                    else:
                        v.tensor_copy(ctxf[:, :, :], asb[:, :, 0:64])
                        tv = vS.inc(v.tensor_copy(zrow[:, 0:6],
                                                  asb[:, :, 64:65].squeeze(2)))
                        v.wait_ge(vS.h, tv)
                    tv = vS.inc(v.tensor_scalar_add(zrow[:, 0:6], zrow[:, 0:6],
                                                    1e-6))
                    v.wait_ge(vS.h, tv)
                    tv = vS.inc(v.reciprocal(gstat[:, 0:6], zrow[:, 0:6]))
                    v.wait_ge(vS.h, tv)
                    v.tensor_tensor(
                        ctxs2[0:64, par, :, 0:64], ctxf[0:64, :, :],
                        gstat[0:64, 0:6].unsqueeze(2).broadcast_to([64, 6, 64]),
                        op=ALU.mult)
                    tv = vS.inc(v.tensor_tensor(
                        ctxs2[64:128, par, :, 64:128], ctxf[64:128, :, :],
                        gstat[64:128, 0:6].unsqueeze(2).broadcast_to([64, 6, 64]),
                        op=ALU.mult))
                    v_ctx[u] = vS.n
                    if odd:
                        v.wait_ge(vS.h, tv)
                        v.wait_ge(tS.h, t_updA[i])
                        v.scalar_tensor_tensor(
                            asb[0:64, :, 0:64], pG[0:64, 0:6, 0:64], 1.0,
                            asb[0:64, :, 0:64], op0=ALU.mult, op1=ALU.add)
                        v.scalar_tensor_tensor(
                            asb[64:128, :, 0:64], pG[64:128, 0:6, 64:128], 1.0,
                            asb[64:128, :, 0:64], op0=ALU.mult, op1=ALU.add)
                        vS.inc(v.scalar_tensor_tensor(
                            asb[:, :, 64:65].squeeze(2), pG[:, 6, 0:6], 1.0,
                            asb[:, :, 64:65].squeeze(2),
                            op0=ALU.mult, op1=ALU.add))
                        v_ctx[u] = vS.n

                @block.tensor
                def _(t, u=u, par=par, i=i, odd=odd):
                    t.wait_ge(vS.h, v_ctx[u])
                    if u >= 2:
                        t.wait_ge(sS.h, s_outg[u - 2])
                    cs = u * 64
                    for j in range(6):
                        ins = t.matmul(pog[:, par, j, :],
                                       ctxs2[:, par, j, :],
                                       qs_fm[:, j, cs:cs + 64],
                                       start=True, stop=True)
                    tS.inc(ins)
                    t_outg[u] = tS.n
                    if not odd:
                        for j in range(6):
                            t.matmul(pC[:, j, :],
                                     kg[:, i, j * 128:(j + 1) * 128],
                                     vstage[:, j * 128:(j + 1) * 128],
                                     start=True, stop=True)
                            ins = t.matmul(pC[:, 6, j:j + 1],
                                           kg[:, i, j * 128:(j + 1) * 128],
                                           onese[:, 0:1],
                                           start=True, stop=True)
                        tS.inc(ins)
                        t_updC[i] = tS.n
                        if i >= 1:
                            t.wait_ge(vS.h, v_ctx[u - 1])
                        for j in range(6):
                            t.matmul(pG[:, j, :],
                                     kg[:, i, j * 128:(j + 1) * 128],
                                     vg[:, i, j * 128:(j + 1) * 128],
                                     start=True, stop=True)
                            ins = t.matmul(pG[:, 6, j:j + 1],
                                           kg[:, i, j * 128:(j + 1) * 128],
                                           onesc[:, 0:1],
                                           start=True, stop=True)
                        tS.inc(ins)
                        t_updA[i] = tS.n

                @block.scalar
                def _(s, u=u, par=par):
                    s.wait_ge(tS.h, t_outg[u])
                    cs = u * 64
                    for j in range(6):
                        ins = s.copy(afm[:, 2 + j, cs:cs + 64],
                                     pog[:, par, j, :])
                    sS.inc(ins)
                    s_outg[u] = sS.n

        if finish_early('p6'):
            return nc

        _phase_scope("ph_p7")
        barrier_all()
        v_x2 = {}
        with nc.psum_tensor("po", [128, 2, 2, 512], F32) as po:
            t_o = {}
            rl_tgt = {}
            slo0, slo1 = proj_loads[("o", 0)], proj_loads[("o", 1)]
            for i in range(16):
                par = i % 2

                @block.sync
                def _(sy, i=i, par=par):
                    if i < 2:
                        sy.wait_ge(vS.h, kg_v[15])
                    if i >= 2:
                        sy.wait_ge(vS.h, v_x2[i - 2])
                    rl_tgt[i] = sd.dma(sy, x_roll[:, par, :],
                                       x1sp[i * 128:(i + 1) * 128, :])

                @block.tensor
                def _(t, i=i, par=par):
                    if i == 0:
                        t.wait_ge(gw.h, max(slo0[1], slo1[1]))
                    bh_last = i * 4 + 3
                    t.wait_ge(vS.h, v_out[bh_last])
                    t.wait_ge(sS.h, s_outg[2 * i + 1])
                    if i >= 2:
                        t.wait_ge(vS.h, v_x2[i - 2])
                    to = i * 128
                    for h, slh in ((0, slo0), (1, slo1)):
                        for c in range(NCH):
                            t.matmul(po[:, par, h, :],
                                     afm[:, c, to:to + 128],
                                     wbuf[slh[0]][:, c, :],
                                     start=(c == 0), stop=False)
                        ins = t.matmul(po[:, par, h, :],
                                       onesr[0:1, :],
                                       bo_sb[0:1, 512 * h:512 * h + 512],
                                       start=False, stop=True)
                    tS.inc(ins)
                    t_o[i] = tS.n

                @block.vector
                def _(v, i=i, par=par):
                    v.wait_ge(tS.h, t_o[i])
                    v.wait_ge(sd.h, rl_tgt[i])
                    v.tensor_tensor(x2[:, i, 0:512], po[:, par, 0, :],
                                    x_roll[:, par, 0:512], op=ALU.add)
                    vS.inc(v.tensor_tensor(x2[:, i, 512:1024], po[:, par, 1, :],
                                           x_roll[:, par, 512:1024],
                                           op=ALU.add))
                    v_x2[i] = vS.n

        for h in range(2):
            proj_loads[("o", h)][2][0] = tS
            proj_loads[("o", h)][2][1] = tS.n

        if STAGE == 2:
            dbgsel = os.environ.get("MK_DBG", "")
            if dbgsel:
                @block.vector
                def _(v):
                    v.wait_ge(vS.h, vS.n)
                    if dbgsel == "kv":
                        v.tensor_copy(scr_a[:, 0, :], kg[:, 0, :])
                        vS.inc(v.tensor_copy(scr_b[:, 0, 0:256], vg[:, 0, 0:256]))
                    elif dbgsel == "qs":
                        vS.inc(v.tensor_copy(
                            scr_a[:, :, :].rearrange("p a b -> p (a b)")[:, 0:768]
                            .rearrange("p (j t) -> p j t", t=128),
                            qs_fm[:, :, 0:128]))
                    elif dbgsel == "afm":
                        vS.inc(v.tensor_copy(
                            scr_a[:, :, :].rearrange("p a b -> p (a b)")[:, 0:768]
                            .rearrange("p (c t) -> p c t", t=128),
                            afm[:, 2:8, 64:192]))
                dbg_v = vS.n

                @block.gpsimd
                def _(g):
                    g.wait_ge(vS.h, dbg_v)
                    gw.dma(g, dbg_d[:, 0:768],
                           scr_a[:, :, :].rearrange("p a b -> p (a b)")[:, 0:768])
                    if dbgsel == "kv":
                        gw.dma(g, dbg_d[:, 768:1024], scr_b[:, 0, 0:256])

            @block.sync
            def _(sy):
                for i in range(16):
                    sy.wait_ge(vS.h, v_x2[i])
                    sd.dma(sy, out_d[i * 128:(i + 1) * 128, :], x2[:, i, :])
                if os.environ.get("MK_DBG") == "spill":
                    sd.dma(sy, dbg_d[:, :], x1sp[0:128, :])
                sy.wait_ge(sd.h, sd.n)
                sy.wait_ge(gw.h, gw.em)
            return nc

        wslot_rr[0] = 0
        ffn2_loads = plan_wloads(
            [((e, h), expT_d[e, :, 512 * h:512 * h + 512]) for e in range(NE)
             for h in range(2)])
        ffn2_plan = wload_plans[-1]

        _phase_scope("ph_p89")
        barrier_all()
        with nc.psum_tensor("pT8", [128, 2, NCH, 128], BF16) as pT8, \
             nc.psum_tensor("pY8", [128, 2, 2, 512], F32) as pY8, \
             nc.psum_tensor("pL8", [128, 2, 512], F32) as pL8:
            groups16 = list(range(16))
            _, xdone2 = emit_lnffn(
                groups16, lambda g: x2[:, g, :], afm, lambda g: g * 128,
                lambda g, h: x2[:, g, 512 * h:512 * h + 512],
                ffn2_loads, pY8, pL8, pT8,
                extra_s_wait=lambda g: (vS, v_x2[g]))
            emit_wload_body(ffn2_plan)

        @block.sync
        def _(sy):
            for i in range(16):
                sy.wait_ge(vS.h, xdone2[i])
                sd.dma(sy, out_d[i * 128:(i + 1) * 128, :], x2[:, i, :])
            sy.wait_ge(sd.h, sd.n)
            sy.wait_ge(gw.h, gw.n)

        return nc


    return nc


def _host_prep(L0_attn_x, L0_attn_Wq, L0_attn_Wk, L0_attn_Wv, L0_attn_Wo,
               L0_attn_bo, L0_ffn_gate_w, L0_ffn_expert_w):
    x = np.asarray(L0_attn_x, np.float32).reshape(16384, 1024)
    xpad = np.concatenate([np.zeros((HALO, D), np.float32), x], 0)
    wq = np.ascontiguousarray(np.asarray(L0_attn_Wq, np.float32))
    wk = np.ascontiguousarray(np.asarray(L0_attn_Wk, np.float32))
    wv = np.ascontiguousarray(np.asarray(L0_attn_Wv, np.float32))
    wo = np.ascontiguousarray(np.asarray(L0_attn_Wo, np.float32))
    expT = np.ascontiguousarray(
        np.asarray(L0_ffn_expert_w, np.float32).transpose(0, 2, 1))
    gT = np.ascontiguousarray(np.asarray(L0_ffn_gate_w, np.float32).T)
    bo = np.asarray(L0_attn_bo, np.float32).reshape(1, D)
    ident = np.eye(128, dtype=np.float32)
    onesc = np.ones((128, 1), np.float32)

    ii = np.arange(128)[:, None]
    jj = np.arange(256)[None, :]
    mrest = np.where((128 + ii) < jj, MASKVAL, 0.0).astype(np.float32)
    mfirst_core0 = mrest.copy()
    mfirst_core0[:, :128] = MASKVAL

    in_maps = []
    for c in range(NCORE):
        mfirst = mfirst_core0 if c == 0 else mrest
        prefm = np.tile((np.arange(NE) < c).astype(np.float32), (128, 1))
        in_maps.append({
            "xin": np.ascontiguousarray(xpad[c * TOK:c * TOK + TTOK]),
            "wq": wq, "wk": wk, "wv": wv, "wo": wo,
            "expT": expT, "gT": gT, "bo": bo,
            "ident": ident, "onesc": onesc,
            "onesr": np.ones((1, 128), np.float32),
            "onese": np.concatenate([np.ones((64, 1), np.float32),
                                     np.zeros((64, 1), np.float32)]),
            "masks": np.ascontiguousarray(
                np.stack([mfirst, mrest]).astype(np.float32)),
            "prefm": np.ascontiguousarray(prefm),
        })
    return in_maps


def kernel(L0_attn_x, L0_attn_Wq, L0_attn_Wk, L0_attn_Wv, L0_attn_Wo,
           L0_attn_bo, L0_ffn_gate_w, L0_ffn_expert_w):
    in_maps = _host_prep(L0_attn_x, L0_attn_Wq, L0_attn_Wk, L0_attn_Wv,
                         L0_attn_Wo, L0_attn_bo, L0_ffn_gate_w, L0_ffn_expert_w)
    nc = build()
    res = run_bass_kernel_spmd(nc, in_maps, core_ids=list(range(NCORE)))
    out = np.concatenate([r["out"] for r in res.results], 0)
    return out.reshape(1, 16384, 1024)




# version 13
# speedup vs baseline: 1.0070x; 1.0070x over previous
import os
import numpy as np
import concourse.bass as bass
import concourse.mybir as mybir
from concourse.bass_utils import run_bass_kernel_spmd
from contextlib import ExitStack

F32 = mybir.dt.float32
BF16 = mybir.dt.bfloat16
U8 = mybir.dt.uint8
AF = mybir.ActivationFunctionType
ALU = mybir.AluOpType
AX = mybir.AxisListType

STAGE = int(os.environ.get("MK_STAGE", "3"))
NEXP = int(os.environ.get("MK_NEXP", "8"))
NOGATE = int(os.environ.get("MK_NOGATE", "0"))
STOP = os.environ.get("MK_STOP", "")
P6N = int(os.environ.get("MK_P6N", "32"))

NCORE = 8
D = 1024
NCH = 8
TOK = 2048
HALO = 128
TTOK = TOK + HALO
G = 17
NE = 8
INV_D = 1.0 / D
MASKVAL = -8.0e9
NLNBUF = 4


class Sem:
    def __init__(self, ctx, nc, name):
        self.h = ctx.enter_context(nc.semaphore(name))
        self.n = 0
        self.em = 0

    def inc(self, inst, k=1):
        inst.then_inc(self.h, k)
        self.n += k
        self.em = max(self.em, self.n)
        return self.n

    def dma(self, eng, out, in_, **kw):
        eng.wait_ge(self.h, self.n)
        return self.inc(eng.dma_start(out=out, in_=in_, **kw), 16)

    def reserve(self, k):
        self.n += k
        return self.n


def build():
    nc = bass.Bass("TRN2", target_bir_lowering=False, debug=False,
                   detect_race_conditions=False)

    xin = nc.dram_tensor("xin", [TTOK, D], F32, kind="ExternalInput").ap()
    wq_d = nc.dram_tensor("wq", [D, D], F32, kind="ExternalInput").ap()
    wk_d = nc.dram_tensor("wk", [D, D], F32, kind="ExternalInput").ap()
    wv_d = nc.dram_tensor("wv", [D, D], F32, kind="ExternalInput").ap()
    wo_d = nc.dram_tensor("wo", [D, D], F32, kind="ExternalInput").ap()
    expT_d = nc.dram_tensor("expT", [NE, D, D], F32, kind="ExternalInput").ap()
    gT_d = nc.dram_tensor("gT", [D, NE], F32, kind="ExternalInput").ap()
    bo_d = nc.dram_tensor("bo", [1, D], F32, kind="ExternalInput").ap()
    ident_d = nc.dram_tensor("ident", [128, 128], F32, kind="ExternalInput").ap()
    ones_d = nc.dram_tensor("onesc", [128, 1], F32, kind="ExternalInput").ap()
    onesr_d = nc.dram_tensor("onesr", [1, 128], F32, kind="ExternalInput").ap()
    onese_d = nc.dram_tensor("onese", [128, 1], F32, kind="ExternalInput").ap()
    masks_d = nc.dram_tensor("masks", [2, 128, 256], F32, kind="ExternalInput").ap()
    pref_d = nc.dram_tensor("prefm", [128, NE], F32, kind="ExternalInput").ap()
    out_d = nc.dram_tensor("out", [TOK, D], F32, kind="ExternalOutput").ap()
    dbg_d = nc.dram_tensor("dbg", [128, 1024], F32, kind="ExternalOutput").ap()
    x1sp = nc.dram_tensor("x1spill", [TOK, D], F32).ap()
    ag_in = nc.dram_tensor("ag_in", [128, 390], F32)
    ag_out = nc.dram_tensor("ag_out", [128 * NCORE, 390], F32)

    ctx = ExitStack()
    with ctx:
        E = ctx.enter_context

        R1 = E(nc.sbuf_tensor("R1", [128, 73728], U8))
        r1b = R1.ap().bitcast(BF16)
        r1f = R1.ap().bitcast(F32)
        x_res = r1f[:, 0:17408].rearrange("p (g d) -> p g d", d=D)
        kg = r1b[:, 0:12288].rearrange("p (g d) -> p g d", d=768)
        vg = r1b[:, 12288:24576].rearrange("p (g d) -> p g d", d=768)
        qs_fm = r1b[:, 24576:36864].rearrange("p (j t) -> p j t", t=TOK)
        x2 = r1f[:, 0:16384].rearrange("p (g d) -> p g d", d=D)

        A = E(nc.sbuf_tensor("A", [128, 34816], U8))
        ab = A.ap().bitcast(BF16)
        xn17 = ab.rearrange("p (c t) -> p c t", t=TTOK)
        afm = ab[:, 0:16384].rearrange("p (c t) -> p c t", t=TOK)

        wbuf = [E(nc.sbuf_tensor(f"wb{i}", [128, NCH, 512], BF16))
                for i in range(3)]
        qloc = E(nc.sbuf_tensor("qloc", [128, 2, TOK], BF16))
        kloc = E(nc.sbuf_tensor("kloc", [128, 2, TTOK], BF16))
        vloc = E(nc.sbuf_tensor("vloc", [128, G, 256], BF16))
        xn_tm = E(nc.sbuf_tensor("xn_tm", [128, 4, D], BF16))
        sqscr = E(nc.sbuf_tensor("sqscr", [128, D], F32))
        scr_a = E(nc.sbuf_tensor("scr_a", [128, 2, 768], F32))

        qstm = E(nc.sbuf_tensor("qstm", [128, 2, 768], BF16))
        sqdmp = qstm.ap().rearrange("p a b -> p (a b)")[:, 0:1024]
        vstage = E(nc.sbuf_tensor("vstage", [128, 768], BF16))
        x_roll = E(nc.sbuf_tensor("x_roll", [128, 2, D], F32))
        scr_b = x_roll.ap()[:, :, 0:768]
        ctxs2 = E(nc.sbuf_tensor("ctxs2", [128, 2, 6, 128], BF16))
        ctxf = E(nc.sbuf_tensor("ctxf", [128, 6, 64], F32))
        w_full = E(nc.sbuf_tensor("w_full", [128, G, NE], F32))
        stats = E(nc.sbuf_tensor("stats", [128, 16], F32))
        gstat = E(nc.sbuf_tensor("gstat", [128, 8], F32))
        masks = E(nc.sbuf_tensor("masks_sb", [128, 2, 256], F32))
        probs_f = E(nc.sbuf_tensor("probs_f", [128, 2, 256], F32))
        probs_b = E(nc.sbuf_tensor("probs_b", [128, 2, 256], BF16))
        pts = E(nc.sbuf_tensor("pts", [128, 2, 128], BF16))
        ident = E(nc.sbuf_tensor("identb", [128, 128], BF16))
        onesc = E(nc.sbuf_tensor("onescb", [128, 1], BF16))
        onesr = E(nc.sbuf_tensor("onesrb", [1, 128], BF16))
        onese = E(nc.sbuf_tensor("oneseb", [128, 1], BF16))
        gT_sb = E(nc.sbuf_tensor("gTsb", [128, NCH, NE], BF16))
        bo_sb = E(nc.sbuf_tensor("bosb", [1, D], BF16))
        pref = E(nc.sbuf_tensor("prefsb", [128, NE], F32))
        tsb = E(nc.sbuf_tensor("tsb", [128, 6, 65], F32))
        tbuf = E(nc.sbuf_tensor("tbuf", [128, 6, 65], F32))
        carry = E(nc.sbuf_tensor("carry", [128, 6, 65], F32))
        rsg = E(nc.sbuf_tensor("rsg", [128, 12], F32))
        zrow = E(nc.sbuf_tensor("zrow", [128, 8], F32))

        sd = Sem(ctx, nc, "sd")
        gw = Sem(ctx, nc, "gw")
        cc = Sem(ctx, nc, "cc")
        tS = Sem(ctx, nc, "tS")
        vS = Sem(ctx, nc, "vS")
        sS = Sem(ctx, nc, "sS")

        block = E(nc.Block())

        wload_plans = []
        wslot_rr = [0]
        wslot_tenant = {}

        def plan_wloads(loads, slots=(0, 1, 2)):
            res = {}
            plan = []
            for key, ap in loads:
                slot = slots[wslot_rr[0] % len(slots)]
                wslot_rr[0] += 1
                free_ref = wslot_tenant.get(slot)
                tgt = gw.reserve(16)
                myref = [None, None]
                wslot_tenant[slot] = myref
                plan.append((key, ap, slot, tgt, free_ref))
                res[key] = (slot, tgt, myref)
            wload_plans.append(plan)
            return res

        def emit_wload_body(plan):
            @block.gpsimd
            def _(g):
                for key, ap, slot, tgt, free_ref in plan:
                    if free_ref is not None:
                        assert free_ref[0] is not None, f"free target unset {key}"
                        g.wait_ge(free_ref[0].h, free_ref[1])
                    g.wait_ge(gw.h, tgt - 16)
                    g.dma_start(
                        out=wbuf[slot][:, :, :],
                        in_=ap.rearrange("(c p) d -> p c d", p=128)
                    ).then_inc(gw.h, 16)
                    gw.em = max(gw.em, tgt)

        xload_tgt = []

        @block.sync
        def _(sy):
            for i in range(5):
                g0, g1 = i * 4, min((i + 1) * 4, G)
                if g0 >= g1:
                    break
                t = sd.dma(sy, x_res[:, g0:g1, :],
                           xin[g0 * 128:g1 * 128, :].rearrange(
                               "(g p) d -> p g d", p=128))
                xload_tgt.extend([t] * (g1 - g0))
            sd.dma(sy, masks[:, :, :], masks_d.rearrange("m p k -> p m k"))
            sd.dma(sy, pref[:, :], pref_d)
        misc_ld = sd.n

        @block.gpsimd
        def _(g):
            gw.dma(g, ident[:, :], ident_d)
            gw.dma(g, onesc[:, :], ones_d)
            gw.dma(g, onesr[:, :], onesr_d)
            gw.dma(g, onese[:, :], onese_d)
            gw.dma(g, gT_sb[:, :, :], gT_d.rearrange("(c p) e -> p c e", p=128))
            gw.dma(g, bo_sb[:, :], bo_d)
        consts_tgt = gw.n

        eps_tgt = [None]

        @block.vector
        def _(v):
            v.memset(ctxs2.ap().bitcast(U8), 0)
            v.memset(vstage.ap().bitcast(U8), 0)
            vS.inc(v.memset(stats[:, 6:7], 1e-5))
            eps_tgt[0] = vS.n

        @block.scalar
        def _(s):
            from concourse.hw_specs import get_activation_tables
            need = {AF.Exp, AF.Ln, AF.Square, AF.Identity}
            for idx, funcs in enumerate(get_activation_tables(nc.m.arch).values()):
                if need.issubset(set(funcs)):
                    s.add_instruction(mybir.InstLoadActFuncSet(
                        name=nc.get_next_instruction_name(),
                        act_func_set_id=idx, ins=[], outs=[]))
                    break
            else:
                raise RuntimeError("no act table set with all needed funcs")

        wslot_rr[0] = 0
        ffn1_loads = plan_wloads(
            [((e, h), expT_d[e, :, 512 * h:512 * h + 512]) for e in range(NE)
             for h in range(2)])

        def emit_sln(g, src, extra_s_wait, buf):
            s = nc.scalar
            s.wait_ge(vS.h, eps_tgt[0])
            if extra_s_wait is not None:
                s.wait_ge(extra_s_wait[0].h, extra_s_wait[1])
            tv = sS.inc(s.activation(sqdmp[:, :], src, AF.Identity,
                                     accum_out=stats[:, 0:1]))
            s.wait_ge(sS.h, tv)
            tv = sS.inc(s.activation(stats[:, 1:2], stats[:, 0:1], AF.Identity,
                                     scale=-INV_D))
            s.wait_ge(sS.h, tv)
            tv = sS.inc(s.activation(sqscr[:, :], src, AF.Identity,
                                     bias=stats[:, 1:2]))
            s.wait_ge(sS.h, tv)
            tv = sS.inc(s.activation(sqdmp[:, :], sqscr[:, :], AF.Square,
                                     accum_out=stats[:, 2:3]))
            s.wait_ge(sS.h, tv)
            tv = sS.inc(s.activation(stats[:, 3:4], stats[:, 2:3], AF.Identity,
                                     scale=INV_D, bias=stats[:, 6:7]))
            s.wait_ge(sS.h, tv)
            tv = sS.inc(s.activation(stats[:, 4:5], stats[:, 3:4], AF.Ln))
            s.wait_ge(sS.h, tv)
            tv = sS.inc(s.activation(stats[:, 5:6], stats[:, 4:5], AF.Exp,
                                     scale=-0.5))
            s.wait_ge(sS.h, tv)
            tv = sS.inc(s.activation(xn_tm[:, buf, :], sqscr[:, :],
                                     AF.Identity, scale=stats[:, 5:6]))
            return tv

        def emit_ln(groups, src_fn, xn_view, tok_off, pT, extra_v_wait=None,
                    extra_s_wait=None, spill=False):
            v0, s0, t0 = vS.n, sS.n, tS.n
            n = len(groups)
            ln_s = {}

            @block.scalar
            def _(s):
                for i, g in enumerate(groups):
                    w = extra_s_wait(g) if extra_s_wait is not None else None
                    if i >= NLNBUF:
                        s.wait_ge(tS.h, t0 + (i - NLNBUF) + 1)
                    ln_s[i] = emit_sln(g, src_fn(g), w, i % NLNBUF)

            @block.tensor
            def _(t):
                t.wait_ge(gw.h, consts_tgt)
                for i, g in enumerate(groups):
                    t.wait_ge(sS.h, s0 + 8 * i + 8)
                    if i >= 2:
                        t.wait_ge(vS.h, v0 + (i - 2) + 1)
                    for c in range(NCH):
                        ins = t.transpose(pT[:, i % 2, c, :],
                                          xn_tm[:, i % NLNBUF,
                                                c * 128:(c + 1) * 128],
                                          ident[:, :])
                    tS.inc(ins)

            @block.vector
            def _(v):
                for i, g in enumerate(groups):
                    v.wait_ge(tS.h, t0 + i + 1)
                    vS.inc(v.tensor_copy(
                        xn_view[:, :, tok_off(g):tok_off(g) + 128],
                        pT[:, i % 2, :, :]))

            if spill:
                @block.sync
                def _(sy):
                    for i, g in enumerate(groups):
                        if g == 0:
                            continue
                        sy.wait_ge(sS.h, ln_s[i])
                        sd.dma(sy, x1sp[(g - 1) * 128:g * 128, :], src_fn(g))
            ln_t = {i: t0 + i + 1 for i in range(n)}
            ln_v = {i: v0 + i + 1 for i in range(n)}
            return ln_s, ln_t, ln_v

        def emit_lnffn(groups, src_fn, xn_view, tok_off, res_fn, loads,
                       pY, pL, pT, extra_s_wait=None, spill=False):
            n = len(groups)
            ln_s = {}
            ln_t = {}
            ln_v = {}
            f0_t = {}
            f0_v = {}
            fvg = {}
            sg = {}
            sl00, sl01 = loads[(0, 0)], loads[(0, 1)]

            def e0_mms(t, j):
                g = groups[j]
                if j == 0:
                    t.wait_ge(gw.h, max(sl00[1], sl01[1]))
                t.wait_ge(vS.h, ln_v[j])
                if j >= 2:
                    t.wait_ge(vS.h, f0_v[j - 2])
                par = j % 2
                to = tok_off(g)
                for c in range(NCH):
                    lhs = xn_view[:, c, to:to + 128]
                    t.matmul(pY[:, par, 0, :], lhs, wbuf[sl00[0]][:, c, :],
                             start=(c == 0), stop=(c == NCH - 1))
                    t.matmul(pY[:, par, 1, :], lhs, wbuf[sl01[0]][:, c, :],
                             start=(c == 0), stop=(c == NCH - 1))
                    ins = t.matmul(pL[:, par, 0:8], lhs, gT_sb[:, c, :],
                                   start=(c == 0), stop=(c == NCH - 1))
                tS.inc(ins)
                f0_t[j] = tS.n

            def gating_part1(v, j):
                par = j % 2
                pl = pL[:, par, 0:8]
                v.wait_ge(tS.h, f0_t[j])
                tv = vS.inc(v.reduce_max(stats[:, 8:9], pl, axis=AX.X))
                v.wait_ge(vS.h, tv)
                tv = vS.inc(v.tensor_tensor(
                    zrow[:, :], pl, stats[:, 8:9].broadcast_to([128, NE]),
                    op=ALU.is_ge))
                v.wait_ge(vS.h, tv)
                tv = vS.inc(v.scalar_tensor_tensor(
                    w_full[:, groups[j], :], zrow[:, :], -1e30, pl,
                    op0=ALU.mult, op1=ALU.add))
                v.wait_ge(vS.h, tv)
                tv = vS.inc(v.reduce_max(stats[:, 9:10], w_full[:, groups[j], :],
                                         axis=AX.X))
                v.wait_ge(vS.h, tv)
                vS.inc(v.tensor_scalar_mul(stats[:, 10:11], stats[:, 8:9],
                                           -1.0))
                fvg[j] = vS.n

            def gating_exps(s, j):
                s.wait_ge(vS.h, fvg[j])
                par = j % 2
                sS.inc(s.activation(gstat[:, 0:8], pL[:, par, 0:8], AF.Exp,
                                    bias=stats[:, 10:11]))
                sS.inc(s.activation(stats[:, 11:12], stats[:, 9:10], AF.Exp,
                                    bias=stats[:, 10:11]))
                sg[j] = sS.n

            def gating_part2_combine(v, j):
                par = j % 2
                pl = pL[:, par, 0:8]
                g = groups[j]
                v.wait_ge(sS.h, sg[j])
                tv = vS.inc(v.tensor_scalar(
                    stats[:, 12:13], stats[:, 11:12], 2.0, 2.0,
                    op0=ALU.mult, op1=ALU.add))
                v.wait_ge(vS.h, tv)
                tv = vS.inc(v.reciprocal(stats[:, 13:14], stats[:, 12:13]))
                v.wait_ge(vS.h, tv)
                tv = vS.inc(v.tensor_tensor(
                    zrow[:, :], pl, stats[:, 9:10].broadcast_to([128, NE]),
                    op=ALU.is_ge))
                v.wait_ge(vS.h, tv)
                tv = vS.inc(v.scalar_tensor_tensor(
                    w_full[:, g, :], gstat[:, 0:8], stats[:, 13:14],
                    zrow[:, :], op0=ALU.mult, op1=ALU.mult))
                v.wait_ge(vS.h, tv)
                wcol = w_full[:, g, 0:1]
                v.scalar_tensor_tensor(
                    res_fn(g, 0), pY[:, par, 0, :], wcol, res_fn(g, 0),
                    op0=ALU.mult, op1=ALU.add)
                vS.inc(v.scalar_tensor_tensor(
                    res_fn(g, 1), pY[:, par, 1, :], wcol, res_fn(g, 1),
                    op0=ALU.mult, op1=ALU.add))
                f0_v[j] = vS.n

            for i, g in enumerate(groups):
                @block.scalar
                def _(s, i=i, g=g):
                    if i >= NLNBUF:
                        s.wait_ge(tS.h, ln_t[i - NLNBUF])
                    w = extra_s_wait(g) if extra_s_wait is not None else None
                    ln_s[i] = emit_sln(g, src_fn(g), w, i % NLNBUF)
                    if i >= 2:
                        gating_exps(s, i - 2)

                @block.tensor
                def _(t, i=i):
                    if i == 0:
                        t.wait_ge(gw.h, consts_tgt)
                    t.wait_ge(sS.h, ln_s[i])
                    if i >= 2:
                        t.wait_ge(vS.h, ln_v[i - 2])
                    for c in range(NCH):
                        ins = t.transpose(pT[:, i % 2, c, :],
                                          xn_tm[:, i % NLNBUF,
                                                c * 128:(c + 1) * 128],
                                          ident[:, :])
                    tS.inc(ins)
                    ln_t[i] = tS.n
                    if i >= 1:
                        e0_mms(t, i - 1)

                @block.vector
                def _(v, i=i, g=g):
                    v.wait_ge(tS.h, ln_t[i])
                    vS.inc(v.tensor_copy(
                        xn_view[:, :, tok_off(g):tok_off(g) + 128],
                        pT[:, i % 2, :, :]))
                    ln_v[i] = vS.n
                    if i >= 2:
                        gating_part2_combine(v, i - 2)
                    if i >= 1:
                        gating_part1(v, i - 1)

            @block.tensor
            def _(t):
                e0_mms(t, n - 1)

            @block.scalar
            def _(s):
                gating_exps(s, n - 2)

            @block.vector
            def _(v):
                gating_part2_combine(v, n - 2)
                gating_part1(v, n - 1)

            @block.scalar
            def _(s):
                gating_exps(s, n - 1)

            @block.vector
            def _(v):
                gating_part2_combine(v, n - 1)

            t_base = tS.n
            v_base = vS.n

            @block.tensor
            def _(t):
                for e in range(1, NEXP):
                    for h in range(2):
                        sl = loads[(e, h)]
                        t.wait_ge(gw.h, sl[1])
                        for i, g in enumerate(groups):
                            m = (2 * (e - 1) + h) * n + i
                            if m >= 2:
                                t.wait_ge(vS.h, v_base + m - 1)
                            else:
                                t.wait_ge(vS.h, f0_v[n - 2 + m])
                            par = i % 2
                            to = tok_off(g)
                            for c in range(NCH):
                                ins = t.matmul(pY[:, par, h, :],
                                               xn_view[:, c, to:to + 128],
                                               wbuf[sl[0]][:, c, :],
                                               start=(c == 0),
                                               stop=(c == NCH - 1))
                            tS.inc(ins)

            @block.vector
            def _(v):
                for e in range(1, NEXP):
                    for h in range(2):
                        for i, g in enumerate(groups):
                            m = (2 * (e - 1) + h) * n + i
                            v.wait_ge(tS.h, t_base + m + 1)
                            wcol = w_full[:, g, e:e + 1]
                            vS.inc(v.scalar_tensor_tensor(
                                res_fn(g, h), pY[:, i % 2, h, :], wcol,
                                res_fn(g, h), op0=ALU.mult, op1=ALU.add))

            loads[(0, 0)][2][0] = tS
            loads[(0, 0)][2][1] = f0_t[n - 1]
            loads[(0, 1)][2][0] = tS
            loads[(0, 1)][2][1] = f0_t[n - 1]
            for e in range(1, NEXP):
                for h in range(2):
                    ref = loads[(e, h)][2]
                    ref[0] = tS
                    ref[1] = t_base + (2 * (e - 1) + h) * n + n
            xdone = {i: v_base + 13 * n + i + 1 for i in range(n)}
            if spill:
                @block.sync
                def _(sy):
                    for i, g in enumerate(groups):
                        if g == 0:
                            continue
                        sy.wait_ge(vS.h, xdone[i])
                        sd.dma(sy, x1sp[(g - 1) * 128:g * 128, :], src_fn(g))
            return ln_v, xdone

        with nc.psum_tensor("pT1", [128, 2, NCH, 128], BF16) as pT1, \
             nc.psum_tensor("pY1", [128, 2, 2, 512], F32) as pY1, \
             nc.psum_tensor("pL1", [128, 2, 512], F32) as pL1:

            groups17 = list(range(G))
            _, xdone1 = emit_lnffn(
                groups17, lambda g: x_res[:, g, :], xn17, lambda g: g * 128,
                lambda g, h: x_res[:, g, 512 * h:512 * h + 512],
                ffn1_loads, pY1, pL1, pT1,
                extra_s_wait=lambda g: (sd, xload_tgt[g]))
            emit_wload_body(wload_plans[0])

        def x1_done(g):
            return xdone1[groups17.index(g)]

        if STAGE == 1:
            @block.gpsimd
            def _(g):
                g.wait_ge(vS.h, x1_done(16))
                gw.dma(g, dbg_d[:, 0:136], w_full[:, :, :].rearrange("p a b -> p (a b)"))

            @block.sync
            def _(sy):
                for g in range(1, G):
                    sy.wait_ge(vS.h, x1_done(g))
                    sd.dma(sy, out_d[(g - 1) * 128:g * 128, :], x_res[:, g, :])
                sy.wait_ge(sd.h, sd.n)
                sy.wait_ge(gw.h, gw.em)
            return nc

        def finish_early(tag):
            if STOP != tag:
                return False
            dbgsel = os.environ.get("MK_DBG", "")
            if dbgsel:
                @block.vector
                def _(v):
                    v.wait_ge(tS.h, tS.n)
                    v.wait_ge(sS.h, sS.n)
                    if dbgsel == "kv":
                        v.tensor_copy(scr_a[:, 0, :], kg[:, 0, :])
                        vS.inc(v.tensor_copy(scr_b[:, 0, 0:256],
                                             vg[:, 0, 0:256]))
                    elif dbgsel == "qs":
                        vS.inc(v.tensor_copy(
                            scr_a[:, 0, :].rearrange("p (j t) -> p j t", t=128),
                            qs_fm[:, :, 0:128]))
                    elif dbgsel == "afm":
                        vS.inc(v.tensor_copy(
                            scr_a[:, 0, :].rearrange("p (c t) -> p c t", t=128),
                            afm[:, 2:8, 64:192]))
                dbg_v = vS.n

                @block.gpsimd
                def _(g):
                    g.wait_ge(vS.h, dbg_v)
                    gw.dma(g, dbg_d[:, 0:768], scr_a[:, 0, :])
                    if dbgsel == "kv":
                        gw.dma(g, dbg_d[:, 768:1024], scr_b[:, 0, 0:256])

            @block.sync
            def _(sy):
                sy.wait_ge(tS.h, tS.n)
                sy.wait_ge(vS.h, vS.n)
                sy.wait_ge(sS.h, sS.n)
                sy.wait_ge(gw.h, gw.em)
                for g in range(1, G):
                    sd.dma(sy, out_d[(g - 1) * 128:g * 128, :], x_res[:, g, :])
                sy.wait_ge(sd.h, sd.n)
            return True

        _scopes = []

        def _phase_scope(tag):
            if _scopes:
                nm, sid = _scopes.pop()
                nc.leave_named_scope(nm, sid, False)
            sid, _ = nc.enter_named_scope(tag, False)
            _scopes.append((tag, sid))

        def barrier_all():
            tT, tV, tSS = tS.n, vS.n, sS.n

            @block.tensor
            def _(t):
                t.wait_ge(vS.h, tV)
                t.wait_ge(sS.h, tSS)

            @block.vector
            def _(v):
                v.wait_ge(tS.h, tT)
                v.wait_ge(sS.h, tSS)

            @block.scalar
            def _(s):
                s.wait_ge(tS.h, tT)
                s.wait_ge(vS.h, tV)

        wslot_rr[0] = 0
        proj_loads = plan_wloads([
            (("q", 0), wq_d[:, 0:512]), (("q", 1), wq_d[:, 512:1024]),
            (("k", 0), wk_d[:, 0:512]), (("k", 1), wk_d[:, 512:1024]),
            (("v", 0), wv_d[:, 0:512]), (("v", 1), wv_d[:, 512:1024]),
            (("o", 0), wo_d[:, 0:512]), (("o", 1), wo_d[:, 512:1024]),
        ])
        proj_plan = wload_plans[-1]
        emit_wload_body(proj_plan[0:3])

        _phase_scope("ph_p3")
        barrier_all()
        sl0, sl1 = proj_loads[("q", 0)], proj_loads[("q", 1)]
        qloc_done = {}
        with nc.psum_tensor("pT3", [128, 2, NCH, 128], BF16) as pT3, \
             nc.psum_tensor("pql", [128, 2, 2, 512], F32) as pql:
            ln_s3 = {}
            ln_t3 = {}
            ln_v3 = {}
            ql_t = {}

            def qloc_mms(t, ti):
                if ti == 0:
                    t.wait_ge(gw.h, sl0[1])
                t.wait_ge(vS.h, ln_v3[min(4 * ti + 5, G - 1)])
                if ti >= 2:
                    t.wait_ge(vS.h, qloc_done[ti - 2])
                t0c = 128 + 512 * ti
                par = ti % 2
                for dch in range(2):
                    for c in range(NCH):
                        ins = t.matmul(
                            pql[:, par, dch, :],
                            wbuf[sl0[0]][:, c, dch * 128:(dch + 1) * 128],
                            xn17[:, c, t0c:t0c + 512],
                            start=(c == 0), stop=(c == NCH - 1))
                    tS.inc(ins)
                ql_t[ti] = tS.n

            def qloc_evac(v, ti):
                v.wait_ge(tS.h, ql_t[ti])
                t0c = 512 * ti
                vS.inc(v.tensor_copy(qloc[:, :, t0c:t0c + 512],
                                     pql[:, ti % 2, 0:2, :]))
                qloc_done[ti] = vS.n

            for i in range(G):
                @block.scalar
                def _(s, i=i):
                    if i >= NLNBUF:
                        s.wait_ge(tS.h, ln_t3[i - NLNBUF])
                    ln_s3[i] = emit_sln(i, x_res[:, i, :],
                                        (vS, x1_done(i)), i % NLNBUF)

                @block.tensor
                def _(t, i=i):
                    t.wait_ge(sS.h, ln_s3[i])
                    if i >= 2:
                        t.wait_ge(vS.h, ln_v3[i - 2])
                    for c in range(NCH):
                        ins = t.transpose(pT3[:, i % 2, c, :],
                                          xn_tm[:, i % NLNBUF,
                                                c * 128:(c + 1) * 128],
                                          ident[:, :])
                    tS.inc(ins)
                    ln_t3[i] = tS.n
                    if i in (6, 10, 14):
                        qloc_mms(t, (i - 6) // 4)

                @block.vector
                def _(v, i=i):
                    v.wait_ge(tS.h, ln_t3[i])
                    vS.inc(v.tensor_copy(
                        xn17[:, :, i * 128:i * 128 + 128],
                        pT3[:, i % 2, :, :]))
                    ln_v3[i] = vS.n
                    if i in (7, 11, 15):
                        qloc_evac(v, (i - 7) // 4)

            @block.tensor
            def _(t):
                qloc_mms(t, 3)

            @block.vector
            def _(v):
                qloc_evac(v, 3)

            @block.sync
            def _(sy):
                for i in range(1, G):
                    sy.wait_ge(sS.h, ln_s3[i])
                    sd.dma(sy, x1sp[(i - 1) * 128:i * 128, :], x_res[:, i, :])

        spill_tgt = sd.n

        @block.vector
        def _(v):
            v.wait_ge(sd.h, spill_tgt)

        @block.scalar
        def _(s):
            s.wait_ge(sd.h, spill_tgt)

        @block.tensor
        def _(t):
            t.wait_ge(sd.h, spill_tgt)



        GOWN = list(range(1, G))

        def emit_fm_proj(wkey, dst, dst_ncols, tok_lo, tok_n, loads):
            sl0 = loads[(wkey, 0)]
            tiles = []
            t_ = tok_lo
            while t_ < tok_lo + tok_n:
                w = min(512, tok_lo + tok_n - t_)
                tiles.append((t_, w))
                t_ += w
            ev_tgt = {}
            mm_tgt = {}
            for ti, (t0c, w) in enumerate(tiles):
                @block.tensor
                def _(t, ti=ti, t0c=t0c, w=w):
                    if ti == 0:
                        t.wait_ge(gw.h, sl0[1])
                    if ti >= 2:
                        t.wait_ge(vS.h, ev_tgt[ti - 2])
                    par = ti % 2
                    for dch in range(dst_ncols // 128):
                        for c in range(NCH):
                            ins = t.matmul(
                                pql[:, par, dch, 0:w],
                                wbuf[sl0[0]][:, c, dch * 128:(dch + 1) * 128],
                                xn17[:, c, t0c:t0c + w],
                                start=(c == 0), stop=(c == NCH - 1))
                        tS.inc(ins)
                    mm_tgt[ti] = tS.n

                @block.vector
                def _(v, ti=ti, t0c=t0c, w=w):
                    v.wait_ge(tS.h, mm_tgt[ti])
                    par = ti % 2
                    vS.inc(v.tensor_copy(
                        dst[:, :, t0c - tok_lo:t0c - tok_lo + w],
                        pql[:, par, 0:dst_ncols // 128, 0:w]))
                    ev_tgt[ti] = vS.n
            return ev_tgt

        if finish_early('p4a'):
            return nc

        _phase_scope("ph_p4b")
        barrier_all()
        qg_v2 = {}
        with nc.psum_tensor("pqg", [128, 2, 2, 512], F32) as pqg, \
             nc.psum_tensor("ptrq", [128, 2, 6, 128], BF16) as ptrq:
            qg_t = {}
            qg_s = {}
            qg_v1 = {}
            qg_t2 = {}
            sl0, sl1 = proj_loads[("q", 0)], proj_loads[("q", 1)]
            for i, g in enumerate(GOWN):
                par = i % 2

                @block.tensor
                def _(t, i=i, g=g, par=par):
                    if i == 0:
                        t.wait_ge(gw.h, max(sl0[1], sl1[1]))
                    if i >= 2:
                        t.wait_ge(sS.h, qg_s[i - 2])
                        t.wait_ge(vS.h, qg_v1[i - 2])
                    to = g * 128
                    for c in range(NCH):
                        t.matmul(pqg[:, par, 0, 0:256],
                                 xn17[:, c, to:to + 128],
                                 wbuf[sl0[0]][:, c, 256:512],
                                 start=(c == 0), stop=(c == NCH - 1))
                        ins = t.matmul(pqg[:, par, 1, :],
                                       xn17[:, c, to:to + 128],
                                       wbuf[sl1[0]][:, c, :],
                                       start=(c == 0), stop=(c == NCH - 1))
                    tS.inc(ins)
                    qg_t[i] = tS.n

                @block.scalar
                def _(s, i=i, par=par):
                    s.wait_ge(tS.h, qg_t[i])
                    sS.inc(s.activation(scr_a[:, par, 0:256],
                                        pqg[:, par, 0, 0:256], AF.Exp))
                    sS.inc(s.activation(scr_a[:, par, 256:768],
                                        pqg[:, par, 1, :], AF.Exp))
                    qg_s[i] = sS.n

                @block.vector
                def _(v, i=i, par=par):
                    v.wait_ge(sS.h, qg_s[i])
                    sa = scr_a[:, par, :].rearrange("p (h e) -> p h e", e=64)
                    tv = vS.inc(v.reduce_sum(rsg[:, :], sa, axis=AX.X))
                    v.wait_ge(vS.h, tv)
                    tv = vS.inc(v.reciprocal(rsg[:, :], rsg[:, :]))
                    v.wait_ge(vS.h, tv)
                    vS.inc(v.tensor_tensor(
                        qstm[:, par, :].rearrange("p (h e) -> p h e", e=64),
                        sa,
                        rsg[:, :].unsqueeze(2).broadcast_to([128, 12, 64]),
                        op=ALU.mult))
                    qg_v1[i] = vS.n

                @block.tensor
                def _(t, i=i, par=par):
                    t.wait_ge(vS.h, qg_v1[i])
                    if i >= 2:
                        t.wait_ge(vS.h, qg_v2[i - 2])
                    for j in range(6):
                        ins = t.transpose(ptrq[:, par, j, :],
                                          qstm[:, par, j * 128:(j + 1) * 128],
                                          ident[:, :])
                    tS.inc(ins)
                    qg_t2[i] = tS.n

                @block.vector
                def _(v, i=i, g=g, par=par):
                    v.wait_ge(tS.h, qg_t2[i])
                    vS.inc(v.tensor_copy(
                        qs_fm[:, :, (g - 1) * 128:g * 128],
                        ptrq[:, par, :, :]))
                    qg_v2[i] = vS.n

        if finish_early('p4b'):
            return nc

        for h in range(2):
            proj_loads[("q", h)][2][0] = tS
            proj_loads[("q", h)][2][1] = tS.n
        emit_wload_body(proj_plan[3:5])

        _phase_scope("ph_p4c")
        barrier_all()
        with nc.psum_tensor("pkl", [128, 2, 2, 512], F32) as pql:
            kloc_done = emit_fm_proj("k", kloc, 256, 0, TTOK, proj_loads)

        if finish_early('p4c'):
            return nc

        _phase_scope("ph_p4d")
        barrier_all()
        with nc.psum_tensor("pkg", [128, 2, 2, 512], F32) as pkg:
            kg_t = {}
            kg_s = {}
            kg_vm = {}
            kg_v = {}
            sl0, sl1 = proj_loads[("k", 0)], proj_loads[("k", 1)]
            for i, g in enumerate(GOWN):
                par = i % 2

                @block.tensor
                def _(t, i=i, g=g, par=par):
                    if i == 0:
                        t.wait_ge(gw.h, max(sl0[1], sl1[1]))
                    if i >= 2:
                        t.wait_ge(sS.h, kg_s[i - 2])
                        t.wait_ge(vS.h, kg_vm[i - 2])
                    to = g * 128
                    for c in range(NCH):
                        t.matmul(pkg[:, par, 0, 0:256],
                                 xn17[:, c, to:to + 128],
                                 wbuf[sl0[0]][:, c, 256:512],
                                 start=(c == 0), stop=(c == NCH - 1))
                        ins = t.matmul(pkg[:, par, 1, :],
                                       xn17[:, c, to:to + 128],
                                       wbuf[sl1[0]][:, c, :],
                                       start=(c == 0), stop=(c == NCH - 1))
                    tS.inc(ins)
                    kg_t[i] = tS.n

                @block.scalar
                def _(s, i=i, par=par):
                    s.wait_ge(tS.h, kg_t[i])
                    if i >= 1:
                        s.wait_ge(vS.h, kg_v[i - 1])
                    s.activation(scr_b[:, par, 0:256], pkg[:, par, 0, 0:256],
                                 AF.Relu)
                    sS.inc(s.activation(scr_b[:, par, 256:768],
                                        pkg[:, par, 1, :], AF.Relu))
                    kg_s[i] = sS.n

                @block.vector
                def _(v, i=i, par=par):
                    v.wait_ge(tS.h, kg_t[i])
                    v.wait_ge(sS.h, kg_s[i])
                    v.tensor_scalar_min(scr_a[:, par, 0:256],
                                        pkg[:, par, 0, 0:256], 0.0)
                    vS.inc(v.tensor_scalar_min(scr_a[:, par, 256:768],
                                               pkg[:, par, 1, :], 0.0))
                    kg_vm[i] = vS.n

                @block.scalar
                def _(s, i=i, par=par):
                    s.wait_ge(vS.h, kg_vm[i])
                    sS.inc(s.activation(sqscr[:, 0:768], scr_a[:, par, :],
                                        AF.Exp))
                    kg_s2 = sS.n
                    kg_s[i] = max(kg_s[i], kg_s2)

                @block.vector
                def _(v, i=i, g=g, par=par):
                    v.wait_ge(sS.h, kg_s[i])
                    vS.inc(v.tensor_tensor(kg[:, g - 1, :], sqscr[:, 0:768],
                                           scr_b[:, par, :], op=ALU.add))
                    kg_v[i] = vS.n

        if finish_early('p4d'):
            return nc

        for h in range(2):
            proj_loads[("k", h)][2][0] = tS
            proj_loads[("k", h)][2][1] = tS.n
        emit_wload_body(proj_plan[5:7])

        _phase_scope("ph_p4e")
        barrier_all()
        with nc.psum_tensor("pvl", [128, 512], F32) as pvl, \
             nc.psum_tensor("pvg", [128, 2, 512], F32) as pvg, \
             nc.psum_tensor("pSt", [128, 8, 128], F32) as pSt:
            vl_t = {}
            vl_v = {}
            vg_t = {}
            vg_s = {}
            st_t = {}
            st_v = {}
            slv0, slv1 = proj_loads[("v", 0)], proj_loads[("v", 1)]
            for g in range(G):
                i = g - 1

                @block.tensor
                def _(t, g=g, i=i):
                    if g == 0:
                        t.wait_ge(gw.h, max(slv0[1], slv1[1]))
                    if g >= 1:
                        t.wait_ge(vS.h, vl_v[g - 1])
                    to = g * 128
                    for c in range(NCH):
                        ins = t.matmul(pvl[:, 0:256],
                                       xn17[:, c, to:to + 128],
                                       wbuf[slv0[0]][:, c, 0:256],
                                       start=(c == 0), stop=(c == NCH - 1))
                    tS.inc(ins)
                    vl_t[g] = tS.n

                @block.vector
                def _(v, g=g):
                    v.wait_ge(tS.h, vl_t[g])
                    vS.inc(v.tensor_copy(vloc[:, g, :], pvl[:, 0:256]))
                    vl_v[g] = vS.n

                if g == 0:
                    continue
                par = i % 2

                @block.tensor
                def _(t, g=g, i=i, par=par):
                    if i >= 1:
                        t.wait_ge(sS.h, vg_s[i - 1])
                    to = g * 128
                    for c in range(NCH):
                        t.matmul(pvg[:, 0, 0:256],
                                 xn17[:, c, to:to + 128],
                                 wbuf[slv0[0]][:, c, 256:512],
                                 start=(c == 0), stop=(c == NCH - 1))
                        ins = t.matmul(pvg[:, 1, :],
                                       xn17[:, c, to:to + 128],
                                       wbuf[slv1[0]][:, c, :],
                                       start=(c == 0), stop=(c == NCH - 1))
                    tS.inc(ins)
                    vg_t[i] = tS.n

                @block.scalar
                def _(s, i=i, g=g, par=par):
                    s.wait_ge(tS.h, vg_t[i])
                    s.copy(vg[:, g - 1, 0:256], pvg[:, 0, 0:256])
                    sS.inc(s.copy(vg[:, g - 1, 256:768], pvg[:, 1, :]))
                    vg_s[i] = sS.n

                @block.tensor
                def _(t, i=i, g=g):
                    t.wait_ge(sS.h, vg_s[i])
                    t.wait_ge(vS.h, kg_v[i])
                    if i >= 1:
                        t.wait_ge(vS.h, st_v[i - 1])
                    for j in range(6):
                        t.matmul(pSt[:, j, :],
                                 kg[:, i, j * 128:(j + 1) * 128],
                                 vg[:, i, j * 128:(j + 1) * 128],
                                 start=True, stop=True)
                        ins = t.matmul(pSt[:, 6, j:j + 1],
                                       kg[:, i, j * 128:(j + 1) * 128],
                                       onesc[:, 0:1],
                                       start=True, stop=True)
                    tS.inc(ins)
                    st_t[i] = tS.n

                @block.vector
                def _(v, i=i):
                    v.wait_ge(tS.h, st_t[i])
                    if i == 0:
                        v.tensor_copy(tsb[0:64, :, 0:64], pSt[0:64, 0:6, 0:64])
                        v.tensor_copy(tsb[64:128, :, 0:64],
                                      pSt[64:128, 0:6, 64:128])
                        vS.inc(v.tensor_copy(tsb[:, :, 64:65].squeeze(2),
                                             pSt[:, 6, 0:6]))
                    else:
                        v.scalar_tensor_tensor(
                            tsb[0:64, :, 0:64], pSt[0:64, 0:6, 0:64], 1.0,
                            tsb[0:64, :, 0:64], op0=ALU.mult, op1=ALU.add)
                        v.scalar_tensor_tensor(
                            tsb[64:128, :, 0:64], pSt[64:128, 0:6, 64:128], 1.0,
                            tsb[64:128, :, 0:64], op0=ALU.mult, op1=ALU.add)
                        vS.inc(v.scalar_tensor_tensor(
                            tsb[:, :, 64:65].squeeze(2), pSt[:, 6, 0:6], 1.0,
                            tsb[:, :, 64:65].squeeze(2),
                            op0=ALU.mult, op1=ALU.add))
                    st_v[i] = vS.n

            pub_s = [sS.n]
            pub_v = [st_v[15]]

        if finish_early('p4e'):
            return nc

        for h in range(2):
            proj_loads[("v", h)][2][0] = tS
            proj_loads[("v", h)][2][1] = tS.n
        emit_wload_body(proj_plan[7:8])

        _phase_scope("ph_p5")
        barrier_all()
        with nc.psum_tensor("psc", [128, 2, 512], F32) as psc, \
             nc.psum_tensor("ppt", [128, 2, 128], BF16) as ppt, \
             nc.psum_tensor("pout", [128, 2, 512], F32) as pout:
            sc_t = {}
            v_mask = {}
            s_exp = {}
            v_pb = {}
            t_pt = {}
            s_pt = {}
            t_out = {}
            v_out = {}
            bh = -1
            for b in range(16):
                for h in range(4):
                    bh += 1
                    par = bh % 2
                    mrow = 0 if b == 0 else 1

                    @block.tensor
                    def _(t, b=b, h=h, bh=bh, par=par):
                        if bh == 0:
                            t.wait_ge(sd.h, misc_ld)
                            t.wait_ge(vS.h, qloc_done[3])
                            t.wait_ge(vS.h, kloc_done[4])
                        if bh >= 2:
                            t.wait_ge(vS.h, v_mask[bh - 2])
                        r0 = (h % 2) * 64
                        ins = t.matmul(
                            psc[:, par, 0:256],
                            qloc[r0:r0 + 64, h // 2, b * 128:(b + 1) * 128],
                            kloc[r0:r0 + 64, h // 2, b * 128:b * 128 + 256],
                            start=True, stop=True)
                        tS.inc(ins)
                        sc_t[bh] = tS.n

                    @block.vector
                    def _(v, bh=bh, par=par, mrow=mrow):
                        v.wait_ge(tS.h, sc_t[bh])
                        if bh >= 2:
                            v.wait_ge(sS.h, s_exp[bh - 2])
                        tv = vS.inc(v.tensor_tensor(
                            probs_f[:, par, :], psc[:, par, 0:256],
                            masks[:, mrow, :], op=ALU.add))
                        v.wait_ge(vS.h, tv)
                        tv = vS.inc(v.reduce_max(stats[:, 8 + par:9 + par],
                                                 probs_f[:, par, :], axis=AX.X))
                        v.wait_ge(vS.h, tv)
                        vS.inc(v.tensor_scalar_mul(
                            stats[:, 10 + par:11 + par],
                            stats[:, 8 + par:9 + par], -0.125))
                        v_mask[bh] = vS.n

                    @block.scalar
                    def _(s, bh=bh, par=par):
                        s.wait_ge(vS.h, v_mask[bh])
                        if bh >= 2:
                            s.wait_ge(vS.h, v_pb[bh - 2])
                        sS.inc(s.activation(
                            sqscr[:, par * 256:(par + 1) * 256],
                            probs_f[:, par, :], AF.Exp,
                            bias=stats[:, 10 + par:11 + par], scale=0.125,
                            accum_out=stats[:, 12 + par:13 + par]))
                        s_exp[bh] = sS.n

                    @block.vector
                    def _(v, bh=bh, par=par):
                        v.wait_ge(sS.h, s_exp[bh])
                        tv = vS.inc(v.reciprocal(stats[:, 14 + par:15 + par],
                                                 stats[:, 12 + par:13 + par]))
                        v.wait_ge(vS.h, tv)
                        vS.inc(v.tensor_scalar_mul(
                            probs_b[:, par, :],
                            sqscr[:, par * 256:(par + 1) * 256],
                            stats[:, 14 + par:15 + par]))
                        v_pb[bh] = vS.n

                    @block.tensor
                    def _(t, bh=bh, par=par):
                        t.wait_ge(vS.h, v_pb[bh])
                        if bh >= 1:
                            t.wait_ge(sS.h, s_pt[bh - 1])
                        t.transpose(ppt[:, 0, :], probs_b[:, par, 0:128],
                                    ident[:, :])
                        ins = t.transpose(ppt[:, 1, :], probs_b[:, par, 128:256],
                                         ident[:, :])
                        tS.inc(ins)
                        t_pt[bh] = tS.n

                    @block.scalar
                    def _(s, bh=bh):
                        s.wait_ge(tS.h, t_pt[bh])
                        sS.inc(s.copy(pts[:, :, :], ppt[:, :, :]))
                        s_pt[bh] = sS.n

                    @block.tensor
                    def _(t, b=b, h=h, bh=bh, par=par):
                        t.wait_ge(sS.h, s_pt[bh])
                        t.wait_ge(vS.h, vl_v[b + 1])
                        if bh >= 2:
                            t.wait_ge(vS.h, v_out[bh - 2])
                        hp = h // 2
                        t.matmul(pout[:, par, 0:128],
                                 vloc[:, b, hp * 128:(hp + 1) * 128],
                                 pts[:, 0, :], start=True, stop=False)
                        ins = t.matmul(pout[:, par, 0:128],
                                       vloc[:, b + 1, hp * 128:(hp + 1) * 128],
                                       pts[:, 1, :], start=False, stop=True)
                        tS.inc(ins)
                        t_out[bh] = tS.n

                    @block.vector
                    def _(v, b=b, h=h, bh=bh, par=par):
                        v.wait_ge(tS.h, t_out[bh])
                        r0 = (h % 2) * 64
                        vS.inc(v.tensor_copy(
                            afm[r0:r0 + 64, h // 2, b * 128:(b + 1) * 128],
                            pout[r0:r0 + 64, par, 0:128]))
                        v_out[bh] = vS.n

        tbuf_tgt = []
        carry_v = []
        cv0 = [None]

        @block.gpsimd
        def _(g):
            g.wait_ge(sS.h, pub_s[0])
            g.wait_ge(vS.h, pub_v[0])
            gw.dma(g, ag_in.ap(), tsb[:, :, :].rearrange("p j f -> p (j f)"))
            g.collective_compute(
                "AllGather", ALU.bypass,
                replica_groups=[list(range(NCORE))],
                ins=[ag_in.ap().opt()],
                outs=[ag_out.ap().opt()],
            ).then_inc(cc.h, 1)
            cc.n += 1
            g.wait_ge(cc.h, cc.n)
            vbase_c = vS.n
            for j in range(NCORE):
                if j >= 1:
                    g.wait_ge(vS.h, vbase_c + j)
                tbuf_tgt.append(gw.dma(g, tbuf[:, :, :].rearrange('p j f -> p (j f)'), ag_out.ap()[j * 128:(j + 1) * 128, :]))

        @block.vector
        def _(v):
            for j in range(NCORE):
                v.wait_ge(gw.h, tbuf_tgt[j])
                if j == 0:
                    tv = vS.inc(v.tensor_scalar_mul(
                        carry[:, :, :], tbuf[:, :, :], pref[:, 0:1]))
                else:
                    tv = vS.inc(v.scalar_tensor_tensor(
                        carry[:, :, :], tbuf[:, :, :],
                        pref[:, j:j + 1], carry[:, :, :],
                        op0=ALU.mult, op1=ALU.add))
                v.wait_ge(vS.h, tv)
                carry_v.append(tv)

        if finish_early('exch'):
            return nc

        if finish_early('p5'):
            return nc

        _phase_scope("ph_p6")
        barrier_all()
        v_ctx = {}
        t_outg = {}
        t_updA = {}
        t_updC = {}
        s_outg = {}
        s_ctx = {}
        asb = tsb
        with nc.psum_tensor("pC", [128, 8, 128], F32) as pC, \
             nc.psum_tensor("pG", [128, 8, 128], F32) as pG, \
             nc.psum_tensor("pog", [128, 2, 8, 64], F32) as pog:
            av0 = [None]

            @block.vector
            def _(v):
                v.wait_ge(vS.h, carry_v[7])
                tv = vS.inc(v.tensor_copy(asb[:, :, :], carry[:, :, :]))
                v.wait_ge(vS.h, tv)
                av0[0] = vS.n
            for u in range(P6N):
                par = u % 2
                i = u // 2
                odd = u % 2 == 1

                @block.vector
                def _(v, u=u, par=par, i=i, odd=odd):
                    if u == 0:
                        v.wait_ge(vS.h, av0[0])
                    if odd:
                        v.wait_ge(tS.h, t_updC[i])
                    if u >= 2:
                        v.wait_ge(tS.h, t_outg[u - 2])
                    if not odd and i >= 1:
                        v.wait_ge(tS.h, t_updC[i - 1])
                    if not odd:
                        vS.inc(v.tensor_copy(vstage[0:64, :], vg[0:64, i, :]))
                    if odd:
                        v.scalar_tensor_tensor(
                            ctxf[0:64, :, :], pC[0:64, 0:6, 0:64], 1.0,
                            asb[0:64, :, 0:64], op0=ALU.mult, op1=ALU.add)
                        v.scalar_tensor_tensor(
                            ctxf[64:128, :, :], pC[64:128, 0:6, 64:128], 1.0,
                            asb[64:128, :, 0:64], op0=ALU.mult, op1=ALU.add)
                        tv = vS.inc(v.scalar_tensor_tensor(
                            zrow[:, 0:6], pC[:, 6, 0:6], 1.0,
                            asb[:, :, 64:65].squeeze(2),
                            op0=ALU.mult, op1=ALU.add))
                        v.wait_ge(vS.h, tv)
                    else:
                        v.tensor_copy(ctxf[:, :, :], asb[:, :, 0:64])
                        tv = vS.inc(v.tensor_copy(zrow[:, 0:6],
                                                  asb[:, :, 64:65].squeeze(2)))
                        v.wait_ge(vS.h, tv)
                    tv = vS.inc(v.tensor_scalar_add(zrow[:, 0:6], zrow[:, 0:6],
                                                    1e-6))
                    v.wait_ge(vS.h, tv)
                    tv = vS.inc(v.reciprocal(gstat[:, 0:6], zrow[:, 0:6]))
                    v.wait_ge(vS.h, tv)
                    v.tensor_tensor(
                        ctxs2[0:64, par, :, 0:64], ctxf[0:64, :, :],
                        gstat[0:64, 0:6].unsqueeze(2).broadcast_to([64, 6, 64]),
                        op=ALU.mult)
                    tv = vS.inc(v.tensor_tensor(
                        ctxs2[64:128, par, :, 64:128], ctxf[64:128, :, :],
                        gstat[64:128, 0:6].unsqueeze(2).broadcast_to([64, 6, 64]),
                        op=ALU.mult))
                    v_ctx[u] = vS.n
                    if odd:
                        v.wait_ge(vS.h, tv)
                        v.wait_ge(tS.h, t_updA[i])
                        v.scalar_tensor_tensor(
                            asb[0:64, :, 0:64], pG[0:64, 0:6, 0:64], 1.0,
                            asb[0:64, :, 0:64], op0=ALU.mult, op1=ALU.add)
                        v.scalar_tensor_tensor(
                            asb[64:128, :, 0:64], pG[64:128, 0:6, 64:128], 1.0,
                            asb[64:128, :, 0:64], op0=ALU.mult, op1=ALU.add)
                        vS.inc(v.scalar_tensor_tensor(
                            asb[:, :, 64:65].squeeze(2), pG[:, 6, 0:6], 1.0,
                            asb[:, :, 64:65].squeeze(2),
                            op0=ALU.mult, op1=ALU.add))
                        v_ctx[u] = vS.n

                @block.tensor
                def _(t, u=u, par=par, i=i, odd=odd):
                    t.wait_ge(vS.h, v_ctx[u])
                    if u >= 2:
                        t.wait_ge(sS.h, s_outg[u - 2])
                    cs = u * 64
                    for j in range(6):
                        ins = t.matmul(pog[:, par, j, :],
                                       ctxs2[:, par, j, :],
                                       qs_fm[:, j, cs:cs + 64],
                                       start=True, stop=True)
                    tS.inc(ins)
                    t_outg[u] = tS.n
                    if not odd:
                        for j in range(6):
                            t.matmul(pC[:, j, :],
                                     kg[:, i, j * 128:(j + 1) * 128],
                                     vstage[:, j * 128:(j + 1) * 128],
                                     start=True, stop=True)
                            ins = t.matmul(pC[:, 6, j:j + 1],
                                           kg[:, i, j * 128:(j + 1) * 128],
                                           onese[:, 0:1],
                                           start=True, stop=True)
                        tS.inc(ins)
                        t_updC[i] = tS.n
                        if i >= 1:
                            t.wait_ge(vS.h, v_ctx[u - 1])
                        for j in range(6):
                            t.matmul(pG[:, j, :],
                                     kg[:, i, j * 128:(j + 1) * 128],
                                     vg[:, i, j * 128:(j + 1) * 128],
                                     start=True, stop=True)
                            ins = t.matmul(pG[:, 6, j:j + 1],
                                           kg[:, i, j * 128:(j + 1) * 128],
                                           onesc[:, 0:1],
                                           start=True, stop=True)
                        tS.inc(ins)
                        t_updA[i] = tS.n

                @block.scalar
                def _(s, u=u, par=par):
                    s.wait_ge(tS.h, t_outg[u])
                    cs = u * 64
                    for j in range(6):
                        ins = s.copy(afm[:, 2 + j, cs:cs + 64],
                                     pog[:, par, j, :])
                    sS.inc(ins)
                    s_outg[u] = sS.n

        if finish_early('p6'):
            return nc

        _phase_scope("ph_p7")
        barrier_all()
        v_x2 = {}
        with nc.psum_tensor("po", [128, 2, 2, 512], F32) as po:
            t_o = {}
            rl_tgt = {}
            slo0, slo1 = proj_loads[("o", 0)], proj_loads[("o", 1)]
            for i in range(16):
                par = i % 2

                @block.sync
                def _(sy, i=i, par=par):
                    if i < 2:
                        sy.wait_ge(vS.h, kg_v[15])
                    if i >= 2:
                        sy.wait_ge(vS.h, v_x2[i - 2])
                    rl_tgt[i] = sd.dma(sy, x_roll[:, par, :],
                                       x1sp[i * 128:(i + 1) * 128, :])

                @block.tensor
                def _(t, i=i, par=par):
                    if i == 0:
                        t.wait_ge(gw.h, max(slo0[1], slo1[1]))
                    bh_last = i * 4 + 3
                    t.wait_ge(vS.h, v_out[bh_last])
                    t.wait_ge(sS.h, s_outg[2 * i + 1])
                    if i >= 2:
                        t.wait_ge(vS.h, v_x2[i - 2])
                    to = i * 128
                    for h, slh in ((0, slo0), (1, slo1)):
                        for c in range(NCH):
                            t.matmul(po[:, par, h, :],
                                     afm[:, c, to:to + 128],
                                     wbuf[slh[0]][:, c, :],
                                     start=(c == 0), stop=False)
                        ins = t.matmul(po[:, par, h, :],
                                       onesr[0:1, :],
                                       bo_sb[0:1, 512 * h:512 * h + 512],
                                       start=False, stop=True)
                    tS.inc(ins)
                    t_o[i] = tS.n

                @block.vector
                def _(v, i=i, par=par):
                    v.wait_ge(tS.h, t_o[i])
                    v.wait_ge(sd.h, rl_tgt[i])
                    v.tensor_tensor(x2[:, i, 0:512], po[:, par, 0, :],
                                    x_roll[:, par, 0:512], op=ALU.add)
                    vS.inc(v.tensor_tensor(x2[:, i, 512:1024], po[:, par, 1, :],
                                           x_roll[:, par, 512:1024],
                                           op=ALU.add))
                    v_x2[i] = vS.n

        for h in range(2):
            proj_loads[("o", h)][2][0] = tS
            proj_loads[("o", h)][2][1] = tS.n

        if STAGE == 2:
            dbgsel = os.environ.get("MK_DBG", "")
            if dbgsel:
                @block.vector
                def _(v):
                    v.wait_ge(vS.h, vS.n)
                    if dbgsel == "kv":
                        v.tensor_copy(scr_a[:, 0, :], kg[:, 0, :])
                        vS.inc(v.tensor_copy(scr_b[:, 0, 0:256], vg[:, 0, 0:256]))
                    elif dbgsel == "qs":
                        vS.inc(v.tensor_copy(
                            scr_a[:, :, :].rearrange("p a b -> p (a b)")[:, 0:768]
                            .rearrange("p (j t) -> p j t", t=128),
                            qs_fm[:, :, 0:128]))
                    elif dbgsel == "afm":
                        vS.inc(v.tensor_copy(
                            scr_a[:, :, :].rearrange("p a b -> p (a b)")[:, 0:768]
                            .rearrange("p (c t) -> p c t", t=128),
                            afm[:, 2:8, 64:192]))
                dbg_v = vS.n

                @block.gpsimd
                def _(g):
                    g.wait_ge(vS.h, dbg_v)
                    gw.dma(g, dbg_d[:, 0:768],
                           scr_a[:, :, :].rearrange("p a b -> p (a b)")[:, 0:768])
                    if dbgsel == "kv":
                        gw.dma(g, dbg_d[:, 768:1024], scr_b[:, 0, 0:256])

            @block.sync
            def _(sy):
                for i in range(16):
                    sy.wait_ge(vS.h, v_x2[i])
                    sd.dma(sy, out_d[i * 128:(i + 1) * 128, :], x2[:, i, :])
                if os.environ.get("MK_DBG") == "spill":
                    sd.dma(sy, dbg_d[:, :], x1sp[0:128, :])
                sy.wait_ge(sd.h, sd.n)
                sy.wait_ge(gw.h, gw.em)
            return nc

        wslot_rr[0] = 0
        ffn2_loads = plan_wloads(
            [((e, h), expT_d[e, :, 512 * h:512 * h + 512]) for e in range(NE)
             for h in range(2)])
        ffn2_plan = wload_plans[-1]

        _phase_scope("ph_p89")
        barrier_all()
        with nc.psum_tensor("pT8", [128, 2, NCH, 128], BF16) as pT8, \
             nc.psum_tensor("pY8", [128, 2, 2, 512], F32) as pY8, \
             nc.psum_tensor("pL8", [128, 2, 512], F32) as pL8:
            groups16 = list(range(16))
            _, xdone2 = emit_lnffn(
                groups16, lambda g: x2[:, g, :], afm, lambda g: g * 128,
                lambda g, h: x2[:, g, 512 * h:512 * h + 512],
                ffn2_loads, pY8, pL8, pT8,
                extra_s_wait=lambda g: (vS, v_x2[g]))
            emit_wload_body(ffn2_plan)

        @block.sync
        def _(sy):
            for i in range(16):
                sy.wait_ge(vS.h, xdone2[i])
                sd.dma(sy, out_d[i * 128:(i + 1) * 128, :], x2[:, i, :])
            sy.wait_ge(sd.h, sd.n)
            sy.wait_ge(gw.h, gw.n)

        return nc


    return nc


def _host_prep(L0_attn_x, L0_attn_Wq, L0_attn_Wk, L0_attn_Wv, L0_attn_Wo,
               L0_attn_bo, L0_ffn_gate_w, L0_ffn_expert_w):
    x = np.asarray(L0_attn_x, np.float32).reshape(16384, 1024)
    xpad = np.concatenate([np.zeros((HALO, D), np.float32), x], 0)
    wq = np.ascontiguousarray(np.asarray(L0_attn_Wq, np.float32))
    wk = np.ascontiguousarray(np.asarray(L0_attn_Wk, np.float32))
    wv = np.ascontiguousarray(np.asarray(L0_attn_Wv, np.float32))
    wo = np.ascontiguousarray(np.asarray(L0_attn_Wo, np.float32))
    expT = np.ascontiguousarray(
        np.asarray(L0_ffn_expert_w, np.float32).transpose(0, 2, 1))
    gT = np.ascontiguousarray(np.asarray(L0_ffn_gate_w, np.float32).T)
    bo = np.asarray(L0_attn_bo, np.float32).reshape(1, D)
    ident = np.eye(128, dtype=np.float32)
    onesc = np.ones((128, 1), np.float32)

    ii = np.arange(128)[:, None]
    jj = np.arange(256)[None, :]
    mrest = np.where((128 + ii) < jj, MASKVAL, 0.0).astype(np.float32)
    mfirst_core0 = mrest.copy()
    mfirst_core0[:, :128] = MASKVAL

    in_maps = []
    for c in range(NCORE):
        mfirst = mfirst_core0 if c == 0 else mrest
        prefm = np.tile((np.arange(NE) < c).astype(np.float32), (128, 1))
        in_maps.append({
            "xin": np.ascontiguousarray(xpad[c * TOK:c * TOK + TTOK]),
            "wq": wq, "wk": wk, "wv": wv, "wo": wo,
            "expT": expT, "gT": gT, "bo": bo,
            "ident": ident, "onesc": onesc,
            "onesr": np.ones((1, 128), np.float32),
            "onese": np.concatenate([np.ones((64, 1), np.float32),
                                     np.zeros((64, 1), np.float32)]),
            "masks": np.ascontiguousarray(
                np.stack([mfirst, mrest]).astype(np.float32)),
            "prefm": np.ascontiguousarray(prefm),
        })
    return in_maps


def kernel(L0_attn_x, L0_attn_Wq, L0_attn_Wk, L0_attn_Wv, L0_attn_Wo,
           L0_attn_bo, L0_ffn_gate_w, L0_ffn_expert_w):
    in_maps = _host_prep(L0_attn_x, L0_attn_Wq, L0_attn_Wk, L0_attn_Wv,
                         L0_attn_Wo, L0_attn_bo, L0_ffn_gate_w, L0_ffn_expert_w)
    nc = build()
    res = run_bass_kernel_spmd(nc, in_maps, core_ids=list(range(NCORE)))
    out = np.concatenate([r["out"] for r in res.results], 0)
    return out.reshape(1, 16384, 1024)




# version 14
# speedup vs baseline: 1.0859x; 1.0783x over previous
import os
import numpy as np
import concourse.bass as bass
import concourse.mybir as mybir
from concourse.bass_utils import run_bass_kernel_spmd
from contextlib import ExitStack

F32 = mybir.dt.float32
BF16 = mybir.dt.bfloat16
U8 = mybir.dt.uint8
AF = mybir.ActivationFunctionType
ALU = mybir.AluOpType
AX = mybir.AxisListType

STAGE = int(os.environ.get("MK_STAGE", "3"))
NEXP = int(os.environ.get("MK_NEXP", "8"))
NOGATE = int(os.environ.get("MK_NOGATE", "0"))
STOP = os.environ.get("MK_STOP", "")
P6N = int(os.environ.get("MK_P6N", "32"))

NCORE = 8
D = 1024
NCH = 8
TOK = 2048
HALO = 128
TTOK = TOK + HALO
G = 17
NE = 8
INV_D = 1.0 / D
MASKVAL = -8.0e9
NLNBUF = 4


class Sem:
    def __init__(self, ctx, nc, name):
        self.h = ctx.enter_context(nc.semaphore(name))
        self.n = 0
        self.em = 0

    def inc(self, inst, k=1):
        inst.then_inc(self.h, k)
        self.n += k
        self.em = max(self.em, self.n)
        return self.n

    def dma(self, eng, out, in_, **kw):
        eng.wait_ge(self.h, self.n)
        return self.inc(eng.dma_start(out=out, in_=in_, **kw), 16)

    def reserve(self, k):
        self.n += k
        return self.n


def build():
    nc = bass.Bass("TRN2", target_bir_lowering=False, debug=False,
                   detect_race_conditions=False)

    xin = nc.dram_tensor("xin", [TTOK, D], F32, kind="ExternalInput").ap()
    wq_d = nc.dram_tensor("wq", [D, D], F32, kind="ExternalInput").ap()
    wk_d = nc.dram_tensor("wk", [D, D], F32, kind="ExternalInput").ap()
    wv_d = nc.dram_tensor("wv", [D, D], F32, kind="ExternalInput").ap()
    wo_d = nc.dram_tensor("wo", [D, D], F32, kind="ExternalInput").ap()
    expT_d = nc.dram_tensor("expT", [NE, D, D], F32, kind="ExternalInput").ap()
    gT_d = nc.dram_tensor("gT", [D, NE], F32, kind="ExternalInput").ap()
    bo_d = nc.dram_tensor("bo", [1, D], F32, kind="ExternalInput").ap()
    ident_d = nc.dram_tensor("ident", [128, 128], F32, kind="ExternalInput").ap()
    ones_d = nc.dram_tensor("onesc", [128, 1], F32, kind="ExternalInput").ap()
    onesr_d = nc.dram_tensor("onesr", [1, 128], F32, kind="ExternalInput").ap()
    onese_d = nc.dram_tensor("onese", [128, 1], F32, kind="ExternalInput").ap()
    masks_d = nc.dram_tensor("masks", [2, 128, 256], F32, kind="ExternalInput").ap()
    pref_d = nc.dram_tensor("prefm", [128, NE], F32, kind="ExternalInput").ap()
    out_d = nc.dram_tensor("out", [TOK, D], F32, kind="ExternalOutput").ap()
    dbg_d = nc.dram_tensor("dbg", [128, 1024], F32, kind="ExternalOutput").ap()
    x1sp = nc.dram_tensor("x1spill", [TOK, D], F32).ap()
    ag_in = nc.dram_tensor("ag_in", [128, 390], F32)
    ag_out = nc.dram_tensor("ag_out", [128 * NCORE, 390], F32)

    ctx = ExitStack()
    with ctx:
        E = ctx.enter_context

        R1 = E(nc.sbuf_tensor("R1", [128, 73728], U8))
        r1b = R1.ap().bitcast(BF16)
        r1f = R1.ap().bitcast(F32)
        x_res = r1f[:, 0:17408].rearrange("p (g d) -> p g d", d=D)
        kg = r1b[:, 0:12288].rearrange("p (g d) -> p g d", d=768)
        vg = r1b[:, 12288:24576].rearrange("p (g d) -> p g d", d=768)
        qs_fm = r1b[:, 24576:36864].rearrange("p (j t) -> p j t", t=TOK)
        x2 = r1f[:, 0:16384].rearrange("p (g d) -> p g d", d=D)

        A = E(nc.sbuf_tensor("A", [128, 34816], U8))
        ab = A.ap().bitcast(BF16)
        xn17 = ab.rearrange("p (c t) -> p c t", t=TTOK)
        afm = ab[:, 0:16384].rearrange("p (c t) -> p c t", t=TOK)

        wbuf = [E(nc.sbuf_tensor(f"wb{i}", [128, NCH, 512], BF16))
                for i in range(3)]
        qloc = E(nc.sbuf_tensor("qloc", [128, 2, TOK], BF16))
        kloc = E(nc.sbuf_tensor("kloc", [128, 2, TTOK], BF16))
        vloc = E(nc.sbuf_tensor("vloc", [128, G, 256], BF16))
        xn_tm = E(nc.sbuf_tensor("xn_tm", [128, 4, D], BF16))
        sqscr = E(nc.sbuf_tensor("sqscr", [128, D], F32))
        scr_a = E(nc.sbuf_tensor("scr_a", [128, 2, 768], F32))

        qstm = E(nc.sbuf_tensor("qstm", [128, 2, 768], BF16))
        sqdmp = qstm.ap().rearrange("p a b -> p (a b)")[:, 0:1024]
        vstage = E(nc.sbuf_tensor("vstage", [128, 768], BF16))
        x_roll = E(nc.sbuf_tensor("x_roll", [128, 2, D], F32))
        scr_b = x_roll.ap()[:, :, 0:768]
        ctxs2 = E(nc.sbuf_tensor("ctxs2", [128, 2, 6, 128], BF16))
        ctxf = E(nc.sbuf_tensor("ctxf", [128, 6, 64], F32))
        w_full = E(nc.sbuf_tensor("w_full", [128, G, NE], F32))
        stats = E(nc.sbuf_tensor("stats", [128, 16], F32))
        gstat = E(nc.sbuf_tensor("gstat", [128, 8], F32))
        masks = E(nc.sbuf_tensor("masks_sb", [128, 2, 256], F32))
        probs_f = E(nc.sbuf_tensor("probs_f", [128, 2, 256], F32))
        probs_b = E(nc.sbuf_tensor("probs_b", [128, 2, 256], BF16))
        pts = E(nc.sbuf_tensor("pts", [128, 2, 128], BF16))
        ident = E(nc.sbuf_tensor("identb", [128, 128], BF16))
        onesc = E(nc.sbuf_tensor("onescb", [128, 1], BF16))
        onesr = E(nc.sbuf_tensor("onesrb", [1, 128], BF16))
        onese = E(nc.sbuf_tensor("oneseb", [128, 1], BF16))
        gT_sb = E(nc.sbuf_tensor("gTsb", [128, NCH, NE], BF16))
        bo_sb = E(nc.sbuf_tensor("bosb", [1, D], BF16))
        pref = E(nc.sbuf_tensor("prefsb", [128, NE], F32))
        tsb = E(nc.sbuf_tensor("tsb", [128, 6, 65], F32))
        tbuf = E(nc.sbuf_tensor("tbuf", [128, 6, 65], F32))
        carry = E(nc.sbuf_tensor("carry", [128, 6, 65], F32))
        rsg = E(nc.sbuf_tensor("rsg", [128, 12], F32))
        zrow = E(nc.sbuf_tensor("zrow", [128, 8], F32))

        sd = Sem(ctx, nc, "sd")
        gw = Sem(ctx, nc, "gw")
        cc = Sem(ctx, nc, "cc")
        tS = Sem(ctx, nc, "tS")
        vS = Sem(ctx, nc, "vS")
        sS = Sem(ctx, nc, "sS")

        block = E(nc.Block())

        wload_plans = []
        wslot_rr = [0]
        wslot_tenant = {}

        def plan_wloads(loads, slots=(0, 1, 2)):
            res = {}
            plan = []
            for key, ap in loads:
                slot = slots[wslot_rr[0] % len(slots)]
                wslot_rr[0] += 1
                free_ref = wslot_tenant.get(slot)
                tgt = gw.reserve(16)
                myref = [None, None]
                wslot_tenant[slot] = myref
                plan.append((key, ap, slot, tgt, free_ref))
                res[key] = (slot, tgt, myref)
            wload_plans.append(plan)
            return res

        def emit_wload_body(plan):
            @block.gpsimd
            def _(g):
                for key, ap, slot, tgt, free_ref in plan:
                    if free_ref is not None:
                        assert free_ref[0] is not None, f"free target unset {key}"
                        g.wait_ge(free_ref[0].h, free_ref[1])
                    g.wait_ge(gw.h, tgt - 16)
                    g.dma_start(
                        out=wbuf[slot][:, :, :],
                        in_=ap.rearrange("(c p) d -> p c d", p=128)
                    ).then_inc(gw.h, 16)
                    gw.em = max(gw.em, tgt)

        xload_tgt = []

        @block.sync
        def _(sy):
            for i in range(5):
                g0, g1 = i * 4, min((i + 1) * 4, G)
                if g0 >= g1:
                    break
                t = sd.dma(sy, x_res[:, g0:g1, :],
                           xin[g0 * 128:g1 * 128, :].rearrange(
                               "(g p) d -> p g d", p=128))
                xload_tgt.extend([t] * (g1 - g0))
            sd.dma(sy, masks[:, :, :], masks_d.rearrange("m p k -> p m k"))
            sd.dma(sy, pref[:, :], pref_d)
        misc_ld = sd.n

        @block.gpsimd
        def _(g):
            gw.dma(g, ident[:, :], ident_d)
            gw.dma(g, onesc[:, :], ones_d)
            gw.dma(g, onesr[:, :], onesr_d)
            gw.dma(g, onese[:, :], onese_d)
            gw.dma(g, gT_sb[:, :, :], gT_d.rearrange("(c p) e -> p c e", p=128))
            gw.dma(g, bo_sb[:, :], bo_d)
        consts_tgt = gw.n

        eps_tgt = [None]

        @block.vector
        def _(v):
            v.memset(ctxs2.ap().bitcast(U8), 0)
            v.memset(vstage.ap().bitcast(U8), 0)
            vS.inc(v.memset(stats[:, 6:7], 1e-5))
            eps_tgt[0] = vS.n

        @block.scalar
        def _(s):
            from concourse.hw_specs import get_activation_tables
            need = {AF.Exp, AF.Ln, AF.Square, AF.Identity}
            for idx, funcs in enumerate(get_activation_tables(nc.m.arch).values()):
                if need.issubset(set(funcs)):
                    s.add_instruction(mybir.InstLoadActFuncSet(
                        name=nc.get_next_instruction_name(),
                        act_func_set_id=idx, ins=[], outs=[]))
                    break
            else:
                raise RuntimeError("no act table set with all needed funcs")

        wslot_rr[0] = 0
        ffn1_loads = plan_wloads(
            [((e, h), expT_d[e, :, 512 * h:512 * h + 512]) for e in range(NE)
             for h in range(2)])

        def emit_sln(g, src, extra_s_wait, buf):
            s = nc.scalar
            s.wait_ge(vS.h, eps_tgt[0])
            if extra_s_wait is not None:
                s.wait_ge(extra_s_wait[0].h, extra_s_wait[1])
            tv = sS.inc(s.activation(sqdmp[:, :], src, AF.Identity,
                                     accum_out=stats[:, 0:1]))
            s.wait_ge(sS.h, tv)
            tv = sS.inc(s.activation(stats[:, 1:2], stats[:, 0:1], AF.Identity,
                                     scale=-INV_D))
            s.wait_ge(sS.h, tv)
            tv = sS.inc(s.activation(sqscr[:, :], src, AF.Identity,
                                     bias=stats[:, 1:2]))
            s.wait_ge(sS.h, tv)
            tv = sS.inc(s.activation(sqdmp[:, :], sqscr[:, :], AF.Square,
                                     accum_out=stats[:, 2:3]))
            s.wait_ge(sS.h, tv)
            tv = sS.inc(s.activation(stats[:, 3:4], stats[:, 2:3], AF.Identity,
                                     scale=INV_D, bias=stats[:, 6:7]))
            s.wait_ge(sS.h, tv)
            tv = sS.inc(s.activation(stats[:, 4:5], stats[:, 3:4], AF.Ln))
            s.wait_ge(sS.h, tv)
            tv = sS.inc(s.activation(stats[:, 5:6], stats[:, 4:5], AF.Exp,
                                     scale=-0.5))
            s.wait_ge(sS.h, tv)
            tv = sS.inc(s.activation(xn_tm[:, buf, :], sqscr[:, :],
                                     AF.Identity, scale=stats[:, 5:6]))
            return tv

        def emit_ln(groups, src_fn, xn_view, tok_off, pT, extra_v_wait=None,
                    extra_s_wait=None, spill=False):
            v0, s0, t0 = vS.n, sS.n, tS.n
            n = len(groups)
            ln_s = {}

            @block.scalar
            def _(s):
                for i, g in enumerate(groups):
                    w = extra_s_wait(g) if extra_s_wait is not None else None
                    if i >= NLNBUF:
                        s.wait_ge(tS.h, t0 + (i - NLNBUF) + 1)
                    ln_s[i] = emit_sln(g, src_fn(g), w, i % NLNBUF)

            @block.tensor
            def _(t):
                t.wait_ge(gw.h, consts_tgt)
                for i, g in enumerate(groups):
                    t.wait_ge(sS.h, s0 + 8 * i + 8)
                    if i >= 2:
                        t.wait_ge(vS.h, v0 + (i - 2) + 1)
                    for c in range(NCH):
                        ins = t.transpose(pT[:, i % 2, c, :],
                                          xn_tm[:, i % NLNBUF,
                                                c * 128:(c + 1) * 128],
                                          ident[:, :])
                    tS.inc(ins)

            @block.vector
            def _(v):
                for i, g in enumerate(groups):
                    v.wait_ge(tS.h, t0 + i + 1)
                    vS.inc(v.tensor_copy(
                        xn_view[:, :, tok_off(g):tok_off(g) + 128],
                        pT[:, i % 2, :, :]))

            if spill:
                @block.sync
                def _(sy):
                    for i, g in enumerate(groups):
                        if g == 0:
                            continue
                        sy.wait_ge(sS.h, ln_s[i])
                        sd.dma(sy, x1sp[(g - 1) * 128:g * 128, :], src_fn(g))
            ln_t = {i: t0 + i + 1 for i in range(n)}
            ln_v = {i: v0 + i + 1 for i in range(n)}
            return ln_s, ln_t, ln_v

        def emit_lnffn(groups, src_fn, xn_view, tok_off, res_fn, loads,
                       pY, pL, pT, extra_s_wait=None, spill=False):
            n = len(groups)
            ln_s = {}
            ln_t = {}
            ln_v = {}
            f0_t = {}
            f0_v = {}
            fvg = {}
            sg = {}
            sl00, sl01 = loads[(0, 0)], loads[(0, 1)]

            def e0_mms(t, j):
                g = groups[j]
                if j == 0:
                    t.wait_ge(gw.h, max(sl00[1], sl01[1]))
                t.wait_ge(vS.h, ln_v[j])
                if j >= 2:
                    t.wait_ge(vS.h, f0_v[j - 2])
                par = j % 2
                to = tok_off(g)
                for c in range(NCH):
                    lhs = xn_view[:, c, to:to + 128]
                    t.matmul(pY[:, par, 0, :], lhs, wbuf[sl00[0]][:, c, :],
                             start=(c == 0), stop=(c == NCH - 1))
                    t.matmul(pY[:, par, 1, :], lhs, wbuf[sl01[0]][:, c, :],
                             start=(c == 0), stop=(c == NCH - 1))
                    ins = t.matmul(pL[:, par, 0:8], lhs, gT_sb[:, c, :],
                                   start=(c == 0), stop=(c == NCH - 1))
                tS.inc(ins)
                f0_t[j] = tS.n

            def gating_part1(v, j):
                par = j % 2
                pl = pL[:, par, 0:8]
                v.wait_ge(tS.h, f0_t[j])
                tv = vS.inc(v.reduce_max(stats[:, 8:9], pl, axis=AX.X))
                v.wait_ge(vS.h, tv)
                tv = vS.inc(v.tensor_tensor(
                    zrow[:, :], pl, stats[:, 8:9].broadcast_to([128, NE]),
                    op=ALU.is_ge))
                v.wait_ge(vS.h, tv)
                tv = vS.inc(v.scalar_tensor_tensor(
                    w_full[:, groups[j], :], zrow[:, :], -1e30, pl,
                    op0=ALU.mult, op1=ALU.add))
                v.wait_ge(vS.h, tv)
                tv = vS.inc(v.reduce_max(stats[:, 9:10], w_full[:, groups[j], :],
                                         axis=AX.X))
                v.wait_ge(vS.h, tv)
                vS.inc(v.tensor_scalar_mul(stats[:, 10:11], stats[:, 8:9],
                                           -1.0))
                fvg[j] = vS.n

            def gating_exps(s, j):
                s.wait_ge(vS.h, fvg[j])
                par = j % 2
                sS.inc(s.activation(gstat[:, 0:8], pL[:, par, 0:8], AF.Exp,
                                    bias=stats[:, 10:11]))
                sS.inc(s.activation(stats[:, 11:12], stats[:, 9:10], AF.Exp,
                                    bias=stats[:, 10:11]))
                sg[j] = sS.n

            def gating_part2_combine(v, j):
                par = j % 2
                pl = pL[:, par, 0:8]
                g = groups[j]
                v.wait_ge(sS.h, sg[j])
                tv = vS.inc(v.tensor_scalar(
                    stats[:, 12:13], stats[:, 11:12], 2.0, 2.0,
                    op0=ALU.mult, op1=ALU.add))
                v.wait_ge(vS.h, tv)
                tv = vS.inc(v.reciprocal(stats[:, 13:14], stats[:, 12:13]))
                v.wait_ge(vS.h, tv)
                tv = vS.inc(v.tensor_tensor(
                    zrow[:, :], pl, stats[:, 9:10].broadcast_to([128, NE]),
                    op=ALU.is_ge))
                v.wait_ge(vS.h, tv)
                tv = vS.inc(v.scalar_tensor_tensor(
                    w_full[:, g, :], gstat[:, 0:8], stats[:, 13:14],
                    zrow[:, :], op0=ALU.mult, op1=ALU.mult))
                v.wait_ge(vS.h, tv)
                wcol = w_full[:, g, 0:1]
                v.scalar_tensor_tensor(
                    res_fn(g, 0), pY[:, par, 0, :], wcol, res_fn(g, 0),
                    op0=ALU.mult, op1=ALU.add)
                vS.inc(v.scalar_tensor_tensor(
                    res_fn(g, 1), pY[:, par, 1, :], wcol, res_fn(g, 1),
                    op0=ALU.mult, op1=ALU.add))
                f0_v[j] = vS.n

            for i, g in enumerate(groups):
                @block.scalar
                def _(s, i=i, g=g):
                    if i >= NLNBUF:
                        s.wait_ge(tS.h, ln_t[i - NLNBUF])
                    w = extra_s_wait(g) if extra_s_wait is not None else None
                    ln_s[i] = emit_sln(g, src_fn(g), w, i % NLNBUF)
                    if i >= 2:
                        gating_exps(s, i - 2)

                @block.tensor
                def _(t, i=i):
                    if i == 0:
                        t.wait_ge(gw.h, consts_tgt)
                    t.wait_ge(sS.h, ln_s[i])
                    if i >= 2:
                        t.wait_ge(vS.h, ln_v[i - 2])
                    for c in range(NCH):
                        ins = t.transpose(pT[:, i % 2, c, :],
                                          xn_tm[:, i % NLNBUF,
                                                c * 128:(c + 1) * 128],
                                          ident[:, :])
                    tS.inc(ins)
                    ln_t[i] = tS.n
                    if i >= 1:
                        e0_mms(t, i - 1)

                @block.vector
                def _(v, i=i, g=g):
                    v.wait_ge(tS.h, ln_t[i])
                    vS.inc(v.tensor_copy(
                        xn_view[:, :, tok_off(g):tok_off(g) + 128],
                        pT[:, i % 2, :, :]))
                    ln_v[i] = vS.n
                    if i >= 2:
                        gating_part2_combine(v, i - 2)
                    if i >= 1:
                        gating_part1(v, i - 1)

            @block.tensor
            def _(t):
                e0_mms(t, n - 1)

            @block.scalar
            def _(s):
                gating_exps(s, n - 2)

            @block.vector
            def _(v):
                gating_part2_combine(v, n - 2)
                gating_part1(v, n - 1)

            @block.scalar
            def _(s):
                gating_exps(s, n - 1)

            @block.vector
            def _(v):
                gating_part2_combine(v, n - 1)

            t_base = tS.n
            v_base = vS.n

            @block.tensor
            def _(t):
                for e in range(1, NEXP):
                    for h in range(2):
                        sl = loads[(e, h)]
                        t.wait_ge(gw.h, sl[1])
                        for i, g in enumerate(groups):
                            m = (2 * (e - 1) + h) * n + i
                            if m >= 2:
                                t.wait_ge(vS.h, v_base + m - 1)
                            else:
                                t.wait_ge(vS.h, f0_v[n - 2 + m])
                            par = i % 2
                            to = tok_off(g)
                            for c in range(NCH):
                                ins = t.matmul(pY[:, par, h, :],
                                               xn_view[:, c, to:to + 128],
                                               wbuf[sl[0]][:, c, :],
                                               start=(c == 0),
                                               stop=(c == NCH - 1))
                            tS.inc(ins)

            @block.vector
            def _(v):
                for e in range(1, NEXP):
                    for h in range(2):
                        for i, g in enumerate(groups):
                            m = (2 * (e - 1) + h) * n + i
                            v.wait_ge(tS.h, t_base + m + 1)
                            wcol = w_full[:, g, e:e + 1]
                            vS.inc(v.scalar_tensor_tensor(
                                res_fn(g, h), pY[:, i % 2, h, :], wcol,
                                res_fn(g, h), op0=ALU.mult, op1=ALU.add))

            loads[(0, 0)][2][0] = tS
            loads[(0, 0)][2][1] = f0_t[n - 1]
            loads[(0, 1)][2][0] = tS
            loads[(0, 1)][2][1] = f0_t[n - 1]
            for e in range(1, NEXP):
                for h in range(2):
                    ref = loads[(e, h)][2]
                    ref[0] = tS
                    ref[1] = t_base + (2 * (e - 1) + h) * n + n
            xdone = {i: v_base + 13 * n + i + 1 for i in range(n)}
            if spill:
                @block.sync
                def _(sy):
                    for i, g in enumerate(groups):
                        if g == 0:
                            continue
                        sy.wait_ge(vS.h, xdone[i])
                        sd.dma(sy, x1sp[(g - 1) * 128:g * 128, :], src_fn(g))
            return ln_v, xdone

        with nc.psum_tensor("pT1", [128, 2, NCH, 128], BF16) as pT1, \
             nc.psum_tensor("pY1", [128, 2, 2, 512], F32) as pY1, \
             nc.psum_tensor("pL1", [128, 2, 512], F32) as pL1:

            groups17 = list(range(G))
            _, xdone1 = emit_lnffn(
                groups17, lambda g: x_res[:, g, :], xn17, lambda g: g * 128,
                lambda g, h: x_res[:, g, 512 * h:512 * h + 512],
                ffn1_loads, pY1, pL1, pT1,
                extra_s_wait=lambda g: (sd, xload_tgt[g]))
            emit_wload_body(wload_plans[0])

        def x1_done(g):
            return xdone1[groups17.index(g)]

        if STAGE == 1:
            @block.gpsimd
            def _(g):
                g.wait_ge(vS.h, x1_done(16))
                gw.dma(g, dbg_d[:, 0:136], w_full[:, :, :].rearrange("p a b -> p (a b)"))

            @block.sync
            def _(sy):
                for g in range(1, G):
                    sy.wait_ge(vS.h, x1_done(g))
                    sd.dma(sy, out_d[(g - 1) * 128:g * 128, :], x_res[:, g, :])
                sy.wait_ge(sd.h, sd.n)
                sy.wait_ge(gw.h, gw.em)
            return nc

        def finish_early(tag):
            if STOP != tag:
                return False
            dbgsel = os.environ.get("MK_DBG", "")
            if dbgsel:
                @block.vector
                def _(v):
                    v.wait_ge(tS.h, tS.n)
                    v.wait_ge(sS.h, sS.n)
                    if dbgsel == "kv":
                        v.tensor_copy(scr_a[:, 0, :], kg[:, 0, :])
                        vS.inc(v.tensor_copy(scr_b[:, 0, 0:256],
                                             vg[:, 0, 0:256]))
                    elif dbgsel == "qs":
                        vS.inc(v.tensor_copy(
                            scr_a[:, 0, :].rearrange("p (j t) -> p j t", t=128),
                            qs_fm[:, :, 0:128]))
                    elif dbgsel == "afm":
                        vS.inc(v.tensor_copy(
                            scr_a[:, 0, :].rearrange("p (c t) -> p c t", t=128),
                            afm[:, 2:8, 64:192]))
                dbg_v = vS.n

                @block.gpsimd
                def _(g):
                    g.wait_ge(vS.h, dbg_v)
                    gw.dma(g, dbg_d[:, 0:768], scr_a[:, 0, :])
                    if dbgsel == "kv":
                        gw.dma(g, dbg_d[:, 768:1024], scr_b[:, 0, 0:256])

            @block.sync
            def _(sy):
                sy.wait_ge(tS.h, tS.n)
                sy.wait_ge(vS.h, vS.n)
                sy.wait_ge(sS.h, sS.n)
                sy.wait_ge(gw.h, gw.em)
                for g in range(1, G):
                    sd.dma(sy, out_d[(g - 1) * 128:g * 128, :], x_res[:, g, :])
                sy.wait_ge(sd.h, sd.n)
            return True

        _scopes = []

        def _phase_scope(tag):
            if _scopes:
                nm, sid = _scopes.pop()
                nc.leave_named_scope(nm, sid, False)
            sid, _ = nc.enter_named_scope(tag, False)
            _scopes.append((tag, sid))

        def barrier_all():
            tT, tV, tSS = tS.n, vS.n, sS.n

            @block.tensor
            def _(t):
                t.wait_ge(vS.h, tV)
                t.wait_ge(sS.h, tSS)

            @block.vector
            def _(v):
                v.wait_ge(tS.h, tT)
                v.wait_ge(sS.h, tSS)

            @block.scalar
            def _(s):
                s.wait_ge(tS.h, tT)
                s.wait_ge(vS.h, tV)

        wslot_rr[0] = 0
        proj_loads = plan_wloads([
            (("q", 0), wq_d[:, 0:512]), (("q", 1), wq_d[:, 512:1024]),
            (("k", 0), wk_d[:, 0:512]), (("k", 1), wk_d[:, 512:1024]),
            (("v", 0), wv_d[:, 0:512]), (("v", 1), wv_d[:, 512:1024]),
            (("o", 0), wo_d[:, 0:512]), (("o", 1), wo_d[:, 512:1024]),
        ])
        proj_plan = wload_plans[-1]
        emit_wload_body(proj_plan[0:3])

        _phase_scope("ph_p3")
        barrier_all()
        sl0, sl1 = proj_loads[("q", 0)], proj_loads[("q", 1)]
        qloc_done = {}
        with nc.psum_tensor("pT3", [128, 2, NCH, 128], BF16) as pT3, \
             nc.psum_tensor("pql", [128, 2, 2, 512], F32) as pql:
            ln_s3 = {}
            ln_t3 = {}
            ln_v3 = {}
            ql_t = {}

            def qloc_mms(t, ti):
                if ti == 0:
                    t.wait_ge(gw.h, sl0[1])
                t.wait_ge(vS.h, ln_v3[min(4 * ti + 5, G - 1)])
                if ti >= 2:
                    t.wait_ge(vS.h, qloc_done[ti - 2])
                t0c = 128 + 512 * ti
                par = ti % 2
                for dch in range(2):
                    for c in range(NCH):
                        ins = t.matmul(
                            pql[:, par, dch, :],
                            wbuf[sl0[0]][:, c, dch * 128:(dch + 1) * 128],
                            xn17[:, c, t0c:t0c + 512],
                            start=(c == 0), stop=(c == NCH - 1))
                    tS.inc(ins)
                ql_t[ti] = tS.n

            def qloc_evac(v, ti):
                v.wait_ge(tS.h, ql_t[ti])
                t0c = 512 * ti
                vS.inc(v.tensor_copy(qloc[:, :, t0c:t0c + 512],
                                     pql[:, ti % 2, 0:2, :]))
                qloc_done[ti] = vS.n

            for i in range(G):
                @block.scalar
                def _(s, i=i):
                    if i >= NLNBUF:
                        s.wait_ge(tS.h, ln_t3[i - NLNBUF])
                    ln_s3[i] = emit_sln(i, x_res[:, i, :],
                                        (vS, x1_done(i)), i % NLNBUF)

                @block.tensor
                def _(t, i=i):
                    t.wait_ge(sS.h, ln_s3[i])
                    if i >= 2:
                        t.wait_ge(vS.h, ln_v3[i - 2])
                    for c in range(NCH):
                        ins = t.transpose(pT3[:, i % 2, c, :],
                                          xn_tm[:, i % NLNBUF,
                                                c * 128:(c + 1) * 128],
                                          ident[:, :])
                    tS.inc(ins)
                    ln_t3[i] = tS.n
                    if i in (6, 10, 14):
                        qloc_mms(t, (i - 6) // 4)

                @block.vector
                def _(v, i=i):
                    v.wait_ge(tS.h, ln_t3[i])
                    vS.inc(v.tensor_copy(
                        xn17[:, :, i * 128:i * 128 + 128],
                        pT3[:, i % 2, :, :]))
                    ln_v3[i] = vS.n
                    if i in (7, 11, 15):
                        qloc_evac(v, (i - 7) // 4)

            @block.tensor
            def _(t):
                qloc_mms(t, 3)

            @block.vector
            def _(v):
                qloc_evac(v, 3)

            @block.sync
            def _(sy):
                for i in range(1, G):
                    sy.wait_ge(sS.h, ln_s3[i])
                    sd.dma(sy, x1sp[(i - 1) * 128:i * 128, :], x_res[:, i, :])

        spill_tgt = sd.n

        @block.vector
        def _(v):
            v.wait_ge(sd.h, spill_tgt)

        @block.scalar
        def _(s):
            s.wait_ge(sd.h, spill_tgt)

        @block.tensor
        def _(t):
            t.wait_ge(sd.h, spill_tgt)



        GOWN = list(range(1, G))

        def emit_fm_proj(wkey, dst, dst_ncols, tok_lo, tok_n, loads):
            sl0 = loads[(wkey, 0)]
            tiles = []
            t_ = tok_lo
            while t_ < tok_lo + tok_n:
                w = min(512, tok_lo + tok_n - t_)
                tiles.append((t_, w))
                t_ += w
            ev_tgt = {}
            mm_tgt = {}
            for ti, (t0c, w) in enumerate(tiles):
                @block.tensor
                def _(t, ti=ti, t0c=t0c, w=w):
                    if ti == 0:
                        t.wait_ge(gw.h, sl0[1])
                    if ti >= 2:
                        t.wait_ge(vS.h, ev_tgt[ti - 2])
                    par = ti % 2
                    for dch in range(dst_ncols // 128):
                        for c in range(NCH):
                            ins = t.matmul(
                                pql[:, par, dch, 0:w],
                                wbuf[sl0[0]][:, c, dch * 128:(dch + 1) * 128],
                                xn17[:, c, t0c:t0c + w],
                                start=(c == 0), stop=(c == NCH - 1))
                        tS.inc(ins)
                    mm_tgt[ti] = tS.n

                @block.vector
                def _(v, ti=ti, t0c=t0c, w=w):
                    v.wait_ge(tS.h, mm_tgt[ti])
                    par = ti % 2
                    vS.inc(v.tensor_copy(
                        dst[:, :, t0c - tok_lo:t0c - tok_lo + w],
                        pql[:, par, 0:dst_ncols // 128, 0:w]))
                    ev_tgt[ti] = vS.n
            return ev_tgt

        if finish_early('p4a'):
            return nc

        _phase_scope("ph_p4b")
        barrier_all()
        qg_v2 = {}
        with nc.psum_tensor("pqg", [128, 2, 2, 512], F32) as pqg, \
             nc.psum_tensor("ptrq", [128, 2, 6, 128], BF16) as ptrq:
            qg_t = {}
            qg_s = {}
            qg_v1 = {}
            qg_t2 = {}
            sl0, sl1 = proj_loads[("q", 0)], proj_loads[("q", 1)]
            for i, g in enumerate(GOWN):
                par = i % 2

                @block.tensor
                def _(t, i=i, g=g, par=par):
                    if i == 0:
                        t.wait_ge(gw.h, max(sl0[1], sl1[1]))
                    if i >= 2:
                        t.wait_ge(sS.h, qg_s[i - 2])
                        t.wait_ge(vS.h, qg_v1[i - 2])
                    to = g * 128
                    for c in range(NCH):
                        t.matmul(pqg[:, par, 0, 0:256],
                                 xn17[:, c, to:to + 128],
                                 wbuf[sl0[0]][:, c, 256:512],
                                 start=(c == 0), stop=(c == NCH - 1))
                        ins = t.matmul(pqg[:, par, 1, :],
                                       xn17[:, c, to:to + 128],
                                       wbuf[sl1[0]][:, c, :],
                                       start=(c == 0), stop=(c == NCH - 1))
                    tS.inc(ins)
                    qg_t[i] = tS.n

                @block.scalar
                def _(s, i=i, par=par):
                    s.wait_ge(tS.h, qg_t[i])
                    sS.inc(s.activation(scr_a[:, par, 0:256],
                                        pqg[:, par, 0, 0:256], AF.Exp))
                    sS.inc(s.activation(scr_a[:, par, 256:768],
                                        pqg[:, par, 1, :], AF.Exp))
                    qg_s[i] = sS.n

                @block.vector
                def _(v, i=i, par=par):
                    v.wait_ge(sS.h, qg_s[i])
                    sa = scr_a[:, par, :].rearrange("p (h e) -> p h e", e=64)
                    tv = vS.inc(v.reduce_sum(rsg[:, :], sa, axis=AX.X))
                    v.wait_ge(vS.h, tv)
                    tv = vS.inc(v.reciprocal(rsg[:, :], rsg[:, :]))
                    v.wait_ge(vS.h, tv)
                    vS.inc(v.tensor_tensor(
                        qstm[:, par, :].rearrange("p (h e) -> p h e", e=64),
                        sa,
                        rsg[:, :].unsqueeze(2).broadcast_to([128, 12, 64]),
                        op=ALU.mult))
                    qg_v1[i] = vS.n

                @block.tensor
                def _(t, i=i, par=par):
                    t.wait_ge(vS.h, qg_v1[i])
                    if i >= 2:
                        t.wait_ge(vS.h, qg_v2[i - 2])
                    for j in range(6):
                        ins = t.transpose(ptrq[:, par, j, :],
                                          qstm[:, par, j * 128:(j + 1) * 128],
                                          ident[:, :])
                    tS.inc(ins)
                    qg_t2[i] = tS.n

                @block.vector
                def _(v, i=i, g=g, par=par):
                    v.wait_ge(tS.h, qg_t2[i])
                    vS.inc(v.tensor_copy(
                        qs_fm[:, :, (g - 1) * 128:g * 128],
                        ptrq[:, par, :, :]))
                    qg_v2[i] = vS.n

        if finish_early('p4b'):
            return nc

        for h in range(2):
            proj_loads[("q", h)][2][0] = tS
            proj_loads[("q", h)][2][1] = tS.n
        emit_wload_body(proj_plan[3:5])

        _phase_scope("ph_p4c")
        barrier_all()
        with nc.psum_tensor("pkl", [128, 2, 2, 512], F32) as pql:
            kloc_done = emit_fm_proj("k", kloc, 256, 0, TTOK, proj_loads)

        if finish_early('p4c'):
            return nc

        _phase_scope("ph_p4d")
        barrier_all()
        with nc.psum_tensor("pkg", [128, 2, 2, 512], F32) as pkg:
            kg_t = {}
            kg_s = {}
            kg_vm = {}
            kg_v = {}
            sl0, sl1 = proj_loads[("k", 0)], proj_loads[("k", 1)]
            for i, g in enumerate(GOWN):
                par = i % 2

                @block.tensor
                def _(t, i=i, g=g, par=par):
                    if i == 0:
                        t.wait_ge(gw.h, max(sl0[1], sl1[1]))
                    if i >= 2:
                        t.wait_ge(sS.h, kg_s[i - 2])
                        t.wait_ge(vS.h, kg_vm[i - 2])
                    to = g * 128
                    for c in range(NCH):
                        t.matmul(pkg[:, par, 0, 0:256],
                                 xn17[:, c, to:to + 128],
                                 wbuf[sl0[0]][:, c, 256:512],
                                 start=(c == 0), stop=(c == NCH - 1))
                        ins = t.matmul(pkg[:, par, 1, :],
                                       xn17[:, c, to:to + 128],
                                       wbuf[sl1[0]][:, c, :],
                                       start=(c == 0), stop=(c == NCH - 1))
                    tS.inc(ins)
                    kg_t[i] = tS.n

                @block.scalar
                def _(s, i=i, par=par):
                    s.wait_ge(tS.h, kg_t[i])
                    if i >= 1:
                        s.wait_ge(vS.h, kg_v[i - 1])
                    s.activation(scr_b[:, par, 0:256], pkg[:, par, 0, 0:256],
                                 AF.Relu)
                    sS.inc(s.activation(scr_b[:, par, 256:768],
                                        pkg[:, par, 1, :], AF.Relu))
                    kg_s[i] = sS.n

                @block.vector
                def _(v, i=i, par=par):
                    v.wait_ge(tS.h, kg_t[i])
                    v.wait_ge(sS.h, kg_s[i])
                    v.tensor_scalar_min(scr_a[:, par, 0:256],
                                        pkg[:, par, 0, 0:256], 0.0)
                    vS.inc(v.tensor_scalar_min(scr_a[:, par, 256:768],
                                               pkg[:, par, 1, :], 0.0))
                    kg_vm[i] = vS.n

                @block.scalar
                def _(s, i=i, par=par):
                    s.wait_ge(vS.h, kg_vm[i])
                    sS.inc(s.activation(sqscr[:, 0:768], scr_a[:, par, :],
                                        AF.Exp))
                    kg_s2 = sS.n
                    kg_s[i] = max(kg_s[i], kg_s2)

                @block.vector
                def _(v, i=i, g=g, par=par):
                    v.wait_ge(sS.h, kg_s[i])
                    vS.inc(v.tensor_tensor(kg[:, g - 1, :], sqscr[:, 0:768],
                                           scr_b[:, par, :], op=ALU.add))
                    kg_v[i] = vS.n

        if finish_early('p4d'):
            return nc

        for h in range(2):
            proj_loads[("k", h)][2][0] = tS
            proj_loads[("k", h)][2][1] = tS.n
        emit_wload_body(proj_plan[5:7])

        _phase_scope("ph_p4e")
        barrier_all()
        with nc.psum_tensor("pvl", [128, 512], F32) as pvl, \
             nc.psum_tensor("pvg", [128, 2, 512], F32) as pvg, \
             nc.psum_tensor("pSt", [128, 8, 128], F32) as pSt:
            vl_t = {}
            vl_v = {}
            vg_t = {}
            vg_s = {}
            st_t = {}
            st_v = {}
            slv0, slv1 = proj_loads[("v", 0)], proj_loads[("v", 1)]
            for g in range(G):
                i = g - 1

                @block.tensor
                def _(t, g=g, i=i):
                    if g == 0:
                        t.wait_ge(gw.h, max(slv0[1], slv1[1]))
                    if g >= 1:
                        t.wait_ge(vS.h, vl_v[g - 1])
                    to = g * 128
                    for c in range(NCH):
                        ins = t.matmul(pvl[:, 0:256],
                                       xn17[:, c, to:to + 128],
                                       wbuf[slv0[0]][:, c, 0:256],
                                       start=(c == 0), stop=(c == NCH - 1))
                    tS.inc(ins)
                    vl_t[g] = tS.n

                @block.vector
                def _(v, g=g):
                    v.wait_ge(tS.h, vl_t[g])
                    vS.inc(v.tensor_copy(vloc[:, g, :], pvl[:, 0:256]))
                    vl_v[g] = vS.n

                if g == 0:
                    continue
                par = i % 2

                @block.tensor
                def _(t, g=g, i=i, par=par):
                    if i >= 1:
                        t.wait_ge(sS.h, vg_s[i - 1])
                    to = g * 128
                    for c in range(NCH):
                        t.matmul(pvg[:, 0, 0:256],
                                 xn17[:, c, to:to + 128],
                                 wbuf[slv0[0]][:, c, 256:512],
                                 start=(c == 0), stop=(c == NCH - 1))
                        ins = t.matmul(pvg[:, 1, :],
                                       xn17[:, c, to:to + 128],
                                       wbuf[slv1[0]][:, c, :],
                                       start=(c == 0), stop=(c == NCH - 1))
                    tS.inc(ins)
                    vg_t[i] = tS.n

                @block.scalar
                def _(s, i=i, g=g, par=par):
                    s.wait_ge(tS.h, vg_t[i])
                    s.copy(vg[:, g - 1, 0:256], pvg[:, 0, 0:256])
                    sS.inc(s.copy(vg[:, g - 1, 256:768], pvg[:, 1, :]))
                    vg_s[i] = sS.n

                @block.tensor
                def _(t, i=i, g=g):
                    t.wait_ge(sS.h, vg_s[i])
                    t.wait_ge(vS.h, kg_v[i])
                    if i >= 1:
                        t.wait_ge(vS.h, st_v[i - 1])
                    for j in range(6):
                        t.matmul(pSt[:, j, :],
                                 kg[:, i, j * 128:(j + 1) * 128],
                                 vg[:, i, j * 128:(j + 1) * 128],
                                 start=True, stop=True)
                        ins = t.matmul(pSt[:, 6, j:j + 1],
                                       kg[:, i, j * 128:(j + 1) * 128],
                                       onesc[:, 0:1],
                                       start=True, stop=True)
                    tS.inc(ins)
                    st_t[i] = tS.n

                @block.vector
                def _(v, i=i):
                    v.wait_ge(tS.h, st_t[i])
                    if i == 0:
                        v.tensor_copy(tsb[0:64, :, 0:64], pSt[0:64, 0:6, 0:64])
                        v.tensor_copy(tsb[64:128, :, 0:64],
                                      pSt[64:128, 0:6, 64:128])
                        vS.inc(v.tensor_copy(tsb[:, :, 64:65].squeeze(2),
                                             pSt[:, 6, 0:6]))
                    else:
                        v.scalar_tensor_tensor(
                            tsb[0:64, :, 0:64], pSt[0:64, 0:6, 0:64], 1.0,
                            tsb[0:64, :, 0:64], op0=ALU.mult, op1=ALU.add)
                        v.scalar_tensor_tensor(
                            tsb[64:128, :, 0:64], pSt[64:128, 0:6, 64:128], 1.0,
                            tsb[64:128, :, 0:64], op0=ALU.mult, op1=ALU.add)
                        vS.inc(v.scalar_tensor_tensor(
                            tsb[:, :, 64:65].squeeze(2), pSt[:, 6, 0:6], 1.0,
                            tsb[:, :, 64:65].squeeze(2),
                            op0=ALU.mult, op1=ALU.add))
                    st_v[i] = vS.n

            pub_s = [sS.n]
            pub_v = [st_v[15]]

        if finish_early('p4e'):
            return nc

        for h in range(2):
            proj_loads[("v", h)][2][0] = tS
            proj_loads[("v", h)][2][1] = tS.n
        emit_wload_body(proj_plan[7:8])

        _phase_scope("ph_p5")
        barrier_all()
        with nc.psum_tensor("psc", [128, 2, 512], F32) as psc, \
             nc.psum_tensor("ppt", [128, 2, 8, 128], BF16) as ppt, \
             nc.psum_tensor("pout", [128, 2, 512], F32) as pout:
            sc_t = {}
            v_mask = {}
            s_exp = {}
            v_pb = {}
            t_pt = {}
            s_pt = {}
            t_out = {}
            o_done = {}
            for step in range(66):
                if step < 64:
                    i = step
                    b, h, par = i // 4, i % 4, i % 2

                    @block.tensor
                    def _(t, i=i, b=b, h=h, par=par):
                        if i == 0:
                            t.wait_ge(sd.h, misc_ld)
                            t.wait_ge(vS.h, qloc_done[3])
                            t.wait_ge(vS.h, kloc_done[4])
                        if i >= 2:
                            t.wait_ge(vS.h, v_mask[i - 2])
                        r0 = (h % 2) * 64
                        ins = t.matmul(
                            psc[:, par, 0:256],
                            qloc[r0:r0 + 64, h // 2, b * 128:(b + 1) * 128],
                            kloc[r0:r0 + 64, h // 2, b * 128:b * 128 + 256],
                            start=True, stop=True)
                        tS.inc(ins)
                        sc_t[i] = tS.n

                if 0 <= step - 1 < 64:
                    i = step - 1
                    b, par = i // 4, i % 2
                    mrow = 0 if b == 0 else 1

                    @block.vector
                    def _(v, i=i, par=par, mrow=mrow):
                        v.wait_ge(tS.h, sc_t[i])
                        if i >= 2:
                            v.wait_ge(sS.h, s_exp[i - 2])
                        tv = vS.inc(v.tensor_tensor(
                            probs_f[:, par, :], psc[:, par, 0:256],
                            masks[:, mrow, :], op=ALU.add))
                        v.wait_ge(vS.h, tv)
                        vS.inc(v.reduce_max(stats[:, 8 + par:9 + par],
                                            probs_f[:, par, :], axis=AX.X))
                        v_mask[i] = vS.n

                    @block.scalar
                    def _(s, i=i, par=par):
                        s.wait_ge(vS.h, v_mask[i])
                        if i >= 2:
                            s.wait_ge(vS.h, v_pb[i - 2])
                        tv = sS.inc(s.activation(
                            stats[:, 10 + par:11 + par],
                            stats[:, 8 + par:9 + par], AF.Identity,
                            scale=-0.125))
                        s.wait_ge(sS.h, tv)
                        sS.inc(s.activation(
                            sqscr[:, par * 256:(par + 1) * 256],
                            probs_f[:, par, :], AF.Exp,
                            bias=stats[:, 10 + par:11 + par], scale=0.125,
                            accum_out=stats[:, 12 + par:13 + par]))
                        s_exp[i] = sS.n

                if 0 <= step - 2 < 64:
                    i = step - 2
                    b, h, par = i // 4, i % 4, i % 2

                    @block.vector
                    def _(v, i=i, par=par):
                        v.wait_ge(sS.h, s_exp[i])
                        if i >= 2:
                            v.wait_ge(tS.h, t_pt[i - 2])
                        tv = vS.inc(v.reciprocal(stats[:, 14 + par:15 + par],
                                                 stats[:, 12 + par:13 + par]))
                        v.wait_ge(vS.h, tv)
                        vS.inc(v.tensor_scalar_mul(
                            probs_b[:, par, :],
                            sqscr[:, par * 256:(par + 1) * 256],
                            stats[:, 14 + par:15 + par]))
                        v_pb[i] = vS.n

                    @block.tensor
                    def _(t, i=i, par=par):
                        t.wait_ge(vS.h, v_pb[i])
                        if i >= 2:
                            t.wait_ge(sS.h, s_pt[i - 2])
                        t.transpose(ppt[:, par, 0, :], probs_b[:, par, 0:128],
                                    ident[:, :])
                        ins = t.transpose(ppt[:, par, 1, :],
                                          probs_b[:, par, 128:256],
                                          ident[:, :])
                        tS.inc(ins)
                        t_pt[i] = tS.n

                    @block.scalar
                    def _(s, i=i, par=par):
                        s.wait_ge(tS.h, t_pt[i])
                        sS.inc(s.copy(pts[:, :, :], ppt[:, par, 0:2, :]))
                        s_pt[i] = sS.n

                    @block.tensor
                    def _(t, i=i, b=b, h=h, par=par):
                        t.wait_ge(sS.h, s_pt[i])
                        t.wait_ge(vS.h, vl_v[b + 1])
                        if i >= 2:
                            t.wait_ge(sS.h, o_done[i - 2])
                        hp = h // 2
                        t.matmul(pout[:, par, 0:128],
                                 vloc[:, b, hp * 128:(hp + 1) * 128],
                                 pts[:, 0, :], start=True, stop=False)
                        ins = t.matmul(pout[:, par, 0:128],
                                       vloc[:, b + 1, hp * 128:(hp + 1) * 128],
                                       pts[:, 1, :], start=False, stop=True)
                        tS.inc(ins)
                        t_out[i] = tS.n

                    @block.scalar
                    def _(s, i=i, b=b, h=h, par=par):
                        s.wait_ge(tS.h, t_out[i])
                        r0 = (h % 2) * 64
                        sS.inc(s.copy(
                            afm[r0:r0 + 64, h // 2, b * 128:(b + 1) * 128],
                            pout[r0:r0 + 64, par, 0:128]))
                        o_done[i] = sS.n

        tbuf_tgt = []
        carry_v = []
        cv0 = [None]

        @block.gpsimd
        def _(g):
            g.wait_ge(sS.h, pub_s[0])
            g.wait_ge(vS.h, pub_v[0])
            gw.dma(g, ag_in.ap(), tsb[:, :, :].rearrange("p j f -> p (j f)"))
            g.collective_compute(
                "AllGather", ALU.bypass,
                replica_groups=[list(range(NCORE))],
                ins=[ag_in.ap().opt()],
                outs=[ag_out.ap().opt()],
            ).then_inc(cc.h, 1)
            cc.n += 1
            g.wait_ge(cc.h, cc.n)
            vbase_c = vS.n
            for j in range(NCORE):
                if j >= 1:
                    g.wait_ge(vS.h, vbase_c + j)
                tbuf_tgt.append(gw.dma(g, tbuf[:, :, :].rearrange('p j f -> p (j f)'), ag_out.ap()[j * 128:(j + 1) * 128, :]))

        @block.vector
        def _(v):
            for j in range(NCORE):
                v.wait_ge(gw.h, tbuf_tgt[j])
                if j == 0:
                    tv = vS.inc(v.tensor_scalar_mul(
                        carry[:, :, :], tbuf[:, :, :], pref[:, 0:1]))
                else:
                    tv = vS.inc(v.scalar_tensor_tensor(
                        carry[:, :, :], tbuf[:, :, :],
                        pref[:, j:j + 1], carry[:, :, :],
                        op0=ALU.mult, op1=ALU.add))
                v.wait_ge(vS.h, tv)
                carry_v.append(tv)

        if finish_early('exch'):
            return nc

        if finish_early('p5'):
            return nc

        _phase_scope("ph_p6")
        barrier_all()
        v_ctx = {}
        t_outg = {}
        t_updA = {}
        t_updC = {}
        s_outg = {}
        s_ctx = {}
        asb = tsb
        with nc.psum_tensor("pC", [128, 8, 128], F32) as pC, \
             nc.psum_tensor("pG", [128, 8, 128], F32) as pG, \
             nc.psum_tensor("pog", [128, 2, 8, 64], F32) as pog:
            av0 = [None]

            @block.vector
            def _(v):
                v.wait_ge(vS.h, carry_v[7])
                tv = vS.inc(v.tensor_copy(asb[:, :, :], carry[:, :, :]))
                v.wait_ge(vS.h, tv)
                av0[0] = vS.n
            for u in range(P6N):
                par = u % 2
                i = u // 2
                odd = u % 2 == 1

                @block.vector
                def _(v, u=u, par=par, i=i, odd=odd):
                    if u == 0:
                        v.wait_ge(vS.h, av0[0])
                    if odd:
                        v.wait_ge(tS.h, t_updC[i])
                    if u >= 2:
                        v.wait_ge(tS.h, t_outg[u - 2])
                    if not odd and i >= 1:
                        v.wait_ge(tS.h, t_updC[i - 1])
                    if not odd:
                        vS.inc(v.tensor_copy(vstage[0:64, :], vg[0:64, i, :]))
                    if odd:
                        v.scalar_tensor_tensor(
                            ctxf[0:64, :, :], pC[0:64, 0:6, 0:64], 1.0,
                            asb[0:64, :, 0:64], op0=ALU.mult, op1=ALU.add)
                        v.scalar_tensor_tensor(
                            ctxf[64:128, :, :], pC[64:128, 0:6, 64:128], 1.0,
                            asb[64:128, :, 0:64], op0=ALU.mult, op1=ALU.add)
                        tv = vS.inc(v.scalar_tensor_tensor(
                            zrow[:, 0:6], pC[:, 6, 0:6], 1.0,
                            asb[:, :, 64:65].squeeze(2),
                            op0=ALU.mult, op1=ALU.add))
                        v.wait_ge(vS.h, tv)
                    else:
                        v.tensor_copy(ctxf[:, :, :], asb[:, :, 0:64])
                        tv = vS.inc(v.tensor_copy(zrow[:, 0:6],
                                                  asb[:, :, 64:65].squeeze(2)))
                        v.wait_ge(vS.h, tv)
                    tv = vS.inc(v.tensor_scalar_add(zrow[:, 0:6], zrow[:, 0:6],
                                                    1e-6))
                    v.wait_ge(vS.h, tv)
                    tv = vS.inc(v.reciprocal(gstat[:, 0:6], zrow[:, 0:6]))
                    v.wait_ge(vS.h, tv)
                    v.tensor_tensor(
                        ctxs2[0:64, par, :, 0:64], ctxf[0:64, :, :],
                        gstat[0:64, 0:6].unsqueeze(2).broadcast_to([64, 6, 64]),
                        op=ALU.mult)
                    tv = vS.inc(v.tensor_tensor(
                        ctxs2[64:128, par, :, 64:128], ctxf[64:128, :, :],
                        gstat[64:128, 0:6].unsqueeze(2).broadcast_to([64, 6, 64]),
                        op=ALU.mult))
                    v_ctx[u] = vS.n
                    if odd:
                        v.wait_ge(vS.h, tv)
                        v.wait_ge(tS.h, t_updA[i])
                        v.scalar_tensor_tensor(
                            asb[0:64, :, 0:64], pG[0:64, 0:6, 0:64], 1.0,
                            asb[0:64, :, 0:64], op0=ALU.mult, op1=ALU.add)
                        v.scalar_tensor_tensor(
                            asb[64:128, :, 0:64], pG[64:128, 0:6, 64:128], 1.0,
                            asb[64:128, :, 0:64], op0=ALU.mult, op1=ALU.add)
                        vS.inc(v.scalar_tensor_tensor(
                            asb[:, :, 64:65].squeeze(2), pG[:, 6, 0:6], 1.0,
                            asb[:, :, 64:65].squeeze(2),
                            op0=ALU.mult, op1=ALU.add))
                        v_ctx[u] = vS.n

                @block.tensor
                def _(t, u=u, par=par, i=i, odd=odd):
                    t.wait_ge(vS.h, v_ctx[u])
                    if u >= 2:
                        t.wait_ge(sS.h, s_outg[u - 2])
                    cs = u * 64
                    for j in range(6):
                        ins = t.matmul(pog[:, par, j, :],
                                       ctxs2[:, par, j, :],
                                       qs_fm[:, j, cs:cs + 64],
                                       start=True, stop=True)
                    tS.inc(ins)
                    t_outg[u] = tS.n
                    if not odd:
                        for j in range(6):
                            t.matmul(pC[:, j, :],
                                     kg[:, i, j * 128:(j + 1) * 128],
                                     vstage[:, j * 128:(j + 1) * 128],
                                     start=True, stop=True)
                            ins = t.matmul(pC[:, 6, j:j + 1],
                                           kg[:, i, j * 128:(j + 1) * 128],
                                           onese[:, 0:1],
                                           start=True, stop=True)
                        tS.inc(ins)
                        t_updC[i] = tS.n
                        if i >= 1:
                            t.wait_ge(vS.h, v_ctx[u - 1])
                        for j in range(6):
                            t.matmul(pG[:, j, :],
                                     kg[:, i, j * 128:(j + 1) * 128],
                                     vg[:, i, j * 128:(j + 1) * 128],
                                     start=True, stop=True)
                            ins = t.matmul(pG[:, 6, j:j + 1],
                                           kg[:, i, j * 128:(j + 1) * 128],
                                           onesc[:, 0:1],
                                           start=True, stop=True)
                        tS.inc(ins)
                        t_updA[i] = tS.n

                @block.scalar
                def _(s, u=u, par=par):
                    s.wait_ge(tS.h, t_outg[u])
                    cs = u * 64
                    for j in range(6):
                        ins = s.copy(afm[:, 2 + j, cs:cs + 64],
                                     pog[:, par, j, :])
                    sS.inc(ins)
                    s_outg[u] = sS.n

        if finish_early('p6'):
            return nc

        _phase_scope("ph_p7")
        barrier_all()
        v_x2 = {}
        with nc.psum_tensor("po", [128, 2, 2, 512], F32) as po:
            t_o = {}
            rl_tgt = {}
            slo0, slo1 = proj_loads[("o", 0)], proj_loads[("o", 1)]
            for i in range(16):
                par = i % 2

                @block.sync
                def _(sy, i=i, par=par):
                    if i < 2:
                        sy.wait_ge(vS.h, kg_v[15])
                    if i >= 2:
                        sy.wait_ge(vS.h, v_x2[i - 2])
                    rl_tgt[i] = sd.dma(sy, x_roll[:, par, :],
                                       x1sp[i * 128:(i + 1) * 128, :])

                @block.tensor
                def _(t, i=i, par=par):
                    if i == 0:
                        t.wait_ge(gw.h, max(slo0[1], slo1[1]))
                    bh_last = i * 4 + 3
                    t.wait_ge(sS.h, o_done[bh_last])
                    t.wait_ge(sS.h, s_outg[2 * i + 1])
                    if i >= 2:
                        t.wait_ge(vS.h, v_x2[i - 2])
                    to = i * 128
                    for h, slh in ((0, slo0), (1, slo1)):
                        for c in range(NCH):
                            t.matmul(po[:, par, h, :],
                                     afm[:, c, to:to + 128],
                                     wbuf[slh[0]][:, c, :],
                                     start=(c == 0), stop=False)
                        ins = t.matmul(po[:, par, h, :],
                                       onesr[0:1, :],
                                       bo_sb[0:1, 512 * h:512 * h + 512],
                                       start=False, stop=True)
                    tS.inc(ins)
                    t_o[i] = tS.n

                @block.vector
                def _(v, i=i, par=par):
                    v.wait_ge(tS.h, t_o[i])
                    v.wait_ge(sd.h, rl_tgt[i])
                    v.tensor_tensor(x2[:, i, 0:512], po[:, par, 0, :],
                                    x_roll[:, par, 0:512], op=ALU.add)
                    vS.inc(v.tensor_tensor(x2[:, i, 512:1024], po[:, par, 1, :],
                                           x_roll[:, par, 512:1024],
                                           op=ALU.add))
                    v_x2[i] = vS.n

        for h in range(2):
            proj_loads[("o", h)][2][0] = tS
            proj_loads[("o", h)][2][1] = tS.n

        if STAGE == 2:
            dbgsel = os.environ.get("MK_DBG", "")
            if dbgsel:
                @block.vector
                def _(v):
                    v.wait_ge(vS.h, vS.n)
                    if dbgsel == "kv":
                        v.tensor_copy(scr_a[:, 0, :], kg[:, 0, :])
                        vS.inc(v.tensor_copy(scr_b[:, 0, 0:256], vg[:, 0, 0:256]))
                    elif dbgsel == "qs":
                        vS.inc(v.tensor_copy(
                            scr_a[:, :, :].rearrange("p a b -> p (a b)")[:, 0:768]
                            .rearrange("p (j t) -> p j t", t=128),
                            qs_fm[:, :, 0:128]))
                    elif dbgsel == "afm":
                        vS.inc(v.tensor_copy(
                            scr_a[:, :, :].rearrange("p a b -> p (a b)")[:, 0:768]
                            .rearrange("p (c t) -> p c t", t=128),
                            afm[:, 2:8, 64:192]))
                dbg_v = vS.n

                @block.gpsimd
                def _(g):
                    g.wait_ge(vS.h, dbg_v)
                    gw.dma(g, dbg_d[:, 0:768],
                           scr_a[:, :, :].rearrange("p a b -> p (a b)")[:, 0:768])
                    if dbgsel == "kv":
                        gw.dma(g, dbg_d[:, 768:1024], scr_b[:, 0, 0:256])

            @block.sync
            def _(sy):
                for i in range(16):
                    sy.wait_ge(vS.h, v_x2[i])
                    sd.dma(sy, out_d[i * 128:(i + 1) * 128, :], x2[:, i, :])
                if os.environ.get("MK_DBG") == "spill":
                    sd.dma(sy, dbg_d[:, :], x1sp[0:128, :])
                sy.wait_ge(sd.h, sd.n)
                sy.wait_ge(gw.h, gw.em)
            return nc

        wslot_rr[0] = 0
        ffn2_loads = plan_wloads(
            [((e, h), expT_d[e, :, 512 * h:512 * h + 512]) for e in range(NE)
             for h in range(2)])
        ffn2_plan = wload_plans[-1]

        _phase_scope("ph_p89")
        barrier_all()
        with nc.psum_tensor("pT8", [128, 2, NCH, 128], BF16) as pT8, \
             nc.psum_tensor("pY8", [128, 2, 2, 512], F32) as pY8, \
             nc.psum_tensor("pL8", [128, 2, 512], F32) as pL8:
            groups16 = list(range(16))
            _, xdone2 = emit_lnffn(
                groups16, lambda g: x2[:, g, :], afm, lambda g: g * 128,
                lambda g, h: x2[:, g, 512 * h:512 * h + 512],
                ffn2_loads, pY8, pL8, pT8,
                extra_s_wait=lambda g: (vS, v_x2[g]))
            emit_wload_body(ffn2_plan)

        @block.sync
        def _(sy):
            for i in range(16):
                sy.wait_ge(vS.h, xdone2[i])
                sd.dma(sy, out_d[i * 128:(i + 1) * 128, :], x2[:, i, :])
            sy.wait_ge(sd.h, sd.n)
            sy.wait_ge(gw.h, gw.n)

        return nc


    return nc


def _host_prep(L0_attn_x, L0_attn_Wq, L0_attn_Wk, L0_attn_Wv, L0_attn_Wo,
               L0_attn_bo, L0_ffn_gate_w, L0_ffn_expert_w):
    x = np.asarray(L0_attn_x, np.float32).reshape(16384, 1024)
    xpad = np.concatenate([np.zeros((HALO, D), np.float32), x], 0)
    wq = np.ascontiguousarray(np.asarray(L0_attn_Wq, np.float32))
    wk = np.ascontiguousarray(np.asarray(L0_attn_Wk, np.float32))
    wv = np.ascontiguousarray(np.asarray(L0_attn_Wv, np.float32))
    wo = np.ascontiguousarray(np.asarray(L0_attn_Wo, np.float32))
    expT = np.ascontiguousarray(
        np.asarray(L0_ffn_expert_w, np.float32).transpose(0, 2, 1))
    gT = np.ascontiguousarray(np.asarray(L0_ffn_gate_w, np.float32).T)
    bo = np.asarray(L0_attn_bo, np.float32).reshape(1, D)
    ident = np.eye(128, dtype=np.float32)
    onesc = np.ones((128, 1), np.float32)

    ii = np.arange(128)[:, None]
    jj = np.arange(256)[None, :]
    mrest = np.where((128 + ii) < jj, MASKVAL, 0.0).astype(np.float32)
    mfirst_core0 = mrest.copy()
    mfirst_core0[:, :128] = MASKVAL

    in_maps = []
    for c in range(NCORE):
        mfirst = mfirst_core0 if c == 0 else mrest
        prefm = np.tile((np.arange(NE) < c).astype(np.float32), (128, 1))
        in_maps.append({
            "xin": np.ascontiguousarray(xpad[c * TOK:c * TOK + TTOK]),
            "wq": wq, "wk": wk, "wv": wv, "wo": wo,
            "expT": expT, "gT": gT, "bo": bo,
            "ident": ident, "onesc": onesc,
            "onesr": np.ones((1, 128), np.float32),
            "onese": np.concatenate([np.ones((64, 1), np.float32),
                                     np.zeros((64, 1), np.float32)]),
            "masks": np.ascontiguousarray(
                np.stack([mfirst, mrest]).astype(np.float32)),
            "prefm": np.ascontiguousarray(prefm),
        })
    return in_maps


def kernel(L0_attn_x, L0_attn_Wq, L0_attn_Wk, L0_attn_Wv, L0_attn_Wo,
           L0_attn_bo, L0_ffn_gate_w, L0_ffn_expert_w):
    in_maps = _host_prep(L0_attn_x, L0_attn_Wq, L0_attn_Wk, L0_attn_Wv,
                         L0_attn_Wo, L0_attn_bo, L0_ffn_gate_w, L0_ffn_expert_w)
    nc = build()
    res = run_bass_kernel_spmd(nc, in_maps, core_ids=list(range(NCORE)))
    out = np.concatenate([r["out"] for r in res.results], 0)
    return out.reshape(1, 16384, 1024)




# version 15
# speedup vs baseline: 1.1002x; 1.0133x over previous
import os
import numpy as np
import concourse.bass as bass
import concourse.mybir as mybir
from concourse.bass_utils import run_bass_kernel_spmd
from contextlib import ExitStack

F32 = mybir.dt.float32
BF16 = mybir.dt.bfloat16
U8 = mybir.dt.uint8
AF = mybir.ActivationFunctionType
ALU = mybir.AluOpType
AX = mybir.AxisListType

STAGE = int(os.environ.get("MK_STAGE", "3"))
NEXP = int(os.environ.get("MK_NEXP", "8"))
NOGATE = int(os.environ.get("MK_NOGATE", "0"))
STOP = os.environ.get("MK_STOP", "")
P6N = int(os.environ.get("MK_P6N", "32"))

NCORE = 8
D = 1024
NCH = 8
TOK = 2048
HALO = 128
TTOK = TOK + HALO
G = 17
NE = 8
INV_D = 1.0 / D
MASKVAL = -8.0e9
NLNBUF = 4


class Sem:
    def __init__(self, ctx, nc, name):
        self.h = ctx.enter_context(nc.semaphore(name))
        self.n = 0
        self.em = 0

    def inc(self, inst, k=1):
        inst.then_inc(self.h, k)
        self.n += k
        self.em = max(self.em, self.n)
        return self.n

    def dma(self, eng, out, in_, **kw):
        eng.wait_ge(self.h, self.n)
        return self.inc(eng.dma_start(out=out, in_=in_, **kw), 16)

    def reserve(self, k):
        self.n += k
        return self.n


def build():
    nc = bass.Bass("TRN2", target_bir_lowering=False, debug=False,
                   detect_race_conditions=False)

    xin = nc.dram_tensor("xin", [TTOK, D], F32, kind="ExternalInput").ap()
    wq_d = nc.dram_tensor("wq", [D, D], F32, kind="ExternalInput").ap()
    wk_d = nc.dram_tensor("wk", [D, D], F32, kind="ExternalInput").ap()
    wv_d = nc.dram_tensor("wv", [D, D], F32, kind="ExternalInput").ap()
    wo_d = nc.dram_tensor("wo", [D, D], F32, kind="ExternalInput").ap()
    expT_d = nc.dram_tensor("expT", [NE, D, D], F32, kind="ExternalInput").ap()
    gT_d = nc.dram_tensor("gT", [D, NE], F32, kind="ExternalInput").ap()
    bo_d = nc.dram_tensor("bo", [1, D], F32, kind="ExternalInput").ap()
    ident_d = nc.dram_tensor("ident", [128, 128], F32, kind="ExternalInput").ap()
    ones_d = nc.dram_tensor("onesc", [128, 1], F32, kind="ExternalInput").ap()
    onesr_d = nc.dram_tensor("onesr", [1, 128], F32, kind="ExternalInput").ap()
    onese_d = nc.dram_tensor("onese", [128, 1], F32, kind="ExternalInput").ap()
    masks_d = nc.dram_tensor("masks", [2, 128, 256], F32, kind="ExternalInput").ap()
    pref_d = nc.dram_tensor("prefm", [128, NE], F32, kind="ExternalInput").ap()
    out_d = nc.dram_tensor("out", [TOK, D], F32, kind="ExternalOutput").ap()
    dbg_d = nc.dram_tensor("dbg", [128, 1024], F32, kind="ExternalOutput").ap()
    x1sp = nc.dram_tensor("x1spill", [TOK, D], F32).ap()
    ag_in = nc.dram_tensor("ag_in", [128, 390], F32)
    ag_out = nc.dram_tensor("ag_out", [128 * NCORE, 390], F32)

    ctx = ExitStack()
    with ctx:
        E = ctx.enter_context

        R1 = E(nc.sbuf_tensor("R1", [128, 73728], U8))
        r1b = R1.ap().bitcast(BF16)
        r1f = R1.ap().bitcast(F32)
        x_res = r1f[:, 0:17408].rearrange("p (g d) -> p g d", d=D)
        kg = r1b[:, 0:12288].rearrange("p (g d) -> p g d", d=768)
        vg = r1b[:, 12288:24576].rearrange("p (g d) -> p g d", d=768)
        qs_fm = r1b[:, 24576:36864].rearrange("p (j t) -> p j t", t=TOK)
        x2 = r1f[:, 0:16384].rearrange("p (g d) -> p g d", d=D)

        A = E(nc.sbuf_tensor("A", [128, 34816], U8))
        ab = A.ap().bitcast(BF16)
        xn17 = ab.rearrange("p (c t) -> p c t", t=TTOK)
        afm = ab[:, 0:16384].rearrange("p (c t) -> p c t", t=TOK)

        wbuf = [E(nc.sbuf_tensor(f"wb{i}", [128, NCH, 512], BF16))
                for i in range(3)]
        qloc = E(nc.sbuf_tensor("qloc", [128, 2, TOK], BF16))
        kloc = E(nc.sbuf_tensor("kloc", [128, 2, TTOK], BF16))
        vloc = E(nc.sbuf_tensor("vloc", [128, G, 256], BF16))
        xn_tm = E(nc.sbuf_tensor("xn_tm", [128, 4, D], BF16))
        sqscr = E(nc.sbuf_tensor("sqscr", [128, D], F32))
        scr_a = E(nc.sbuf_tensor("scr_a", [128, 2, 768], F32))

        qstm = E(nc.sbuf_tensor("qstm", [128, 2, 768], BF16))
        sqdmp = qstm.ap().rearrange("p a b -> p (a b)")[:, 0:1024]
        vstage = E(nc.sbuf_tensor("vstage", [128, 768], BF16))
        x_roll = E(nc.sbuf_tensor("x_roll", [128, 2, D], F32))
        scr_b = x_roll.ap()[:, :, 0:768]
        ctxs2 = E(nc.sbuf_tensor("ctxs2", [128, 2, 6, 128], BF16))
        ctxf = E(nc.sbuf_tensor("ctxf", [128, 6, 64], F32))
        w_full = E(nc.sbuf_tensor("w_full", [128, G, NE], F32))
        stats = E(nc.sbuf_tensor("stats", [128, 16], F32))
        gstat = E(nc.sbuf_tensor("gstat", [128, 8], F32))
        masks = E(nc.sbuf_tensor("masks_sb", [128, 2, 256], F32))
        probs_f = E(nc.sbuf_tensor("probs_f", [128, 2, 256], F32))
        probs_b = E(nc.sbuf_tensor("probs_b", [128, 2, 256], BF16))
        pts = E(nc.sbuf_tensor("pts", [128, 2, 128], BF16))
        ident = E(nc.sbuf_tensor("identb", [128, 128], BF16))
        onesc = E(nc.sbuf_tensor("onescb", [128, 1], BF16))
        onesr = E(nc.sbuf_tensor("onesrb", [1, 128], BF16))
        onese = E(nc.sbuf_tensor("oneseb", [128, 1], BF16))
        gT_sb = E(nc.sbuf_tensor("gTsb", [128, NCH, NE], BF16))
        bo_sb = E(nc.sbuf_tensor("bosb", [1, D], BF16))
        pref = E(nc.sbuf_tensor("prefsb", [128, NE], F32))
        tsb = E(nc.sbuf_tensor("tsb", [128, 6, 65], F32))
        tbuf = E(nc.sbuf_tensor("tbuf", [128, 6, 65], F32))
        carry = E(nc.sbuf_tensor("carry", [128, 6, 65], F32))
        rsg = E(nc.sbuf_tensor("rsg", [128, 12], F32))
        zrow = E(nc.sbuf_tensor("zrow", [128, 8], F32))

        sd = Sem(ctx, nc, "sd")
        gw = Sem(ctx, nc, "gw")
        cc = Sem(ctx, nc, "cc")
        tS = Sem(ctx, nc, "tS")
        vS = Sem(ctx, nc, "vS")
        sS = Sem(ctx, nc, "sS")

        block = E(nc.Block())

        wload_plans = []
        wslot_rr = [0]
        wslot_tenant = {}

        def plan_wloads(loads, slots=(0, 1, 2)):
            res = {}
            plan = []
            for key, ap in loads:
                slot = slots[wslot_rr[0] % len(slots)]
                wslot_rr[0] += 1
                free_ref = wslot_tenant.get(slot)
                tgt = gw.reserve(16)
                myref = [None, None]
                wslot_tenant[slot] = myref
                plan.append((key, ap, slot, tgt, free_ref))
                res[key] = (slot, tgt, myref)
            wload_plans.append(plan)
            return res

        def emit_wload_body(plan):
            @block.gpsimd
            def _(g):
                for key, ap, slot, tgt, free_ref in plan:
                    if free_ref is not None:
                        assert free_ref[0] is not None, f"free target unset {key}"
                        g.wait_ge(free_ref[0].h, free_ref[1])
                    g.wait_ge(gw.h, tgt - 16)
                    g.dma_start(
                        out=wbuf[slot][:, :, :],
                        in_=ap.rearrange("(c p) d -> p c d", p=128)
                    ).then_inc(gw.h, 16)
                    gw.em = max(gw.em, tgt)

        xload_tgt = []

        @block.sync
        def _(sy):
            for i in range(5):
                g0, g1 = i * 4, min((i + 1) * 4, G)
                if g0 >= g1:
                    break
                t = sd.dma(sy, x_res[:, g0:g1, :],
                           xin[g0 * 128:g1 * 128, :].rearrange(
                               "(g p) d -> p g d", p=128))
                xload_tgt.extend([t] * (g1 - g0))
            sd.dma(sy, masks[:, :, :], masks_d.rearrange("m p k -> p m k"))
            sd.dma(sy, pref[:, :], pref_d)
        misc_ld = sd.n

        @block.gpsimd
        def _(g):
            gw.dma(g, ident[:, :], ident_d)
            gw.dma(g, onesc[:, :], ones_d)
            gw.dma(g, onesr[:, :], onesr_d)
            gw.dma(g, onese[:, :], onese_d)
            gw.dma(g, gT_sb[:, :, :], gT_d.rearrange("(c p) e -> p c e", p=128))
            gw.dma(g, bo_sb[:, :], bo_d)
        consts_tgt = gw.n

        eps_tgt = [None]

        @block.vector
        def _(v):
            v.memset(ctxs2.ap().bitcast(U8), 0)
            v.memset(vstage.ap().bitcast(U8), 0)
            vS.inc(v.memset(stats[:, 6:7], 1e-5))
            eps_tgt[0] = vS.n

        @block.scalar
        def _(s):
            from concourse.hw_specs import get_activation_tables
            need = {AF.Exp, AF.Ln, AF.Square, AF.Identity}
            for idx, funcs in enumerate(get_activation_tables(nc.m.arch).values()):
                if need.issubset(set(funcs)):
                    s.add_instruction(mybir.InstLoadActFuncSet(
                        name=nc.get_next_instruction_name(),
                        act_func_set_id=idx, ins=[], outs=[]))
                    break
            else:
                raise RuntimeError("no act table set with all needed funcs")

        wslot_rr[0] = 0
        ffn1_loads = plan_wloads(
            [((e, h), expT_d[e, :, 512 * h:512 * h + 512]) for e in range(NE)
             for h in range(2)])

        def emit_sln(g, src, extra_s_wait, buf):
            s = nc.scalar
            s.wait_ge(vS.h, eps_tgt[0])
            if extra_s_wait is not None:
                s.wait_ge(extra_s_wait[0].h, extra_s_wait[1])
            tv = sS.inc(s.activation(sqdmp[:, :], src, AF.Identity,
                                     accum_out=stats[:, 0:1]))
            s.wait_ge(sS.h, tv)
            tv = sS.inc(s.activation(stats[:, 1:2], stats[:, 0:1], AF.Identity,
                                     scale=-INV_D))
            s.wait_ge(sS.h, tv)
            tv = sS.inc(s.activation(sqscr[:, :], src, AF.Identity,
                                     bias=stats[:, 1:2]))
            s.wait_ge(sS.h, tv)
            tv = sS.inc(s.activation(sqdmp[:, :], sqscr[:, :], AF.Square,
                                     accum_out=stats[:, 2:3]))
            s.wait_ge(sS.h, tv)
            tv = sS.inc(s.activation(stats[:, 3:4], stats[:, 2:3], AF.Identity,
                                     scale=INV_D, bias=stats[:, 6:7]))
            s.wait_ge(sS.h, tv)
            tv = sS.inc(s.activation(stats[:, 4:5], stats[:, 3:4], AF.Ln))
            s.wait_ge(sS.h, tv)
            tv = sS.inc(s.activation(stats[:, 5:6], stats[:, 4:5], AF.Exp,
                                     scale=-0.5))
            s.wait_ge(sS.h, tv)
            tv = sS.inc(s.activation(xn_tm[:, buf, :], sqscr[:, :],
                                     AF.Identity, scale=stats[:, 5:6]))
            return tv

        def emit_ln(groups, src_fn, xn_view, tok_off, pT, extra_v_wait=None,
                    extra_s_wait=None, spill=False):
            v0, s0, t0 = vS.n, sS.n, tS.n
            n = len(groups)
            ln_s = {}

            @block.scalar
            def _(s):
                for i, g in enumerate(groups):
                    w = extra_s_wait(g) if extra_s_wait is not None else None
                    if i >= NLNBUF:
                        s.wait_ge(tS.h, t0 + (i - NLNBUF) + 1)
                    ln_s[i] = emit_sln(g, src_fn(g), w, i % NLNBUF)

            @block.tensor
            def _(t):
                t.wait_ge(gw.h, consts_tgt)
                for i, g in enumerate(groups):
                    t.wait_ge(sS.h, s0 + 8 * i + 8)
                    if i >= 2:
                        t.wait_ge(vS.h, v0 + (i - 2) + 1)
                    for c in range(NCH):
                        ins = t.transpose(pT[:, i % 2, c, :],
                                          xn_tm[:, i % NLNBUF,
                                                c * 128:(c + 1) * 128],
                                          ident[:, :])
                    tS.inc(ins)

            @block.vector
            def _(v):
                for i, g in enumerate(groups):
                    v.wait_ge(tS.h, t0 + i + 1)
                    vS.inc(v.tensor_copy(
                        xn_view[:, :, tok_off(g):tok_off(g) + 128],
                        pT[:, i % 2, :, :]))

            if spill:
                @block.sync
                def _(sy):
                    for i, g in enumerate(groups):
                        if g == 0:
                            continue
                        sy.wait_ge(sS.h, ln_s[i])
                        sd.dma(sy, x1sp[(g - 1) * 128:g * 128, :], src_fn(g))
            ln_t = {i: t0 + i + 1 for i in range(n)}
            ln_v = {i: v0 + i + 1 for i in range(n)}
            return ln_s, ln_t, ln_v

        def emit_lnffn(groups, src_fn, xn_view, tok_off, res_fn, loads,
                       pY, pL, pT, extra_s_wait=None, spill=False):
            n = len(groups)
            ln_s = {}
            ln_t = {}
            ln_v = {}
            f0_t = {}
            f0_v = {}
            fvg = {}
            sg = {}
            sl00, sl01 = loads[(0, 0)], loads[(0, 1)]

            def e0_mms(t, j):
                g = groups[j]
                if j == 0:
                    t.wait_ge(gw.h, max(sl00[1], sl01[1]))
                t.wait_ge(vS.h, ln_v[j])
                if j >= 2:
                    t.wait_ge(vS.h, f0_v[j - 2])
                par = j % 2
                to = tok_off(g)
                for c in range(NCH):
                    lhs = xn_view[:, c, to:to + 128]
                    t.matmul(pY[:, par, 0, :], lhs, wbuf[sl00[0]][:, c, :],
                             start=(c == 0), stop=(c == NCH - 1))
                    t.matmul(pY[:, par, 1, :], lhs, wbuf[sl01[0]][:, c, :],
                             start=(c == 0), stop=(c == NCH - 1))
                    ins = t.matmul(pL[:, par, 0:8], lhs, gT_sb[:, c, :],
                                   start=(c == 0), stop=(c == NCH - 1))
                tS.inc(ins)
                f0_t[j] = tS.n

            def gating_part1(v, j):
                par = j % 2
                pl = pL[:, par, 0:8]
                v.wait_ge(tS.h, f0_t[j])
                tv = vS.inc(v.reduce_max(stats[:, 8:9], pl, axis=AX.X))
                v.wait_ge(vS.h, tv)
                tv = vS.inc(v.tensor_tensor(
                    zrow[:, :], pl, stats[:, 8:9].broadcast_to([128, NE]),
                    op=ALU.is_ge))
                v.wait_ge(vS.h, tv)
                tv = vS.inc(v.scalar_tensor_tensor(
                    w_full[:, groups[j], :], zrow[:, :], -1e30, pl,
                    op0=ALU.mult, op1=ALU.add))
                v.wait_ge(vS.h, tv)
                tv = vS.inc(v.reduce_max(stats[:, 9:10], w_full[:, groups[j], :],
                                         axis=AX.X))
                v.wait_ge(vS.h, tv)
                vS.inc(v.tensor_scalar_mul(stats[:, 10:11], stats[:, 8:9],
                                           -1.0))
                fvg[j] = vS.n

            def gating_exps(s, j):
                s.wait_ge(vS.h, fvg[j])
                par = j % 2
                sS.inc(s.activation(gstat[:, 0:8], pL[:, par, 0:8], AF.Exp,
                                    bias=stats[:, 10:11]))
                sS.inc(s.activation(stats[:, 11:12], stats[:, 9:10], AF.Exp,
                                    bias=stats[:, 10:11]))
                sg[j] = sS.n

            def gating_part2_combine(v, j):
                par = j % 2
                pl = pL[:, par, 0:8]
                g = groups[j]
                v.wait_ge(sS.h, sg[j])
                tv = vS.inc(v.tensor_scalar(
                    stats[:, 12:13], stats[:, 11:12], 2.0, 2.0,
                    op0=ALU.mult, op1=ALU.add))
                v.wait_ge(vS.h, tv)
                tv = vS.inc(v.reciprocal(stats[:, 13:14], stats[:, 12:13]))
                v.wait_ge(vS.h, tv)
                tv = vS.inc(v.tensor_tensor(
                    zrow[:, :], pl, stats[:, 9:10].broadcast_to([128, NE]),
                    op=ALU.is_ge))
                v.wait_ge(vS.h, tv)
                tv = vS.inc(v.scalar_tensor_tensor(
                    w_full[:, g, :], gstat[:, 0:8], stats[:, 13:14],
                    zrow[:, :], op0=ALU.mult, op1=ALU.mult))
                v.wait_ge(vS.h, tv)
                wcol = w_full[:, g, 0:1]
                v.scalar_tensor_tensor(
                    res_fn(g, 0), pY[:, par, 0, :], wcol, res_fn(g, 0),
                    op0=ALU.mult, op1=ALU.add)
                vS.inc(v.scalar_tensor_tensor(
                    res_fn(g, 1), pY[:, par, 1, :], wcol, res_fn(g, 1),
                    op0=ALU.mult, op1=ALU.add))
                f0_v[j] = vS.n

            for i, g in enumerate(groups):
                @block.scalar
                def _(s, i=i, g=g):
                    if i >= NLNBUF:
                        s.wait_ge(tS.h, ln_t[i - NLNBUF])
                    w = extra_s_wait(g) if extra_s_wait is not None else None
                    ln_s[i] = emit_sln(g, src_fn(g), w, i % NLNBUF)
                    if i >= 2:
                        gating_exps(s, i - 2)

                @block.tensor
                def _(t, i=i):
                    if i == 0:
                        t.wait_ge(gw.h, consts_tgt)
                    t.wait_ge(sS.h, ln_s[i])
                    if i >= 2:
                        t.wait_ge(vS.h, ln_v[i - 2])
                    for c in range(NCH):
                        ins = t.transpose(pT[:, i % 2, c, :],
                                          xn_tm[:, i % NLNBUF,
                                                c * 128:(c + 1) * 128],
                                          ident[:, :])
                    tS.inc(ins)
                    ln_t[i] = tS.n
                    if i >= 1:
                        e0_mms(t, i - 1)

                @block.vector
                def _(v, i=i, g=g):
                    v.wait_ge(tS.h, ln_t[i])
                    vS.inc(v.tensor_copy(
                        xn_view[:, :, tok_off(g):tok_off(g) + 128],
                        pT[:, i % 2, :, :]))
                    ln_v[i] = vS.n
                    if i >= 2:
                        gating_part2_combine(v, i - 2)
                    if i >= 1:
                        gating_part1(v, i - 1)

            @block.tensor
            def _(t):
                e0_mms(t, n - 1)

            @block.scalar
            def _(s):
                gating_exps(s, n - 2)

            @block.vector
            def _(v):
                gating_part2_combine(v, n - 2)
                gating_part1(v, n - 1)

            @block.scalar
            def _(s):
                gating_exps(s, n - 1)

            @block.vector
            def _(v):
                gating_part2_combine(v, n - 1)

            t_base = tS.n
            v_base = vS.n

            @block.tensor
            def _(t):
                for e in range(1, NEXP):
                    for h in range(2):
                        sl = loads[(e, h)]
                        t.wait_ge(gw.h, sl[1])
                        for i, g in enumerate(groups):
                            m = (2 * (e - 1) + h) * n + i
                            if m >= 2:
                                t.wait_ge(vS.h, v_base + m - 1)
                            else:
                                t.wait_ge(vS.h, f0_v[n - 2 + m])
                            par = i % 2
                            to = tok_off(g)
                            for c in range(NCH):
                                ins = t.matmul(pY[:, par, h, :],
                                               xn_view[:, c, to:to + 128],
                                               wbuf[sl[0]][:, c, :],
                                               start=(c == 0),
                                               stop=(c == NCH - 1))
                            tS.inc(ins)

            @block.vector
            def _(v):
                for e in range(1, NEXP):
                    for h in range(2):
                        for i, g in enumerate(groups):
                            m = (2 * (e - 1) + h) * n + i
                            v.wait_ge(tS.h, t_base + m + 1)
                            wcol = w_full[:, g, e:e + 1]
                            vS.inc(v.scalar_tensor_tensor(
                                res_fn(g, h), pY[:, i % 2, h, :], wcol,
                                res_fn(g, h), op0=ALU.mult, op1=ALU.add))

            loads[(0, 0)][2][0] = tS
            loads[(0, 0)][2][1] = f0_t[n - 1]
            loads[(0, 1)][2][0] = tS
            loads[(0, 1)][2][1] = f0_t[n - 1]
            for e in range(1, NEXP):
                for h in range(2):
                    ref = loads[(e, h)][2]
                    ref[0] = tS
                    ref[1] = t_base + (2 * (e - 1) + h) * n + n
            xdone = {i: v_base + 13 * n + i + 1 for i in range(n)}
            if spill:
                @block.sync
                def _(sy):
                    for i, g in enumerate(groups):
                        if g == 0:
                            continue
                        sy.wait_ge(vS.h, xdone[i])
                        sd.dma(sy, x1sp[(g - 1) * 128:g * 128, :], src_fn(g))
            return ln_v, xdone

        with nc.psum_tensor("pT1", [128, 2, NCH, 128], BF16) as pT1, \
             nc.psum_tensor("pY1", [128, 2, 2, 512], F32) as pY1, \
             nc.psum_tensor("pL1", [128, 2, 512], F32) as pL1:

            groups17 = list(range(G))
            _, xdone1 = emit_lnffn(
                groups17, lambda g: x_res[:, g, :], xn17, lambda g: g * 128,
                lambda g, h: x_res[:, g, 512 * h:512 * h + 512],
                ffn1_loads, pY1, pL1, pT1,
                extra_s_wait=lambda g: (sd, xload_tgt[g]))
            emit_wload_body(wload_plans[0])

        def x1_done(g):
            return xdone1[groups17.index(g)]

        if STAGE == 1:
            @block.gpsimd
            def _(g):
                g.wait_ge(vS.h, x1_done(16))
                gw.dma(g, dbg_d[:, 0:136], w_full[:, :, :].rearrange("p a b -> p (a b)"))

            @block.sync
            def _(sy):
                for g in range(1, G):
                    sy.wait_ge(vS.h, x1_done(g))
                    sd.dma(sy, out_d[(g - 1) * 128:g * 128, :], x_res[:, g, :])
                sy.wait_ge(sd.h, sd.n)
                sy.wait_ge(gw.h, gw.em)
            return nc

        def finish_early(tag):
            if STOP != tag:
                return False
            dbgsel = os.environ.get("MK_DBG", "")
            if dbgsel:
                @block.vector
                def _(v):
                    v.wait_ge(tS.h, tS.n)
                    v.wait_ge(sS.h, sS.n)
                    if dbgsel == "kv":
                        v.tensor_copy(scr_a[:, 0, :], kg[:, 0, :])
                        vS.inc(v.tensor_copy(scr_b[:, 0, 0:256],
                                             vg[:, 0, 0:256]))
                    elif dbgsel == "qs":
                        vS.inc(v.tensor_copy(
                            scr_a[:, 0, :].rearrange("p (j t) -> p j t", t=128),
                            qs_fm[:, :, 0:128]))
                    elif dbgsel == "afm":
                        vS.inc(v.tensor_copy(
                            scr_a[:, 0, :].rearrange("p (c t) -> p c t", t=128),
                            afm[:, 2:8, 64:192]))
                dbg_v = vS.n

                @block.gpsimd
                def _(g):
                    g.wait_ge(vS.h, dbg_v)
                    gw.dma(g, dbg_d[:, 0:768], scr_a[:, 0, :])
                    if dbgsel == "kv":
                        gw.dma(g, dbg_d[:, 768:1024], scr_b[:, 0, 0:256])

            @block.sync
            def _(sy):
                sy.wait_ge(tS.h, tS.n)
                sy.wait_ge(vS.h, vS.n)
                sy.wait_ge(sS.h, sS.n)
                sy.wait_ge(gw.h, gw.em)
                for g in range(1, G):
                    sd.dma(sy, out_d[(g - 1) * 128:g * 128, :], x_res[:, g, :])
                sy.wait_ge(sd.h, sd.n)
            return True

        _scopes = []

        def _phase_scope(tag):
            if _scopes:
                nm, sid = _scopes.pop()
                nc.leave_named_scope(nm, sid, False)
            sid, _ = nc.enter_named_scope(tag, False)
            _scopes.append((tag, sid))

        def barrier_all():
            tT, tV, tSS = tS.n, vS.n, sS.n

            @block.tensor
            def _(t):
                t.wait_ge(vS.h, tV)
                t.wait_ge(sS.h, tSS)

            @block.vector
            def _(v):
                v.wait_ge(tS.h, tT)
                v.wait_ge(sS.h, tSS)

            @block.scalar
            def _(s):
                s.wait_ge(tS.h, tT)
                s.wait_ge(vS.h, tV)

        wslot_rr[0] = 0
        proj_loads = plan_wloads([
            (("q", 0), wq_d[:, 0:512]), (("q", 1), wq_d[:, 512:1024]),
            (("k", 0), wk_d[:, 0:512]), (("k", 1), wk_d[:, 512:1024]),
            (("v", 0), wv_d[:, 0:512]), (("v", 1), wv_d[:, 512:1024]),
            (("o", 0), wo_d[:, 0:512]), (("o", 1), wo_d[:, 512:1024]),
        ])
        proj_plan = wload_plans[-1]
        emit_wload_body(proj_plan[0:3])

        _phase_scope("ph_p3")
        barrier_all()
        sl0, sl1 = proj_loads[("q", 0)], proj_loads[("q", 1)]
        qloc_done = {}
        with nc.psum_tensor("pT3", [128, 2, NCH, 128], BF16) as pT3, \
             nc.psum_tensor("pql", [128, 2, 2, 512], F32) as pql:
            ln_s3 = {}
            ln_t3 = {}
            ln_v3 = {}
            ql_t = {}

            def qloc_mms(t, ti):
                if ti == 0:
                    t.wait_ge(gw.h, sl0[1])
                t.wait_ge(vS.h, ln_v3[min(4 * ti + 5, G - 1)])
                if ti >= 2:
                    t.wait_ge(vS.h, qloc_done[ti - 2])
                t0c = 128 + 512 * ti
                par = ti % 2
                for dch in range(2):
                    for c in range(NCH):
                        ins = t.matmul(
                            pql[:, par, dch, :],
                            wbuf[sl0[0]][:, c, dch * 128:(dch + 1) * 128],
                            xn17[:, c, t0c:t0c + 512],
                            start=(c == 0), stop=(c == NCH - 1))
                    tS.inc(ins)
                ql_t[ti] = tS.n

            def qloc_evac(v, ti):
                v.wait_ge(tS.h, ql_t[ti])
                t0c = 512 * ti
                vS.inc(v.tensor_copy(qloc[:, :, t0c:t0c + 512],
                                     pql[:, ti % 2, 0:2, :]))
                qloc_done[ti] = vS.n

            ln_a = {}
            ln_c = {}
            ln_r = {}
            ln_x = {}
            for i in range(G):
                @block.scalar
                def _(s, i=i):
                    s.wait_ge(vS.h, eps_tgt[0])
                    s.wait_ge(vS.h, x1_done(i))
                    tv = sS.inc(s.activation(sqdmp[:, :], x_res[:, i, :],
                                             AF.Identity,
                                             accum_out=stats[:, 0:1]))
                    s.wait_ge(sS.h, tv)
                    ln_a[i] = sS.inc(s.activation(stats[:, 1:2], stats[:, 0:1],
                                                  AF.Identity, scale=-INV_D))

                @block.vector
                def _(v, i=i):
                    v.wait_ge(sS.h, ln_a[i])
                    ln_c[i] = vS.inc(v.tensor_scalar_add(
                        sqscr[:, :], x_res[:, i, :], stats[:, 1:2]))

                @block.scalar
                def _(s, i=i):
                    s.wait_ge(vS.h, ln_c[i])
                    tv = sS.inc(s.activation(sqdmp[:, :], sqscr[:, :],
                                             AF.Square,
                                             accum_out=stats[:, 2:3]))
                    s.wait_ge(sS.h, tv)
                    tv = sS.inc(s.activation(stats[:, 3:4], stats[:, 2:3],
                                             AF.Identity, scale=INV_D,
                                             bias=stats[:, 6:7]))
                    s.wait_ge(sS.h, tv)
                    tv = sS.inc(s.activation(stats[:, 4:5], stats[:, 3:4],
                                             AF.Ln))
                    s.wait_ge(sS.h, tv)
                    ln_r[i] = sS.inc(s.activation(stats[:, 5:6], stats[:, 4:5],
                                                  AF.Exp, scale=-0.5))

                @block.vector
                def _(v, i=i):
                    v.wait_ge(sS.h, ln_r[i])
                    if i >= NLNBUF:
                        v.wait_ge(tS.h, ln_t3[i - NLNBUF])
                    ln_x[i] = vS.inc(v.tensor_scalar_mul(
                        xn_tm[:, i % NLNBUF, :], sqscr[:, :], stats[:, 5:6]))
                    if i >= 1:
                        v.wait_ge(tS.h, ln_t3[i - 1])
                        vS.inc(v.tensor_copy(
                            xn17[:, :, (i - 1) * 128:(i - 1) * 128 + 128],
                            pT3[:, (i - 1) % 2, :, :]))
                        ln_v3[i - 1] = vS.n
                        if (i - 1) in (7, 11, 15):
                            qloc_evac(v, (i - 8) // 4)

                @block.tensor
                def _(t, i=i):
                    t.wait_ge(vS.h, ln_x[i])
                    if i >= 2:
                        t.wait_ge(vS.h, ln_v3[i - 2])
                    for c in range(NCH):
                        ins = t.transpose(pT3[:, i % 2, c, :],
                                          xn_tm[:, i % NLNBUF,
                                                c * 128:(c + 1) * 128],
                                          ident[:, :])
                    tS.inc(ins)
                    ln_t3[i] = tS.n
                    if i in (6, 10, 14):
                        qloc_mms(t, (i - 6) // 4)

            @block.vector
            def _(v):
                v.wait_ge(tS.h, ln_t3[G - 1])
                vS.inc(v.tensor_copy(
                    xn17[:, :, (G - 1) * 128:(G - 1) * 128 + 128],
                    pT3[:, (G - 1) % 2, :, :]))
                ln_v3[G - 1] = vS.n

            @block.tensor
            def _(t):
                qloc_mms(t, 3)

            @block.vector
            def _(v):
                qloc_evac(v, 3)

            @block.sync
            def _(sy):
                for i in range(1, G):
                    sy.wait_ge(sS.h, ln_a[i])
                    sy.wait_ge(vS.h, ln_c[i])
                    sd.dma(sy, x1sp[(i - 1) * 128:i * 128, :], x_res[:, i, :])

        spill_tgt = sd.n

        @block.vector
        def _(v):
            v.wait_ge(sd.h, spill_tgt)

        @block.scalar
        def _(s):
            s.wait_ge(sd.h, spill_tgt)

        @block.tensor
        def _(t):
            t.wait_ge(sd.h, spill_tgt)



        GOWN = list(range(1, G))

        def emit_fm_proj(wkey, dst, dst_ncols, tok_lo, tok_n, loads):
            sl0 = loads[(wkey, 0)]
            tiles = []
            t_ = tok_lo
            while t_ < tok_lo + tok_n:
                w = min(512, tok_lo + tok_n - t_)
                tiles.append((t_, w))
                t_ += w
            ev_tgt = {}
            mm_tgt = {}
            for ti, (t0c, w) in enumerate(tiles):
                @block.tensor
                def _(t, ti=ti, t0c=t0c, w=w):
                    if ti == 0:
                        t.wait_ge(gw.h, sl0[1])
                    if ti >= 2:
                        t.wait_ge(vS.h, ev_tgt[ti - 2])
                    par = ti % 2
                    for dch in range(dst_ncols // 128):
                        for c in range(NCH):
                            ins = t.matmul(
                                pql[:, par, dch, 0:w],
                                wbuf[sl0[0]][:, c, dch * 128:(dch + 1) * 128],
                                xn17[:, c, t0c:t0c + w],
                                start=(c == 0), stop=(c == NCH - 1))
                        tS.inc(ins)
                    mm_tgt[ti] = tS.n

                @block.vector
                def _(v, ti=ti, t0c=t0c, w=w):
                    v.wait_ge(tS.h, mm_tgt[ti])
                    par = ti % 2
                    vS.inc(v.tensor_copy(
                        dst[:, :, t0c - tok_lo:t0c - tok_lo + w],
                        pql[:, par, 0:dst_ncols // 128, 0:w]))
                    ev_tgt[ti] = vS.n
            return ev_tgt

        if finish_early('p4a'):
            return nc

        _phase_scope("ph_p4b")
        barrier_all()
        qg_v2 = {}
        with nc.psum_tensor("pqg", [128, 2, 2, 512], F32) as pqg, \
             nc.psum_tensor("ptrq", [128, 2, 6, 128], BF16) as ptrq:
            qg_t = {}
            qg_s = {}
            qg_v1 = {}
            qg_t2 = {}
            sl0, sl1 = proj_loads[("q", 0)], proj_loads[("q", 1)]
            for i, g in enumerate(GOWN):
                par = i % 2

                @block.tensor
                def _(t, i=i, g=g, par=par):
                    if i == 0:
                        t.wait_ge(gw.h, max(sl0[1], sl1[1]))
                    if i >= 2:
                        t.wait_ge(sS.h, qg_s[i - 2])
                        t.wait_ge(vS.h, qg_v1[i - 2])
                    to = g * 128
                    for c in range(NCH):
                        t.matmul(pqg[:, par, 0, 0:256],
                                 xn17[:, c, to:to + 128],
                                 wbuf[sl0[0]][:, c, 256:512],
                                 start=(c == 0), stop=(c == NCH - 1))
                        ins = t.matmul(pqg[:, par, 1, :],
                                       xn17[:, c, to:to + 128],
                                       wbuf[sl1[0]][:, c, :],
                                       start=(c == 0), stop=(c == NCH - 1))
                    tS.inc(ins)
                    qg_t[i] = tS.n

                @block.scalar
                def _(s, i=i, par=par):
                    s.wait_ge(tS.h, qg_t[i])
                    sS.inc(s.activation(scr_a[:, par, 0:256],
                                        pqg[:, par, 0, 0:256], AF.Exp))
                    sS.inc(s.activation(scr_a[:, par, 256:768],
                                        pqg[:, par, 1, :], AF.Exp))
                    qg_s[i] = sS.n

                @block.vector
                def _(v, i=i, par=par):
                    v.wait_ge(sS.h, qg_s[i])
                    sa = scr_a[:, par, :].rearrange("p (h e) -> p h e", e=64)
                    tv = vS.inc(v.reduce_sum(rsg[:, :], sa, axis=AX.X))
                    v.wait_ge(vS.h, tv)
                    tv = vS.inc(v.reciprocal(rsg[:, :], rsg[:, :]))
                    v.wait_ge(vS.h, tv)
                    vS.inc(v.tensor_tensor(
                        qstm[:, par, :].rearrange("p (h e) -> p h e", e=64),
                        sa,
                        rsg[:, :].unsqueeze(2).broadcast_to([128, 12, 64]),
                        op=ALU.mult))
                    qg_v1[i] = vS.n

                @block.tensor
                def _(t, i=i, par=par):
                    t.wait_ge(vS.h, qg_v1[i])
                    if i >= 2:
                        t.wait_ge(vS.h, qg_v2[i - 2])
                    for j in range(6):
                        ins = t.transpose(ptrq[:, par, j, :],
                                          qstm[:, par, j * 128:(j + 1) * 128],
                                          ident[:, :])
                    tS.inc(ins)
                    qg_t2[i] = tS.n

                @block.vector
                def _(v, i=i, g=g, par=par):
                    v.wait_ge(tS.h, qg_t2[i])
                    vS.inc(v.tensor_copy(
                        qs_fm[:, :, (g - 1) * 128:g * 128],
                        ptrq[:, par, :, :]))
                    qg_v2[i] = vS.n

        if finish_early('p4b'):
            return nc

        for h in range(2):
            proj_loads[("q", h)][2][0] = tS
            proj_loads[("q", h)][2][1] = tS.n
        emit_wload_body(proj_plan[3:5])

        _phase_scope("ph_p4c")
        barrier_all()
        with nc.psum_tensor("pkl", [128, 2, 2, 512], F32) as pql:
            kloc_done = emit_fm_proj("k", kloc, 256, 0, TTOK, proj_loads)

        if finish_early('p4c'):
            return nc

        _phase_scope("ph_p4d")
        barrier_all()
        with nc.psum_tensor("pkg", [128, 2, 2, 512], F32) as pkg:
            kg_t = {}
            kg_s = {}
            kg_vm = {}
            kg_v = {}
            sl0, sl1 = proj_loads[("k", 0)], proj_loads[("k", 1)]
            for i, g in enumerate(GOWN):
                par = i % 2

                @block.tensor
                def _(t, i=i, g=g, par=par):
                    if i == 0:
                        t.wait_ge(gw.h, max(sl0[1], sl1[1]))
                    if i >= 2:
                        t.wait_ge(sS.h, kg_s[i - 2])
                        t.wait_ge(vS.h, kg_vm[i - 2])
                    to = g * 128
                    for c in range(NCH):
                        t.matmul(pkg[:, par, 0, 0:256],
                                 xn17[:, c, to:to + 128],
                                 wbuf[sl0[0]][:, c, 256:512],
                                 start=(c == 0), stop=(c == NCH - 1))
                        ins = t.matmul(pkg[:, par, 1, :],
                                       xn17[:, c, to:to + 128],
                                       wbuf[sl1[0]][:, c, :],
                                       start=(c == 0), stop=(c == NCH - 1))
                    tS.inc(ins)
                    kg_t[i] = tS.n

                @block.scalar
                def _(s, i=i, par=par):
                    s.wait_ge(tS.h, kg_t[i])
                    if i >= 1:
                        s.wait_ge(vS.h, kg_v[i - 1])
                    s.activation(scr_b[:, par, 0:256], pkg[:, par, 0, 0:256],
                                 AF.Relu)
                    sS.inc(s.activation(scr_b[:, par, 256:768],
                                        pkg[:, par, 1, :], AF.Relu))
                    kg_s[i] = sS.n

                @block.vector
                def _(v, i=i, par=par):
                    v.wait_ge(tS.h, kg_t[i])
                    v.wait_ge(sS.h, kg_s[i])
                    v.tensor_scalar_min(scr_a[:, par, 0:256],
                                        pkg[:, par, 0, 0:256], 0.0)
                    vS.inc(v.tensor_scalar_min(scr_a[:, par, 256:768],
                                               pkg[:, par, 1, :], 0.0))
                    kg_vm[i] = vS.n

                @block.scalar
                def _(s, i=i, par=par):
                    s.wait_ge(vS.h, kg_vm[i])
                    sS.inc(s.activation(sqscr[:, 0:768], scr_a[:, par, :],
                                        AF.Exp))
                    kg_s2 = sS.n
                    kg_s[i] = max(kg_s[i], kg_s2)

                @block.vector
                def _(v, i=i, g=g, par=par):
                    v.wait_ge(sS.h, kg_s[i])
                    vS.inc(v.tensor_tensor(kg[:, g - 1, :], sqscr[:, 0:768],
                                           scr_b[:, par, :], op=ALU.add))
                    kg_v[i] = vS.n

        if finish_early('p4d'):
            return nc

        for h in range(2):
            proj_loads[("k", h)][2][0] = tS
            proj_loads[("k", h)][2][1] = tS.n
        emit_wload_body(proj_plan[5:7])

        _phase_scope("ph_p4e")
        barrier_all()
        with nc.psum_tensor("pvl", [128, 512], F32) as pvl, \
             nc.psum_tensor("pvg", [128, 2, 512], F32) as pvg, \
             nc.psum_tensor("pSt", [128, 8, 128], F32) as pSt:
            vl_t = {}
            vl_v = {}
            vg_t = {}
            vg_s = {}
            st_t = {}
            st_v = {}
            slv0, slv1 = proj_loads[("v", 0)], proj_loads[("v", 1)]
            for g in range(G):
                i = g - 1

                @block.tensor
                def _(t, g=g, i=i):
                    if g == 0:
                        t.wait_ge(gw.h, max(slv0[1], slv1[1]))
                    if g >= 1:
                        t.wait_ge(vS.h, vl_v[g - 1])
                    to = g * 128
                    for c in range(NCH):
                        ins = t.matmul(pvl[:, 0:256],
                                       xn17[:, c, to:to + 128],
                                       wbuf[slv0[0]][:, c, 0:256],
                                       start=(c == 0), stop=(c == NCH - 1))
                    tS.inc(ins)
                    vl_t[g] = tS.n

                @block.vector
                def _(v, g=g):
                    v.wait_ge(tS.h, vl_t[g])
                    vS.inc(v.tensor_copy(vloc[:, g, :], pvl[:, 0:256]))
                    vl_v[g] = vS.n

                if g == 0:
                    continue
                par = i % 2

                @block.tensor
                def _(t, g=g, i=i, par=par):
                    if i >= 1:
                        t.wait_ge(sS.h, vg_s[i - 1])
                    to = g * 128
                    for c in range(NCH):
                        t.matmul(pvg[:, 0, 0:256],
                                 xn17[:, c, to:to + 128],
                                 wbuf[slv0[0]][:, c, 256:512],
                                 start=(c == 0), stop=(c == NCH - 1))
                        ins = t.matmul(pvg[:, 1, :],
                                       xn17[:, c, to:to + 128],
                                       wbuf[slv1[0]][:, c, :],
                                       start=(c == 0), stop=(c == NCH - 1))
                    tS.inc(ins)
                    vg_t[i] = tS.n

                @block.scalar
                def _(s, i=i, g=g, par=par):
                    s.wait_ge(tS.h, vg_t[i])
                    s.copy(vg[:, g - 1, 0:256], pvg[:, 0, 0:256])
                    sS.inc(s.copy(vg[:, g - 1, 256:768], pvg[:, 1, :]))
                    vg_s[i] = sS.n

                @block.tensor
                def _(t, i=i, g=g):
                    t.wait_ge(sS.h, vg_s[i])
                    t.wait_ge(vS.h, kg_v[i])
                    if i >= 1:
                        t.wait_ge(vS.h, st_v[i - 1])
                    for j in range(6):
                        t.matmul(pSt[:, j, :],
                                 kg[:, i, j * 128:(j + 1) * 128],
                                 vg[:, i, j * 128:(j + 1) * 128],
                                 start=True, stop=True)
                        ins = t.matmul(pSt[:, 6, j:j + 1],
                                       kg[:, i, j * 128:(j + 1) * 128],
                                       onesc[:, 0:1],
                                       start=True, stop=True)
                    tS.inc(ins)
                    st_t[i] = tS.n

                @block.vector
                def _(v, i=i):
                    v.wait_ge(tS.h, st_t[i])
                    if i == 0:
                        v.tensor_copy(tsb[0:64, :, 0:64], pSt[0:64, 0:6, 0:64])
                        v.tensor_copy(tsb[64:128, :, 0:64],
                                      pSt[64:128, 0:6, 64:128])
                        vS.inc(v.tensor_copy(tsb[:, :, 64:65].squeeze(2),
                                             pSt[:, 6, 0:6]))
                    else:
                        v.scalar_tensor_tensor(
                            tsb[0:64, :, 0:64], pSt[0:64, 0:6, 0:64], 1.0,
                            tsb[0:64, :, 0:64], op0=ALU.mult, op1=ALU.add)
                        v.scalar_tensor_tensor(
                            tsb[64:128, :, 0:64], pSt[64:128, 0:6, 64:128], 1.0,
                            tsb[64:128, :, 0:64], op0=ALU.mult, op1=ALU.add)
                        vS.inc(v.scalar_tensor_tensor(
                            tsb[:, :, 64:65].squeeze(2), pSt[:, 6, 0:6], 1.0,
                            tsb[:, :, 64:65].squeeze(2),
                            op0=ALU.mult, op1=ALU.add))
                    st_v[i] = vS.n

            pub_s = [sS.n]
            pub_v = [st_v[15]]

        if finish_early('p4e'):
            return nc

        for h in range(2):
            proj_loads[("v", h)][2][0] = tS
            proj_loads[("v", h)][2][1] = tS.n
        emit_wload_body(proj_plan[7:8])

        _phase_scope("ph_p5")
        barrier_all()
        with nc.psum_tensor("psc", [128, 2, 512], F32) as psc, \
             nc.psum_tensor("ppt", [128, 2, 8, 128], BF16) as ppt, \
             nc.psum_tensor("pout", [128, 2, 512], F32) as pout:
            sc_t = {}
            v_mask = {}
            s_exp = {}
            v_pb = {}
            t_pt = {}
            s_pt = {}
            t_out = {}
            o_done = {}
            for step in range(66):
                if step < 64:
                    i = step
                    b, h, par = i // 4, i % 4, i % 2

                    @block.tensor
                    def _(t, i=i, b=b, h=h, par=par):
                        if i == 0:
                            t.wait_ge(sd.h, misc_ld)
                            t.wait_ge(vS.h, qloc_done[3])
                            t.wait_ge(vS.h, kloc_done[4])
                        if i >= 2:
                            t.wait_ge(vS.h, v_mask[i - 2])
                        r0 = (h % 2) * 64
                        ins = t.matmul(
                            psc[:, par, 0:256],
                            qloc[r0:r0 + 64, h // 2, b * 128:(b + 1) * 128],
                            kloc[r0:r0 + 64, h // 2, b * 128:b * 128 + 256],
                            start=True, stop=True)
                        tS.inc(ins)
                        sc_t[i] = tS.n

                if 0 <= step - 1 < 64:
                    i = step - 1
                    b, par = i // 4, i % 2
                    mrow = 0 if b == 0 else 1

                    @block.vector
                    def _(v, i=i, par=par, mrow=mrow):
                        v.wait_ge(tS.h, sc_t[i])
                        if i >= 2:
                            v.wait_ge(sS.h, s_exp[i - 2])
                        tv = vS.inc(v.tensor_tensor(
                            probs_f[:, par, :], psc[:, par, 0:256],
                            masks[:, mrow, :], op=ALU.add))
                        v.wait_ge(vS.h, tv)
                        vS.inc(v.reduce_max(stats[:, 8 + par:9 + par],
                                            probs_f[:, par, :], axis=AX.X))
                        v_mask[i] = vS.n

                    @block.scalar
                    def _(s, i=i, par=par):
                        s.wait_ge(vS.h, v_mask[i])
                        if i >= 2:
                            s.wait_ge(vS.h, v_pb[i - 2])
                        tv = sS.inc(s.activation(
                            stats[:, 10 + par:11 + par],
                            stats[:, 8 + par:9 + par], AF.Identity,
                            scale=-0.125))
                        s.wait_ge(sS.h, tv)
                        sS.inc(s.activation(
                            sqscr[:, par * 256:(par + 1) * 256],
                            probs_f[:, par, :], AF.Exp,
                            bias=stats[:, 10 + par:11 + par], scale=0.125,
                            accum_out=stats[:, 12 + par:13 + par]))
                        s_exp[i] = sS.n

                if 0 <= step - 2 < 64:
                    i = step - 2
                    b, h, par = i // 4, i % 4, i % 2

                    @block.vector
                    def _(v, i=i, par=par):
                        v.wait_ge(sS.h, s_exp[i])
                        if i >= 2:
                            v.wait_ge(tS.h, t_pt[i - 2])
                        tv = vS.inc(v.reciprocal(stats[:, 14 + par:15 + par],
                                                 stats[:, 12 + par:13 + par]))
                        v.wait_ge(vS.h, tv)
                        vS.inc(v.tensor_scalar_mul(
                            probs_b[:, par, :],
                            sqscr[:, par * 256:(par + 1) * 256],
                            stats[:, 14 + par:15 + par]))
                        v_pb[i] = vS.n

                    @block.tensor
                    def _(t, i=i, par=par):
                        t.wait_ge(vS.h, v_pb[i])
                        if i >= 2:
                            t.wait_ge(sS.h, s_pt[i - 2])
                        t.transpose(ppt[:, par, 0, :], probs_b[:, par, 0:128],
                                    ident[:, :])
                        ins = t.transpose(ppt[:, par, 1, :],
                                          probs_b[:, par, 128:256],
                                          ident[:, :])
                        tS.inc(ins)
                        t_pt[i] = tS.n

                    @block.scalar
                    def _(s, i=i, par=par):
                        s.wait_ge(tS.h, t_pt[i])
                        sS.inc(s.copy(pts[:, :, :], ppt[:, par, 0:2, :]))
                        s_pt[i] = sS.n

                    @block.tensor
                    def _(t, i=i, b=b, h=h, par=par):
                        t.wait_ge(sS.h, s_pt[i])
                        t.wait_ge(vS.h, vl_v[b + 1])
                        if i >= 2:
                            t.wait_ge(sS.h, o_done[i - 2])
                        hp = h // 2
                        t.matmul(pout[:, par, 0:128],
                                 vloc[:, b, hp * 128:(hp + 1) * 128],
                                 pts[:, 0, :], start=True, stop=False)
                        ins = t.matmul(pout[:, par, 0:128],
                                       vloc[:, b + 1, hp * 128:(hp + 1) * 128],
                                       pts[:, 1, :], start=False, stop=True)
                        tS.inc(ins)
                        t_out[i] = tS.n

                    @block.scalar
                    def _(s, i=i, b=b, h=h, par=par):
                        s.wait_ge(tS.h, t_out[i])
                        r0 = (h % 2) * 64
                        sS.inc(s.copy(
                            afm[r0:r0 + 64, h // 2, b * 128:(b + 1) * 128],
                            pout[r0:r0 + 64, par, 0:128]))
                        o_done[i] = sS.n

        tbuf_tgt = []
        carry_v = []
        cv0 = [None]

        @block.gpsimd
        def _(g):
            g.wait_ge(sS.h, pub_s[0])
            g.wait_ge(vS.h, pub_v[0])
            gw.dma(g, ag_in.ap(), tsb[:, :, :].rearrange("p j f -> p (j f)"))
            g.collective_compute(
                "AllGather", ALU.bypass,
                replica_groups=[list(range(NCORE))],
                ins=[ag_in.ap().opt()],
                outs=[ag_out.ap().opt()],
            ).then_inc(cc.h, 1)
            cc.n += 1
            g.wait_ge(cc.h, cc.n)
            vbase_c = vS.n
            for j in range(NCORE):
                if j >= 1:
                    g.wait_ge(vS.h, vbase_c + j)
                tbuf_tgt.append(gw.dma(g, tbuf[:, :, :].rearrange('p j f -> p (j f)'), ag_out.ap()[j * 128:(j + 1) * 128, :]))

        @block.vector
        def _(v):
            for j in range(NCORE):
                v.wait_ge(gw.h, tbuf_tgt[j])
                if j == 0:
                    tv = vS.inc(v.tensor_scalar_mul(
                        carry[:, :, :], tbuf[:, :, :], pref[:, 0:1]))
                else:
                    tv = vS.inc(v.scalar_tensor_tensor(
                        carry[:, :, :], tbuf[:, :, :],
                        pref[:, j:j + 1], carry[:, :, :],
                        op0=ALU.mult, op1=ALU.add))
                v.wait_ge(vS.h, tv)
                carry_v.append(tv)

        if finish_early('exch'):
            return nc

        if finish_early('p5'):
            return nc

        _phase_scope("ph_p6")
        barrier_all()
        v_ctx = {}
        t_outg = {}
        t_updA = {}
        t_updC = {}
        s_outg = {}
        s_ctx = {}
        asb = tsb
        with nc.psum_tensor("pC", [128, 8, 128], F32) as pC, \
             nc.psum_tensor("pG", [128, 8, 128], F32) as pG, \
             nc.psum_tensor("pog", [128, 2, 8, 64], F32) as pog:
            av0 = [None]

            @block.vector
            def _(v):
                v.wait_ge(vS.h, carry_v[7])
                tv = vS.inc(v.tensor_copy(asb[:, :, :], carry[:, :, :]))
                v.wait_ge(vS.h, tv)
                av0[0] = vS.n
            for u in range(P6N):
                par = u % 2
                i = u // 2
                odd = u % 2 == 1

                @block.vector
                def _(v, u=u, par=par, i=i, odd=odd):
                    if u == 0:
                        v.wait_ge(vS.h, av0[0])
                    if odd:
                        v.wait_ge(tS.h, t_updC[i])
                    if u >= 2:
                        v.wait_ge(tS.h, t_outg[u - 2])
                    if not odd and i >= 1:
                        v.wait_ge(tS.h, t_updC[i - 1])
                    if not odd:
                        vS.inc(v.tensor_copy(vstage[0:64, :], vg[0:64, i, :]))
                    if odd:
                        v.scalar_tensor_tensor(
                            ctxf[0:64, :, :], pC[0:64, 0:6, 0:64], 1.0,
                            asb[0:64, :, 0:64], op0=ALU.mult, op1=ALU.add)
                        v.scalar_tensor_tensor(
                            ctxf[64:128, :, :], pC[64:128, 0:6, 64:128], 1.0,
                            asb[64:128, :, 0:64], op0=ALU.mult, op1=ALU.add)
                        tv = vS.inc(v.scalar_tensor_tensor(
                            zrow[:, 0:6], pC[:, 6, 0:6], 1.0,
                            asb[:, :, 64:65].squeeze(2),
                            op0=ALU.mult, op1=ALU.add))
                        v.wait_ge(vS.h, tv)
                    else:
                        v.tensor_copy(ctxf[:, :, :], asb[:, :, 0:64])
                        tv = vS.inc(v.tensor_copy(zrow[:, 0:6],
                                                  asb[:, :, 64:65].squeeze(2)))
                        v.wait_ge(vS.h, tv)
                    tv = vS.inc(v.tensor_scalar_add(zrow[:, 0:6], zrow[:, 0:6],
                                                    1e-6))
                    v.wait_ge(vS.h, tv)
                    tv = vS.inc(v.reciprocal(gstat[:, 0:6], zrow[:, 0:6]))
                    v.wait_ge(vS.h, tv)
                    v.tensor_tensor(
                        ctxs2[0:64, par, :, 0:64], ctxf[0:64, :, :],
                        gstat[0:64, 0:6].unsqueeze(2).broadcast_to([64, 6, 64]),
                        op=ALU.mult)
                    tv = vS.inc(v.tensor_tensor(
                        ctxs2[64:128, par, :, 64:128], ctxf[64:128, :, :],
                        gstat[64:128, 0:6].unsqueeze(2).broadcast_to([64, 6, 64]),
                        op=ALU.mult))
                    v_ctx[u] = vS.n
                    if odd:
                        v.wait_ge(vS.h, tv)
                        v.wait_ge(tS.h, t_updA[i])
                        v.scalar_tensor_tensor(
                            asb[0:64, :, 0:64], pG[0:64, 0:6, 0:64], 1.0,
                            asb[0:64, :, 0:64], op0=ALU.mult, op1=ALU.add)
                        v.scalar_tensor_tensor(
                            asb[64:128, :, 0:64], pG[64:128, 0:6, 64:128], 1.0,
                            asb[64:128, :, 0:64], op0=ALU.mult, op1=ALU.add)
                        vS.inc(v.scalar_tensor_tensor(
                            asb[:, :, 64:65].squeeze(2), pG[:, 6, 0:6], 1.0,
                            asb[:, :, 64:65].squeeze(2),
                            op0=ALU.mult, op1=ALU.add))
                        v_ctx[u] = vS.n

                @block.tensor
                def _(t, u=u, par=par, i=i, odd=odd):
                    t.wait_ge(vS.h, v_ctx[u])
                    if u >= 2:
                        t.wait_ge(sS.h, s_outg[u - 2])
                    cs = u * 64
                    for j in range(6):
                        ins = t.matmul(pog[:, par, j, :],
                                       ctxs2[:, par, j, :],
                                       qs_fm[:, j, cs:cs + 64],
                                       start=True, stop=True)
                    tS.inc(ins)
                    t_outg[u] = tS.n
                    if not odd:
                        for j in range(6):
                            t.matmul(pC[:, j, :],
                                     kg[:, i, j * 128:(j + 1) * 128],
                                     vstage[:, j * 128:(j + 1) * 128],
                                     start=True, stop=True)
                            ins = t.matmul(pC[:, 6, j:j + 1],
                                           kg[:, i, j * 128:(j + 1) * 128],
                                           onese[:, 0:1],
                                           start=True, stop=True)
                        tS.inc(ins)
                        t_updC[i] = tS.n
                        if i >= 1:
                            t.wait_ge(vS.h, v_ctx[u - 1])
                        for j in range(6):
                            t.matmul(pG[:, j, :],
                                     kg[:, i, j * 128:(j + 1) * 128],
                                     vg[:, i, j * 128:(j + 1) * 128],
                                     start=True, stop=True)
                            ins = t.matmul(pG[:, 6, j:j + 1],
                                           kg[:, i, j * 128:(j + 1) * 128],
                                           onesc[:, 0:1],
                                           start=True, stop=True)
                        tS.inc(ins)
                        t_updA[i] = tS.n

                @block.scalar
                def _(s, u=u, par=par):
                    s.wait_ge(tS.h, t_outg[u])
                    cs = u * 64
                    for j in range(6):
                        ins = s.copy(afm[:, 2 + j, cs:cs + 64],
                                     pog[:, par, j, :])
                    sS.inc(ins)
                    s_outg[u] = sS.n

        if finish_early('p6'):
            return nc

        _phase_scope("ph_p7")
        barrier_all()
        v_x2 = {}
        with nc.psum_tensor("po", [128, 2, 2, 512], F32) as po:
            t_o = {}
            rl_tgt = {}
            slo0, slo1 = proj_loads[("o", 0)], proj_loads[("o", 1)]
            for i in range(16):
                par = i % 2

                @block.sync
                def _(sy, i=i, par=par):
                    if i < 2:
                        sy.wait_ge(vS.h, kg_v[15])
                    if i >= 2:
                        sy.wait_ge(vS.h, v_x2[i - 2])
                    rl_tgt[i] = sd.dma(sy, x_roll[:, par, :],
                                       x1sp[i * 128:(i + 1) * 128, :])

                @block.tensor
                def _(t, i=i, par=par):
                    if i == 0:
                        t.wait_ge(gw.h, max(slo0[1], slo1[1]))
                    bh_last = i * 4 + 3
                    t.wait_ge(sS.h, o_done[bh_last])
                    t.wait_ge(sS.h, s_outg[2 * i + 1])
                    if i >= 2:
                        t.wait_ge(vS.h, v_x2[i - 2])
                    to = i * 128
                    for h, slh in ((0, slo0), (1, slo1)):
                        for c in range(NCH):
                            t.matmul(po[:, par, h, :],
                                     afm[:, c, to:to + 128],
                                     wbuf[slh[0]][:, c, :],
                                     start=(c == 0), stop=False)
                        ins = t.matmul(po[:, par, h, :],
                                       onesr[0:1, :],
                                       bo_sb[0:1, 512 * h:512 * h + 512],
                                       start=False, stop=True)
                    tS.inc(ins)
                    t_o[i] = tS.n

                @block.vector
                def _(v, i=i, par=par):
                    v.wait_ge(tS.h, t_o[i])
                    v.wait_ge(sd.h, rl_tgt[i])
                    v.tensor_tensor(x2[:, i, 0:512], po[:, par, 0, :],
                                    x_roll[:, par, 0:512], op=ALU.add)
                    vS.inc(v.tensor_tensor(x2[:, i, 512:1024], po[:, par, 1, :],
                                           x_roll[:, par, 512:1024],
                                           op=ALU.add))
                    v_x2[i] = vS.n

        for h in range(2):
            proj_loads[("o", h)][2][0] = tS
            proj_loads[("o", h)][2][1] = tS.n

        if STAGE == 2:
            dbgsel = os.environ.get("MK_DBG", "")
            if dbgsel:
                @block.vector
                def _(v):
                    v.wait_ge(vS.h, vS.n)
                    if dbgsel == "kv":
                        v.tensor_copy(scr_a[:, 0, :], kg[:, 0, :])
                        vS.inc(v.tensor_copy(scr_b[:, 0, 0:256], vg[:, 0, 0:256]))
                    elif dbgsel == "qs":
                        vS.inc(v.tensor_copy(
                            scr_a[:, :, :].rearrange("p a b -> p (a b)")[:, 0:768]
                            .rearrange("p (j t) -> p j t", t=128),
                            qs_fm[:, :, 0:128]))
                    elif dbgsel == "afm":
                        vS.inc(v.tensor_copy(
                            scr_a[:, :, :].rearrange("p a b -> p (a b)")[:, 0:768]
                            .rearrange("p (c t) -> p c t", t=128),
                            afm[:, 2:8, 64:192]))
                dbg_v = vS.n

                @block.gpsimd
                def _(g):
                    g.wait_ge(vS.h, dbg_v)
                    gw.dma(g, dbg_d[:, 0:768],
                           scr_a[:, :, :].rearrange("p a b -> p (a b)")[:, 0:768])
                    if dbgsel == "kv":
                        gw.dma(g, dbg_d[:, 768:1024], scr_b[:, 0, 0:256])

            @block.sync
            def _(sy):
                for i in range(16):
                    sy.wait_ge(vS.h, v_x2[i])
                    sd.dma(sy, out_d[i * 128:(i + 1) * 128, :], x2[:, i, :])
                if os.environ.get("MK_DBG") == "spill":
                    sd.dma(sy, dbg_d[:, :], x1sp[0:128, :])
                sy.wait_ge(sd.h, sd.n)
                sy.wait_ge(gw.h, gw.em)
            return nc

        wslot_rr[0] = 0
        ffn2_loads = plan_wloads(
            [((e, h), expT_d[e, :, 512 * h:512 * h + 512]) for e in range(NE)
             for h in range(2)])
        ffn2_plan = wload_plans[-1]

        _phase_scope("ph_p89")
        barrier_all()
        with nc.psum_tensor("pT8", [128, 2, NCH, 128], BF16) as pT8, \
             nc.psum_tensor("pY8", [128, 2, 2, 512], F32) as pY8, \
             nc.psum_tensor("pL8", [128, 2, 512], F32) as pL8:
            groups16 = list(range(16))
            _, xdone2 = emit_lnffn(
                groups16, lambda g: x2[:, g, :], afm, lambda g: g * 128,
                lambda g, h: x2[:, g, 512 * h:512 * h + 512],
                ffn2_loads, pY8, pL8, pT8,
                extra_s_wait=lambda g: (vS, v_x2[g]))
            emit_wload_body(ffn2_plan)

        @block.sync
        def _(sy):
            for i in range(16):
                sy.wait_ge(vS.h, xdone2[i])
                sd.dma(sy, out_d[i * 128:(i + 1) * 128, :], x2[:, i, :])
            sy.wait_ge(sd.h, sd.n)
            sy.wait_ge(gw.h, gw.n)

        return nc


    return nc


def _host_prep(L0_attn_x, L0_attn_Wq, L0_attn_Wk, L0_attn_Wv, L0_attn_Wo,
               L0_attn_bo, L0_ffn_gate_w, L0_ffn_expert_w):
    x = np.asarray(L0_attn_x, np.float32).reshape(16384, 1024)
    xpad = np.concatenate([np.zeros((HALO, D), np.float32), x], 0)
    wq = np.ascontiguousarray(np.asarray(L0_attn_Wq, np.float32))
    wk = np.ascontiguousarray(np.asarray(L0_attn_Wk, np.float32))
    wv = np.ascontiguousarray(np.asarray(L0_attn_Wv, np.float32))
    wo = np.ascontiguousarray(np.asarray(L0_attn_Wo, np.float32))
    expT = np.ascontiguousarray(
        np.asarray(L0_ffn_expert_w, np.float32).transpose(0, 2, 1))
    gT = np.ascontiguousarray(np.asarray(L0_ffn_gate_w, np.float32).T)
    bo = np.asarray(L0_attn_bo, np.float32).reshape(1, D)
    ident = np.eye(128, dtype=np.float32)
    onesc = np.ones((128, 1), np.float32)

    ii = np.arange(128)[:, None]
    jj = np.arange(256)[None, :]
    mrest = np.where((128 + ii) < jj, MASKVAL, 0.0).astype(np.float32)
    mfirst_core0 = mrest.copy()
    mfirst_core0[:, :128] = MASKVAL

    in_maps = []
    for c in range(NCORE):
        mfirst = mfirst_core0 if c == 0 else mrest
        prefm = np.tile((np.arange(NE) < c).astype(np.float32), (128, 1))
        in_maps.append({
            "xin": np.ascontiguousarray(xpad[c * TOK:c * TOK + TTOK]),
            "wq": wq, "wk": wk, "wv": wv, "wo": wo,
            "expT": expT, "gT": gT, "bo": bo,
            "ident": ident, "onesc": onesc,
            "onesr": np.ones((1, 128), np.float32),
            "onese": np.concatenate([np.ones((64, 1), np.float32),
                                     np.zeros((64, 1), np.float32)]),
            "masks": np.ascontiguousarray(
                np.stack([mfirst, mrest]).astype(np.float32)),
            "prefm": np.ascontiguousarray(prefm),
        })
    return in_maps


def kernel(L0_attn_x, L0_attn_Wq, L0_attn_Wk, L0_attn_Wv, L0_attn_Wo,
           L0_attn_bo, L0_ffn_gate_w, L0_ffn_expert_w):
    in_maps = _host_prep(L0_attn_x, L0_attn_Wq, L0_attn_Wk, L0_attn_Wv,
                         L0_attn_Wo, L0_attn_bo, L0_ffn_gate_w, L0_ffn_expert_w)
    nc = build()
    res = run_bass_kernel_spmd(nc, in_maps, core_ids=list(range(NCORE)))
    out = np.concatenate([r["out"] for r in res.results], 0)
    return out.reshape(1, 16384, 1024)




# version 17 (final)
# speedup vs baseline: 1.1053x; 1.0046x over previous
import os
import numpy as np
import concourse.bass as bass
import concourse.mybir as mybir
from concourse.bass_utils import run_bass_kernel_spmd
from contextlib import ExitStack

F32 = mybir.dt.float32
BF16 = mybir.dt.bfloat16
U8 = mybir.dt.uint8
AF = mybir.ActivationFunctionType
ALU = mybir.AluOpType
AX = mybir.AxisListType

STAGE = int(os.environ.get("MK_STAGE", "3"))
NEXP = int(os.environ.get("MK_NEXP", "8"))
NOGATE = int(os.environ.get("MK_NOGATE", "0"))
STOP = os.environ.get("MK_STOP", "")
P6N = int(os.environ.get("MK_P6N", "32"))

NCORE = 8
D = 1024
NCH = 8
TOK = 2048
HALO = 128
TTOK = TOK + HALO
G = 17
NE = 8
INV_D = 1.0 / D
MASKVAL = -8.0e9
NLNBUF = 4


class Sem:
    def __init__(self, ctx, nc, name):
        self.h = ctx.enter_context(nc.semaphore(name))
        self.n = 0
        self.em = 0

    def inc(self, inst, k=1):
        inst.then_inc(self.h, k)
        self.n += k
        self.em = max(self.em, self.n)
        return self.n

    def dma(self, eng, out, in_, **kw):
        eng.wait_ge(self.h, self.n)
        return self.inc(eng.dma_start(out=out, in_=in_, **kw), 16)

    def reserve(self, k):
        self.n += k
        return self.n


def build():
    nc = bass.Bass("TRN2", target_bir_lowering=False, debug=False,
                   detect_race_conditions=False)

    xin = nc.dram_tensor("xin", [TTOK, D], F32, kind="ExternalInput").ap()
    wq_d = nc.dram_tensor("wq", [D, D], F32, kind="ExternalInput").ap()
    wk_d = nc.dram_tensor("wk", [D, D], F32, kind="ExternalInput").ap()
    wv_d = nc.dram_tensor("wv", [D, D], F32, kind="ExternalInput").ap()
    wo_d = nc.dram_tensor("wo", [D, D], F32, kind="ExternalInput").ap()
    expT_d = nc.dram_tensor("expT", [NE, D, D], F32, kind="ExternalInput").ap()
    gT_d = nc.dram_tensor("gT", [D, NE], F32, kind="ExternalInput").ap()
    bo_d = nc.dram_tensor("bo", [1, D], F32, kind="ExternalInput").ap()
    ident_d = nc.dram_tensor("ident", [128, 128], F32, kind="ExternalInput").ap()
    ones_d = nc.dram_tensor("onesc", [128, 1], F32, kind="ExternalInput").ap()
    onesr_d = nc.dram_tensor("onesr", [1, 128], F32, kind="ExternalInput").ap()
    onese_d = nc.dram_tensor("onese", [128, 1], F32, kind="ExternalInput").ap()
    masks_d = nc.dram_tensor("masks", [2, 128, 256], F32, kind="ExternalInput").ap()
    pref_d = nc.dram_tensor("prefm", [128, NE], F32, kind="ExternalInput").ap()
    out_d = nc.dram_tensor("out", [TOK, D], F32, kind="ExternalOutput").ap()
    dbg_d = nc.dram_tensor("dbg", [128, 1024], F32, kind="ExternalOutput").ap()
    x1sp = nc.dram_tensor("x1spill", [TOK, D], F32).ap()
    ag_in = nc.dram_tensor("ag_in", [128, 390], F32)
    ag_out = nc.dram_tensor("ag_out", [128 * NCORE, 390], F32)

    ctx = ExitStack()
    with ctx:
        E = ctx.enter_context

        R1 = E(nc.sbuf_tensor("R1", [128, 73728], U8))
        r1b = R1.ap().bitcast(BF16)
        r1f = R1.ap().bitcast(F32)
        x_res = r1f[:, 0:17408].rearrange("p (g d) -> p g d", d=D)
        kg = r1b[:, 0:12288].rearrange("p (g d) -> p g d", d=768)
        vg = r1b[:, 12288:24576].rearrange("p (g d) -> p g d", d=768)
        qs_fm = r1b[:, 24576:36864].rearrange("p (j t) -> p j t", t=TOK)
        x2 = r1f[:, 0:16384].rearrange("p (g d) -> p g d", d=D)

        A = E(nc.sbuf_tensor("A", [128, 34816], U8))
        ab = A.ap().bitcast(BF16)
        xn17 = ab.rearrange("p (c t) -> p c t", t=TTOK)
        afm = ab[:, 0:16384].rearrange("p (c t) -> p c t", t=TOK)

        wbuf = [E(nc.sbuf_tensor(f"wb{i}", [128, NCH, 512], BF16))
                for i in range(3)]
        qloc = E(nc.sbuf_tensor("qloc", [128, 2, TOK], BF16))
        kloc = E(nc.sbuf_tensor("kloc", [128, 2, TTOK], BF16))
        vloc = E(nc.sbuf_tensor("vloc", [128, G, 256], BF16))
        xn_tm = E(nc.sbuf_tensor("xn_tm", [128, 4, D], BF16))
        sqscr = E(nc.sbuf_tensor("sqscr", [128, D], F32))
        scr_a = E(nc.sbuf_tensor("scr_a", [128, 2, 768], F32))

        qstm = E(nc.sbuf_tensor("qstm", [128, 2, 768], BF16))
        sqdmp = qstm.ap().rearrange("p a b -> p (a b)")[:, 0:1024]
        vstage = E(nc.sbuf_tensor("vstage", [128, 768], BF16))
        x_roll = E(nc.sbuf_tensor("x_roll", [128, 2, D], F32))
        scr_b = x_roll.ap()[:, :, 0:768]
        ctxs2 = E(nc.sbuf_tensor("ctxs2", [128, 2, 6, 128], BF16))
        ctxf = E(nc.sbuf_tensor("ctxf", [128, 6, 64], F32))
        w_full = E(nc.sbuf_tensor("w_full", [128, G, NE], F32))
        stats = E(nc.sbuf_tensor("stats", [128, 16], F32))
        gstat = E(nc.sbuf_tensor("gstat", [128, 8], F32))
        masks = E(nc.sbuf_tensor("masks_sb", [128, 2, 256], F32))
        probs_f = E(nc.sbuf_tensor("probs_f", [128, 2, 256], F32))
        probs_b = E(nc.sbuf_tensor("probs_b", [128, 2, 256], BF16))
        pts = E(nc.sbuf_tensor("pts", [128, 2, 128], BF16))
        ident = E(nc.sbuf_tensor("identb", [128, 128], BF16))
        onesc = E(nc.sbuf_tensor("onescb", [128, 1], BF16))
        onesr = E(nc.sbuf_tensor("onesrb", [1, 128], BF16))
        onese = E(nc.sbuf_tensor("oneseb", [128, 1], BF16))
        gT_sb = E(nc.sbuf_tensor("gTsb", [128, NCH, NE], BF16))
        bo_sb = E(nc.sbuf_tensor("bosb", [1, D], BF16))
        pref = E(nc.sbuf_tensor("prefsb", [128, NE], F32))
        tsb = E(nc.sbuf_tensor("tsb", [128, 6, 65], F32))
        tbuf = E(nc.sbuf_tensor("tbuf", [128, 6, 65], F32))
        carry = E(nc.sbuf_tensor("carry", [128, 6, 65], F32))
        rsg = E(nc.sbuf_tensor("rsg", [128, 12], F32))
        zrow = E(nc.sbuf_tensor("zrow", [128, 8], F32))

        sd = Sem(ctx, nc, "sd")
        gw = Sem(ctx, nc, "gw")
        cc = Sem(ctx, nc, "cc")
        tS = Sem(ctx, nc, "tS")
        vS = Sem(ctx, nc, "vS")
        sS = Sem(ctx, nc, "sS")

        block = E(nc.Block())

        wload_plans = []
        wslot_rr = [0]
        wslot_tenant = {}

        def plan_wloads(loads, slots=(0, 1, 2)):
            res = {}
            plan = []
            for key, ap in loads:
                slot = slots[wslot_rr[0] % len(slots)]
                wslot_rr[0] += 1
                free_ref = wslot_tenant.get(slot)
                tgt = gw.reserve(16)
                myref = [None, None]
                wslot_tenant[slot] = myref
                plan.append((key, ap, slot, tgt, free_ref))
                res[key] = (slot, tgt, myref)
            wload_plans.append(plan)
            return res

        def emit_wload_body(plan):
            @block.gpsimd
            def _(g):
                for key, ap, slot, tgt, free_ref in plan:
                    if free_ref is not None:
                        assert free_ref[0] is not None, f"free target unset {key}"
                        g.wait_ge(free_ref[0].h, free_ref[1])
                    g.wait_ge(gw.h, tgt - 16)
                    g.dma_start(
                        out=wbuf[slot][:, :, :],
                        in_=ap.rearrange("(c p) d -> p c d", p=128)
                    ).then_inc(gw.h, 16)
                    gw.em = max(gw.em, tgt)

        xload_tgt = []

        @block.sync
        def _(sy):
            for i in range(5):
                g0, g1 = i * 4, min((i + 1) * 4, G)
                if g0 >= g1:
                    break
                t = sd.dma(sy, x_res[:, g0:g1, :],
                           xin[g0 * 128:g1 * 128, :].rearrange(
                               "(g p) d -> p g d", p=128))
                xload_tgt.extend([t] * (g1 - g0))
            sd.dma(sy, masks[:, :, :], masks_d.rearrange("m p k -> p m k"))
            sd.dma(sy, pref[:, :], pref_d)
        misc_ld = sd.n

        @block.gpsimd
        def _(g):
            gw.dma(g, ident[:, :], ident_d)
            gw.dma(g, onesc[:, :], ones_d)
            gw.dma(g, onesr[:, :], onesr_d)
            gw.dma(g, onese[:, :], onese_d)
            gw.dma(g, gT_sb[:, :, :], gT_d.rearrange("(c p) e -> p c e", p=128))
            gw.dma(g, bo_sb[:, :], bo_d)
        consts_tgt = gw.n

        eps_tgt = [None]

        @block.vector
        def _(v):
            v.memset(ctxs2.ap().bitcast(U8), 0)
            v.memset(vstage.ap().bitcast(U8), 0)
            vS.inc(v.memset(stats[:, 6:7], 1e-5))
            eps_tgt[0] = vS.n

        @block.scalar
        def _(s):
            from concourse.hw_specs import get_activation_tables
            need = {AF.Exp, AF.Ln, AF.Square, AF.Identity}
            for idx, funcs in enumerate(get_activation_tables(nc.m.arch).values()):
                if need.issubset(set(funcs)):
                    s.add_instruction(mybir.InstLoadActFuncSet(
                        name=nc.get_next_instruction_name(),
                        act_func_set_id=idx, ins=[], outs=[]))
                    break
            else:
                raise RuntimeError("no act table set with all needed funcs")

        wslot_rr[0] = 0
        ffn1_loads = plan_wloads(
            [((e, h), expT_d[e, :, 512 * h:512 * h + 512]) for e in range(NE)
             for h in range(2)])

        def emit_sln(g, src, extra_s_wait, buf):
            s = nc.scalar
            s.wait_ge(vS.h, eps_tgt[0])
            if extra_s_wait is not None:
                s.wait_ge(extra_s_wait[0].h, extra_s_wait[1])
            tv = sS.inc(s.activation(sqdmp[:, :], src, AF.Identity,
                                     accum_out=stats[:, 0:1]))
            s.wait_ge(sS.h, tv)
            tv = sS.inc(s.activation(stats[:, 1:2], stats[:, 0:1], AF.Identity,
                                     scale=-INV_D))
            s.wait_ge(sS.h, tv)
            tv = sS.inc(s.activation(sqscr[:, :], src, AF.Identity,
                                     bias=stats[:, 1:2]))
            s.wait_ge(sS.h, tv)
            tv = sS.inc(s.activation(sqdmp[:, :], sqscr[:, :], AF.Square,
                                     accum_out=stats[:, 2:3]))
            s.wait_ge(sS.h, tv)
            tv = sS.inc(s.activation(stats[:, 3:4], stats[:, 2:3], AF.Identity,
                                     scale=INV_D, bias=stats[:, 6:7]))
            s.wait_ge(sS.h, tv)
            tv = sS.inc(s.activation(stats[:, 4:5], stats[:, 3:4], AF.Ln))
            s.wait_ge(sS.h, tv)
            tv = sS.inc(s.activation(stats[:, 5:6], stats[:, 4:5], AF.Exp,
                                     scale=-0.5))
            s.wait_ge(sS.h, tv)
            tv = sS.inc(s.activation(xn_tm[:, buf, :], sqscr[:, :],
                                     AF.Identity, scale=stats[:, 5:6]))
            return tv

        def emit_ln(groups, src_fn, xn_view, tok_off, pT, extra_v_wait=None,
                    extra_s_wait=None, spill=False):
            v0, s0, t0 = vS.n, sS.n, tS.n
            n = len(groups)
            ln_s = {}

            @block.scalar
            def _(s):
                for i, g in enumerate(groups):
                    w = extra_s_wait(g) if extra_s_wait is not None else None
                    if i >= NLNBUF:
                        s.wait_ge(tS.h, t0 + (i - NLNBUF) + 1)
                    ln_s[i] = emit_sln(g, src_fn(g), w, i % NLNBUF)

            @block.tensor
            def _(t):
                t.wait_ge(gw.h, consts_tgt)
                for i, g in enumerate(groups):
                    t.wait_ge(sS.h, s0 + 8 * i + 8)
                    if i >= 2:
                        t.wait_ge(vS.h, v0 + (i - 2) + 1)
                    for c in range(NCH):
                        ins = t.transpose(pT[:, i % 2, c, :],
                                          xn_tm[:, i % NLNBUF,
                                                c * 128:(c + 1) * 128],
                                          ident[:, :])
                    tS.inc(ins)

            @block.vector
            def _(v):
                for i, g in enumerate(groups):
                    v.wait_ge(tS.h, t0 + i + 1)
                    vS.inc(v.tensor_copy(
                        xn_view[:, :, tok_off(g):tok_off(g) + 128],
                        pT[:, i % 2, :, :]))

            if spill:
                @block.sync
                def _(sy):
                    for i, g in enumerate(groups):
                        if g == 0:
                            continue
                        sy.wait_ge(sS.h, ln_s[i])
                        sd.dma(sy, x1sp[(g - 1) * 128:g * 128, :], src_fn(g))
            ln_t = {i: t0 + i + 1 for i in range(n)}
            ln_v = {i: v0 + i + 1 for i in range(n)}
            return ln_s, ln_t, ln_v

        def emit_lnffn(groups, src_fn, xn_view, tok_off, res_fn, loads,
                       pY, pL, pT, extra_s_wait=None, spill=False):
            n = len(groups)
            ln_s = {}
            ln_t = {}
            ln_v = {}
            f0_t = {}
            f0_v = {}
            fvg = {}
            sg = {}
            sl00, sl01 = loads[(0, 0)], loads[(0, 1)]

            def e0_mms(t, j):
                g = groups[j]
                if j == 0:
                    t.wait_ge(gw.h, max(sl00[1], sl01[1]))
                t.wait_ge(vS.h, ln_v[j])
                if j >= 2:
                    t.wait_ge(vS.h, f0_v[j - 2])
                par = j % 2
                to = tok_off(g)
                for c in range(NCH):
                    lhs = xn_view[:, c, to:to + 128]
                    t.matmul(pY[:, par, 0, :], lhs, wbuf[sl00[0]][:, c, :],
                             start=(c == 0), stop=(c == NCH - 1))
                    t.matmul(pY[:, par, 1, :], lhs, wbuf[sl01[0]][:, c, :],
                             start=(c == 0), stop=(c == NCH - 1))
                    ins = t.matmul(pL[:, par, 0:8], lhs, gT_sb[:, c, :],
                                   start=(c == 0), stop=(c == NCH - 1))
                tS.inc(ins)
                f0_t[j] = tS.n

            def gating_part1(v, j):
                par = j % 2
                pl = pL[:, par, 0:8]
                v.wait_ge(tS.h, f0_t[j])
                tv = vS.inc(v.reduce_max(stats[:, 8:9], pl, axis=AX.X))
                v.wait_ge(vS.h, tv)
                tv = vS.inc(v.tensor_tensor(
                    zrow[:, :], pl, stats[:, 8:9].broadcast_to([128, NE]),
                    op=ALU.is_ge))
                v.wait_ge(vS.h, tv)
                tv = vS.inc(v.scalar_tensor_tensor(
                    w_full[:, groups[j], :], zrow[:, :], -1e30, pl,
                    op0=ALU.mult, op1=ALU.add))
                v.wait_ge(vS.h, tv)
                tv = vS.inc(v.reduce_max(stats[:, 9:10], w_full[:, groups[j], :],
                                         axis=AX.X))
                v.wait_ge(vS.h, tv)
                vS.inc(v.tensor_scalar_mul(stats[:, 10:11], stats[:, 8:9],
                                           -1.0))
                fvg[j] = vS.n

            def gating_exps(s, j):
                s.wait_ge(vS.h, fvg[j])
                par = j % 2
                sS.inc(s.activation(gstat[:, 0:8], pL[:, par, 0:8], AF.Exp,
                                    bias=stats[:, 10:11]))
                sS.inc(s.activation(stats[:, 11:12], stats[:, 9:10], AF.Exp,
                                    bias=stats[:, 10:11]))
                sg[j] = sS.n

            def gating_part2_combine(v, j):
                par = j % 2
                pl = pL[:, par, 0:8]
                g = groups[j]
                v.wait_ge(sS.h, sg[j])
                tv = vS.inc(v.tensor_scalar(
                    stats[:, 12:13], stats[:, 11:12], 2.0, 2.0,
                    op0=ALU.mult, op1=ALU.add))
                v.wait_ge(vS.h, tv)
                tv = vS.inc(v.reciprocal(stats[:, 13:14], stats[:, 12:13]))
                v.wait_ge(vS.h, tv)
                tv = vS.inc(v.tensor_tensor(
                    zrow[:, :], pl, stats[:, 9:10].broadcast_to([128, NE]),
                    op=ALU.is_ge))
                v.wait_ge(vS.h, tv)
                tv = vS.inc(v.scalar_tensor_tensor(
                    w_full[:, g, :], gstat[:, 0:8], stats[:, 13:14],
                    zrow[:, :], op0=ALU.mult, op1=ALU.mult))
                v.wait_ge(vS.h, tv)
                wcol = w_full[:, g, 0:1]
                v.scalar_tensor_tensor(
                    res_fn(g, 0), pY[:, par, 0, :], wcol, res_fn(g, 0),
                    op0=ALU.mult, op1=ALU.add)
                vS.inc(v.scalar_tensor_tensor(
                    res_fn(g, 1), pY[:, par, 1, :], wcol, res_fn(g, 1),
                    op0=ALU.mult, op1=ALU.add))
                f0_v[j] = vS.n

            for i, g in enumerate(groups):
                @block.scalar
                def _(s, i=i, g=g):
                    if i >= NLNBUF:
                        s.wait_ge(tS.h, ln_t[i - NLNBUF])
                    w = extra_s_wait(g) if extra_s_wait is not None else None
                    ln_s[i] = emit_sln(g, src_fn(g), w, i % NLNBUF)
                    if i >= 2:
                        gating_exps(s, i - 2)

                @block.tensor
                def _(t, i=i):
                    if i == 0:
                        t.wait_ge(gw.h, consts_tgt)
                    t.wait_ge(sS.h, ln_s[i])
                    if i >= 2:
                        t.wait_ge(vS.h, ln_v[i - 2])
                    for c in range(NCH):
                        ins = t.transpose(pT[:, i % 2, c, :],
                                          xn_tm[:, i % NLNBUF,
                                                c * 128:(c + 1) * 128],
                                          ident[:, :])
                    tS.inc(ins)
                    ln_t[i] = tS.n
                    if i >= 1:
                        e0_mms(t, i - 1)

                @block.vector
                def _(v, i=i, g=g):
                    v.wait_ge(tS.h, ln_t[i])
                    vS.inc(v.tensor_copy(
                        xn_view[:, :, tok_off(g):tok_off(g) + 128],
                        pT[:, i % 2, :, :]))
                    ln_v[i] = vS.n
                    if i >= 2:
                        gating_part2_combine(v, i - 2)
                    if i >= 1:
                        gating_part1(v, i - 1)

            @block.tensor
            def _(t):
                e0_mms(t, n - 1)

            @block.scalar
            def _(s):
                gating_exps(s, n - 2)

            @block.vector
            def _(v):
                gating_part2_combine(v, n - 2)
                gating_part1(v, n - 1)

            @block.scalar
            def _(s):
                gating_exps(s, n - 1)

            @block.vector
            def _(v):
                gating_part2_combine(v, n - 1)

            t_base = tS.n
            v_base = vS.n

            @block.tensor
            def _(t):
                for e in range(1, NEXP):
                    for h in range(2):
                        sl = loads[(e, h)]
                        t.wait_ge(gw.h, sl[1])
                        for i, g in enumerate(groups):
                            m = (2 * (e - 1) + h) * n + i
                            if m >= 2:
                                t.wait_ge(vS.h, v_base + m - 1)
                            else:
                                t.wait_ge(vS.h, f0_v[n - 2 + m])
                            par = i % 2
                            to = tok_off(g)
                            for c in range(NCH):
                                ins = t.matmul(pY[:, par, h, :],
                                               xn_view[:, c, to:to + 128],
                                               wbuf[sl[0]][:, c, :],
                                               start=(c == 0),
                                               stop=(c == NCH - 1))
                            tS.inc(ins)

            @block.vector
            def _(v):
                for e in range(1, NEXP):
                    for h in range(2):
                        for i, g in enumerate(groups):
                            m = (2 * (e - 1) + h) * n + i
                            v.wait_ge(tS.h, t_base + m + 1)
                            wcol = w_full[:, g, e:e + 1]
                            vS.inc(v.scalar_tensor_tensor(
                                res_fn(g, h), pY[:, i % 2, h, :], wcol,
                                res_fn(g, h), op0=ALU.mult, op1=ALU.add))

            loads[(0, 0)][2][0] = tS
            loads[(0, 0)][2][1] = f0_t[n - 1]
            loads[(0, 1)][2][0] = tS
            loads[(0, 1)][2][1] = f0_t[n - 1]
            for e in range(1, NEXP):
                for h in range(2):
                    ref = loads[(e, h)][2]
                    ref[0] = tS
                    ref[1] = t_base + (2 * (e - 1) + h) * n + n
            xdone = {i: v_base + 13 * n + i + 1 for i in range(n)}
            if spill:
                @block.sync
                def _(sy):
                    for i, g in enumerate(groups):
                        if g == 0:
                            continue
                        sy.wait_ge(vS.h, xdone[i])
                        sd.dma(sy, x1sp[(g - 1) * 128:g * 128, :], src_fn(g))
            return ln_v, xdone

        with nc.psum_tensor("pT1", [128, 2, NCH, 128], BF16) as pT1, \
             nc.psum_tensor("pY1", [128, 2, 2, 512], F32) as pY1, \
             nc.psum_tensor("pL1", [128, 2, 512], F32) as pL1:

            groups17 = list(range(G))
            _, xdone1 = emit_lnffn(
                groups17, lambda g: x_res[:, g, :], xn17, lambda g: g * 128,
                lambda g, h: x_res[:, g, 512 * h:512 * h + 512],
                ffn1_loads, pY1, pL1, pT1,
                extra_s_wait=lambda g: (sd, xload_tgt[g]))
            emit_wload_body(wload_plans[0])

        def x1_done(g):
            return xdone1[groups17.index(g)]

        if STAGE == 1:
            @block.gpsimd
            def _(g):
                g.wait_ge(vS.h, x1_done(16))
                gw.dma(g, dbg_d[:, 0:136], w_full[:, :, :].rearrange("p a b -> p (a b)"))

            @block.sync
            def _(sy):
                for g in range(1, G):
                    sy.wait_ge(vS.h, x1_done(g))
                    sd.dma(sy, out_d[(g - 1) * 128:g * 128, :], x_res[:, g, :])
                sy.wait_ge(sd.h, sd.n)
                sy.wait_ge(gw.h, gw.em)
            return nc

        def finish_early(tag):
            if STOP != tag:
                return False
            dbgsel = os.environ.get("MK_DBG", "")
            if dbgsel:
                @block.vector
                def _(v):
                    v.wait_ge(tS.h, tS.n)
                    v.wait_ge(sS.h, sS.n)
                    if dbgsel == "kv":
                        v.tensor_copy(scr_a[:, 0, :], kg[:, 0, :])
                        vS.inc(v.tensor_copy(scr_b[:, 0, 0:256],
                                             vg[:, 0, 0:256]))
                    elif dbgsel == "qs":
                        vS.inc(v.tensor_copy(
                            scr_a[:, 0, :].rearrange("p (j t) -> p j t", t=128),
                            qs_fm[:, :, 0:128]))
                    elif dbgsel == "afm":
                        vS.inc(v.tensor_copy(
                            scr_a[:, 0, :].rearrange("p (c t) -> p c t", t=128),
                            afm[:, 2:8, 64:192]))
                dbg_v = vS.n

                @block.gpsimd
                def _(g):
                    g.wait_ge(vS.h, dbg_v)
                    gw.dma(g, dbg_d[:, 0:768], scr_a[:, 0, :])
                    if dbgsel == "kv":
                        gw.dma(g, dbg_d[:, 768:1024], scr_b[:, 0, 0:256])

            @block.sync
            def _(sy):
                sy.wait_ge(tS.h, tS.n)
                sy.wait_ge(vS.h, vS.n)
                sy.wait_ge(sS.h, sS.n)
                sy.wait_ge(gw.h, gw.em)
                for g in range(1, G):
                    sd.dma(sy, out_d[(g - 1) * 128:g * 128, :], x_res[:, g, :])
                sy.wait_ge(sd.h, sd.n)
            return True

        _scopes = []

        def _phase_scope(tag):
            if _scopes:
                nm, sid = _scopes.pop()
                nc.leave_named_scope(nm, sid, False)
            sid, _ = nc.enter_named_scope(tag, False)
            _scopes.append((tag, sid))

        def barrier_all():
            tT, tV, tSS = tS.n, vS.n, sS.n

            @block.tensor
            def _(t):
                t.wait_ge(vS.h, tV)
                t.wait_ge(sS.h, tSS)

            @block.vector
            def _(v):
                v.wait_ge(tS.h, tT)
                v.wait_ge(sS.h, tSS)

            @block.scalar
            def _(s):
                s.wait_ge(tS.h, tT)
                s.wait_ge(vS.h, tV)

        wslot_rr[0] = 0
        proj_loads = plan_wloads([
            (("q", 0), wq_d[:, 0:512]), (("q", 1), wq_d[:, 512:1024]),
            (("k", 0), wk_d[:, 0:512]), (("k", 1), wk_d[:, 512:1024]),
            (("v", 0), wv_d[:, 0:512]), (("v", 1), wv_d[:, 512:1024]),
            (("o", 0), wo_d[:, 0:512]), (("o", 1), wo_d[:, 512:1024]),
        ])
        proj_plan = wload_plans[-1]
        emit_wload_body(proj_plan[0:3])

        _phase_scope("ph_p3")
        barrier_all()
        sl0, sl1 = proj_loads[("q", 0)], proj_loads[("q", 1)]
        qloc_done = {}
        with nc.psum_tensor("pT3", [128, 2, NCH, 128], BF16) as pT3, \
             nc.psum_tensor("pql", [128, 2, 2, 512], F32) as pql:
            ln_s3 = {}
            ln_t3 = {}
            ln_v3 = {}
            ql_t = {}

            def qloc_mms(t, ti):
                if ti == 0:
                    t.wait_ge(gw.h, sl0[1])
                t.wait_ge(vS.h, ln_v3[min(4 * ti + 5, G - 1)])
                if ti >= 2:
                    t.wait_ge(vS.h, qloc_done[ti - 2])
                t0c = 128 + 512 * ti
                par = ti % 2
                for dch in range(2):
                    for c in range(NCH):
                        ins = t.matmul(
                            pql[:, par, dch, :],
                            wbuf[sl0[0]][:, c, dch * 128:(dch + 1) * 128],
                            xn17[:, c, t0c:t0c + 512],
                            start=(c == 0), stop=(c == NCH - 1))
                    tS.inc(ins)
                ql_t[ti] = tS.n

            def qloc_evac(v, ti):
                v.wait_ge(tS.h, ql_t[ti])
                t0c = 512 * ti
                vS.inc(v.tensor_copy(qloc[:, :, t0c:t0c + 512],
                                     pql[:, ti % 2, 0:2, :]))
                qloc_done[ti] = vS.n

            ln_a = {}
            ln_c = {}
            ln_r = {}
            ln_x = {}
            for i in range(G):
                @block.scalar
                def _(s, i=i):
                    s.wait_ge(vS.h, eps_tgt[0])
                    s.wait_ge(vS.h, x1_done(i))
                    tv = sS.inc(s.activation(sqdmp[:, :], x_res[:, i, :],
                                             AF.Identity,
                                             accum_out=stats[:, 0:1]))
                    s.wait_ge(sS.h, tv)
                    ln_a[i] = sS.inc(s.activation(stats[:, 1:2], stats[:, 0:1],
                                                  AF.Identity, scale=-INV_D))

                @block.vector
                def _(v, i=i):
                    v.wait_ge(sS.h, ln_a[i])
                    ln_c[i] = vS.inc(v.tensor_scalar_add(
                        sqscr[:, :], x_res[:, i, :], stats[:, 1:2]))

                @block.scalar
                def _(s, i=i):
                    s.wait_ge(vS.h, ln_c[i])
                    tv = sS.inc(s.activation(sqdmp[:, :], sqscr[:, :],
                                             AF.Square,
                                             accum_out=stats[:, 2:3]))
                    s.wait_ge(sS.h, tv)
                    tv = sS.inc(s.activation(stats[:, 3:4], stats[:, 2:3],
                                             AF.Identity, scale=INV_D,
                                             bias=stats[:, 6:7]))
                    s.wait_ge(sS.h, tv)
                    tv = sS.inc(s.activation(stats[:, 4:5], stats[:, 3:4],
                                             AF.Ln))
                    s.wait_ge(sS.h, tv)
                    ln_r[i] = sS.inc(s.activation(stats[:, 5:6], stats[:, 4:5],
                                                  AF.Exp, scale=-0.5))

                @block.vector
                def _(v, i=i):
                    v.wait_ge(sS.h, ln_r[i])
                    if i >= NLNBUF:
                        v.wait_ge(tS.h, ln_t3[i - NLNBUF])
                    ln_x[i] = vS.inc(v.tensor_scalar_mul(
                        xn_tm[:, i % NLNBUF, :], sqscr[:, :], stats[:, 5:6]))
                    if i >= 1:
                        v.wait_ge(tS.h, ln_t3[i - 1])
                        vS.inc(v.tensor_copy(
                            xn17[:, :, (i - 1) * 128:(i - 1) * 128 + 128],
                            pT3[:, (i - 1) % 2, :, :]))
                        ln_v3[i - 1] = vS.n
                        if (i - 1) in (7, 11, 15):
                            qloc_evac(v, (i - 8) // 4)

                @block.tensor
                def _(t, i=i):
                    t.wait_ge(vS.h, ln_x[i])
                    if i >= 2:
                        t.wait_ge(vS.h, ln_v3[i - 2])
                    for c in range(NCH):
                        ins = t.transpose(pT3[:, i % 2, c, :],
                                          xn_tm[:, i % NLNBUF,
                                                c * 128:(c + 1) * 128],
                                          ident[:, :])
                    tS.inc(ins)
                    ln_t3[i] = tS.n
                    if i in (6, 10, 14):
                        qloc_mms(t, (i - 6) // 4)

            @block.vector
            def _(v):
                v.wait_ge(tS.h, ln_t3[G - 1])
                vS.inc(v.tensor_copy(
                    xn17[:, :, (G - 1) * 128:(G - 1) * 128 + 128],
                    pT3[:, (G - 1) % 2, :, :]))
                ln_v3[G - 1] = vS.n

            @block.tensor
            def _(t):
                qloc_mms(t, 3)

            @block.vector
            def _(v):
                qloc_evac(v, 3)

            @block.sync
            def _(sy):
                for i in range(1, G):
                    sy.wait_ge(sS.h, ln_a[i])
                    sy.wait_ge(vS.h, ln_c[i])
                    sd.dma(sy, x1sp[(i - 1) * 128:i * 128, :], x_res[:, i, :])

        spill_tgt = sd.n

        @block.vector
        def _(v):
            v.wait_ge(sd.h, spill_tgt)

        @block.scalar
        def _(s):
            s.wait_ge(sd.h, spill_tgt)

        @block.tensor
        def _(t):
            t.wait_ge(sd.h, spill_tgt)



        GOWN = list(range(1, G))

        def emit_fm_proj(wkey, dst, dst_ncols, tok_lo, tok_n, loads):
            sl0 = loads[(wkey, 0)]
            tiles = []
            t_ = tok_lo
            while t_ < tok_lo + tok_n:
                w = min(512, tok_lo + tok_n - t_)
                tiles.append((t_, w))
                t_ += w
            ev_tgt = {}
            mm_tgt = {}
            for ti, (t0c, w) in enumerate(tiles):
                @block.tensor
                def _(t, ti=ti, t0c=t0c, w=w):
                    if ti == 0:
                        t.wait_ge(gw.h, sl0[1])
                    if ti >= 2:
                        t.wait_ge(vS.h, ev_tgt[ti - 2])
                    par = ti % 2
                    for dch in range(dst_ncols // 128):
                        for c in range(NCH):
                            ins = t.matmul(
                                pql[:, par, dch, 0:w],
                                wbuf[sl0[0]][:, c, dch * 128:(dch + 1) * 128],
                                xn17[:, c, t0c:t0c + w],
                                start=(c == 0), stop=(c == NCH - 1))
                        tS.inc(ins)
                    mm_tgt[ti] = tS.n

                @block.vector
                def _(v, ti=ti, t0c=t0c, w=w):
                    v.wait_ge(tS.h, mm_tgt[ti])
                    par = ti % 2
                    vS.inc(v.tensor_copy(
                        dst[:, :, t0c - tok_lo:t0c - tok_lo + w],
                        pql[:, par, 0:dst_ncols // 128, 0:w]))
                    ev_tgt[ti] = vS.n
            return ev_tgt

        if finish_early('p4a'):
            return nc

        _phase_scope("ph_p4b")
        barrier_all()
        qg_v2 = {}
        with nc.psum_tensor("pqg", [128, 2, 2, 512], F32) as pqg, \
             nc.psum_tensor("ptrq", [128, 2, 6, 128], BF16) as ptrq:
            qg_t = {}
            qg_s = {}
            qg_v1 = {}
            qg_t2 = {}
            sl0, sl1 = proj_loads[("q", 0)], proj_loads[("q", 1)]
            for i, g in enumerate(GOWN):
                par = i % 2

                @block.tensor
                def _(t, i=i, g=g, par=par):
                    if i == 0:
                        t.wait_ge(gw.h, max(sl0[1], sl1[1]))
                    if i >= 2:
                        t.wait_ge(sS.h, qg_s[i - 2])
                        t.wait_ge(vS.h, qg_v1[i - 2])
                    to = g * 128
                    for c in range(NCH):
                        t.matmul(pqg[:, par, 0, 0:256],
                                 xn17[:, c, to:to + 128],
                                 wbuf[sl0[0]][:, c, 256:512],
                                 start=(c == 0), stop=(c == NCH - 1))
                        ins = t.matmul(pqg[:, par, 1, :],
                                       xn17[:, c, to:to + 128],
                                       wbuf[sl1[0]][:, c, :],
                                       start=(c == 0), stop=(c == NCH - 1))
                    tS.inc(ins)
                    qg_t[i] = tS.n

                @block.scalar
                def _(s, i=i, par=par):
                    s.wait_ge(tS.h, qg_t[i])
                    sS.inc(s.activation(scr_a[:, par, 0:256],
                                        pqg[:, par, 0, 0:256], AF.Exp))
                    sS.inc(s.activation(scr_a[:, par, 256:768],
                                        pqg[:, par, 1, :], AF.Exp))
                    qg_s[i] = sS.n

                @block.vector
                def _(v, i=i, par=par):
                    v.wait_ge(sS.h, qg_s[i])
                    sa = scr_a[:, par, :].rearrange("p (h e) -> p h e", e=64)
                    tv = vS.inc(v.reduce_sum(rsg[:, :], sa, axis=AX.X))
                    v.wait_ge(vS.h, tv)
                    tv = vS.inc(v.reciprocal(rsg[:, :], rsg[:, :]))
                    v.wait_ge(vS.h, tv)
                    vS.inc(v.tensor_tensor(
                        qstm[:, par, :].rearrange("p (h e) -> p h e", e=64),
                        sa,
                        rsg[:, :].unsqueeze(2).broadcast_to([128, 12, 64]),
                        op=ALU.mult))
                    qg_v1[i] = vS.n

                @block.tensor
                def _(t, i=i, par=par):
                    t.wait_ge(vS.h, qg_v1[i])
                    if i >= 2:
                        t.wait_ge(vS.h, qg_v2[i - 2])
                    for j in range(6):
                        ins = t.transpose(ptrq[:, par, j, :],
                                          qstm[:, par, j * 128:(j + 1) * 128],
                                          ident[:, :])
                    tS.inc(ins)
                    qg_t2[i] = tS.n

                @block.vector
                def _(v, i=i, g=g, par=par):
                    v.wait_ge(tS.h, qg_t2[i])
                    vS.inc(v.tensor_copy(
                        qs_fm[:, :, (g - 1) * 128:g * 128],
                        ptrq[:, par, :, :]))
                    qg_v2[i] = vS.n

        if finish_early('p4b'):
            return nc

        for h in range(2):
            proj_loads[("q", h)][2][0] = tS
            proj_loads[("q", h)][2][1] = tS.n
        emit_wload_body(proj_plan[3:5])

        _phase_scope("ph_p4c")
        barrier_all()
        with nc.psum_tensor("pkl", [128, 2, 2, 512], F32) as pql:
            kloc_done = emit_fm_proj("k", kloc, 256, 0, TTOK, proj_loads)

        if finish_early('p4c'):
            return nc

        _phase_scope("ph_p4d")
        barrier_all()
        with nc.psum_tensor("pkg", [128, 2, 2, 512], F32) as pkg:
            kg_t = {}
            kg_s = {}
            kg_vm = {}
            kg_v = {}
            sl0, sl1 = proj_loads[("k", 0)], proj_loads[("k", 1)]
            for i, g in enumerate(GOWN):
                par = i % 2

                @block.tensor
                def _(t, i=i, g=g, par=par):
                    if i == 0:
                        t.wait_ge(gw.h, max(sl0[1], sl1[1]))
                    if i >= 2:
                        t.wait_ge(sS.h, kg_s[i - 2])
                        t.wait_ge(vS.h, kg_vm[i - 2])
                    to = g * 128
                    for c in range(NCH):
                        t.matmul(pkg[:, par, 0, 0:256],
                                 xn17[:, c, to:to + 128],
                                 wbuf[sl0[0]][:, c, 256:512],
                                 start=(c == 0), stop=(c == NCH - 1))
                        ins = t.matmul(pkg[:, par, 1, :],
                                       xn17[:, c, to:to + 128],
                                       wbuf[sl1[0]][:, c, :],
                                       start=(c == 0), stop=(c == NCH - 1))
                    tS.inc(ins)
                    kg_t[i] = tS.n

                @block.scalar
                def _(s, i=i, par=par):
                    s.wait_ge(tS.h, kg_t[i])
                    if i >= 1:
                        s.wait_ge(vS.h, kg_v[i - 1])
                    s.activation(scr_b[:, par, 0:256], pkg[:, par, 0, 0:256],
                                 AF.Relu)
                    sS.inc(s.activation(scr_b[:, par, 256:768],
                                        pkg[:, par, 1, :], AF.Relu))
                    kg_s[i] = sS.n

                @block.vector
                def _(v, i=i, par=par):
                    v.wait_ge(tS.h, kg_t[i])
                    v.wait_ge(sS.h, kg_s[i])
                    v.tensor_scalar_min(scr_a[:, par, 0:256],
                                        pkg[:, par, 0, 0:256], 0.0)
                    vS.inc(v.tensor_scalar_min(scr_a[:, par, 256:768],
                                               pkg[:, par, 1, :], 0.0))
                    kg_vm[i] = vS.n

                @block.scalar
                def _(s, i=i, par=par):
                    s.wait_ge(vS.h, kg_vm[i])
                    sS.inc(s.activation(sqscr[:, 0:768], scr_a[:, par, :],
                                        AF.Exp))
                    kg_s2 = sS.n
                    kg_s[i] = max(kg_s[i], kg_s2)

                @block.vector
                def _(v, i=i, g=g, par=par):
                    v.wait_ge(sS.h, kg_s[i])
                    vS.inc(v.tensor_tensor(kg[:, g - 1, :], sqscr[:, 0:768],
                                           scr_b[:, par, :], op=ALU.add))
                    kg_v[i] = vS.n

        if finish_early('p4d'):
            return nc

        for h in range(2):
            proj_loads[("k", h)][2][0] = tS
            proj_loads[("k", h)][2][1] = tS.n
        emit_wload_body(proj_plan[5:7])

        _phase_scope("ph_p4e")
        barrier_all()
        with nc.psum_tensor("pvl", [128, 512], F32) as pvl, \
             nc.psum_tensor("pvg", [128, 2, 512], F32) as pvg, \
             nc.psum_tensor("pSt", [128, 8, 128], F32) as pSt:
            vl_t = {}
            vl_v = {}
            vg_t = {}
            vg_s = {}
            st_t = {}
            st_v = {}
            slv0, slv1 = proj_loads[("v", 0)], proj_loads[("v", 1)]
            for g in range(G):
                i = g - 1

                @block.tensor
                def _(t, g=g, i=i):
                    if g == 0:
                        t.wait_ge(gw.h, max(slv0[1], slv1[1]))
                    if g >= 1:
                        t.wait_ge(vS.h, vl_v[g - 1])
                    to = g * 128
                    for c in range(NCH):
                        ins = t.matmul(pvl[:, 0:256],
                                       xn17[:, c, to:to + 128],
                                       wbuf[slv0[0]][:, c, 0:256],
                                       start=(c == 0), stop=(c == NCH - 1))
                    tS.inc(ins)
                    vl_t[g] = tS.n

                @block.vector
                def _(v, g=g):
                    v.wait_ge(tS.h, vl_t[g])
                    vS.inc(v.tensor_copy(vloc[:, g, :], pvl[:, 0:256]))
                    vl_v[g] = vS.n

                if g == 0:
                    continue
                par = i % 2

                @block.tensor
                def _(t, g=g, i=i, par=par):
                    if i >= 1:
                        t.wait_ge(sS.h, vg_s[i - 1])
                    to = g * 128
                    for c in range(NCH):
                        t.matmul(pvg[:, 0, 0:256],
                                 xn17[:, c, to:to + 128],
                                 wbuf[slv0[0]][:, c, 256:512],
                                 start=(c == 0), stop=(c == NCH - 1))
                        ins = t.matmul(pvg[:, 1, :],
                                       xn17[:, c, to:to + 128],
                                       wbuf[slv1[0]][:, c, :],
                                       start=(c == 0), stop=(c == NCH - 1))
                    tS.inc(ins)
                    vg_t[i] = tS.n

                @block.scalar
                def _(s, i=i, g=g, par=par):
                    s.wait_ge(tS.h, vg_t[i])
                    s.copy(vg[:, g - 1, 0:256], pvg[:, 0, 0:256])
                    sS.inc(s.copy(vg[:, g - 1, 256:768], pvg[:, 1, :]))
                    vg_s[i] = sS.n

                @block.tensor
                def _(t, i=i, g=g):
                    t.wait_ge(sS.h, vg_s[i])
                    t.wait_ge(vS.h, kg_v[i])
                    if i >= 1:
                        t.wait_ge(vS.h, st_v[i - 1])
                    for j in range(6):
                        t.matmul(pSt[:, j, :],
                                 kg[:, i, j * 128:(j + 1) * 128],
                                 vg[:, i, j * 128:(j + 1) * 128],
                                 start=True, stop=True)
                        ins = t.matmul(pSt[:, 6, j:j + 1],
                                       kg[:, i, j * 128:(j + 1) * 128],
                                       onesc[:, 0:1],
                                       start=True, stop=True)
                    tS.inc(ins)
                    st_t[i] = tS.n

                @block.vector
                def _(v, i=i):
                    v.wait_ge(tS.h, st_t[i])
                    if i == 0:
                        v.tensor_copy(tsb[0:64, :, 0:64], pSt[0:64, 0:6, 0:64])
                        v.tensor_copy(tsb[64:128, :, 0:64],
                                      pSt[64:128, 0:6, 64:128])
                        vS.inc(v.tensor_copy(tsb[:, :, 64:65].squeeze(2),
                                             pSt[:, 6, 0:6]))
                    else:
                        v.scalar_tensor_tensor(
                            tsb[0:64, :, 0:64], pSt[0:64, 0:6, 0:64], 1.0,
                            tsb[0:64, :, 0:64], op0=ALU.mult, op1=ALU.add)
                        v.scalar_tensor_tensor(
                            tsb[64:128, :, 0:64], pSt[64:128, 0:6, 64:128], 1.0,
                            tsb[64:128, :, 0:64], op0=ALU.mult, op1=ALU.add)
                        vS.inc(v.scalar_tensor_tensor(
                            tsb[:, :, 64:65].squeeze(2), pSt[:, 6, 0:6], 1.0,
                            tsb[:, :, 64:65].squeeze(2),
                            op0=ALU.mult, op1=ALU.add))
                    st_v[i] = vS.n

            pub_s = [sS.n]
            pub_v = [st_v[15]]

        if finish_early('p4e'):
            return nc

        for h in range(2):
            proj_loads[("v", h)][2][0] = tS
            proj_loads[("v", h)][2][1] = tS.n
        emit_wload_body(proj_plan[7:8])

        _phase_scope("ph_p5")
        barrier_all()
        with nc.psum_tensor("psc", [128, 2, 512], F32) as psc, \
             nc.psum_tensor("ppt", [128, 2, 8, 128], BF16) as ppt, \
             nc.psum_tensor("pout", [128, 2, 512], F32) as pout:
            sc_t = {}
            v_mask = {}
            s_exp = {}
            v_pb = {}
            t_pt = {}
            s_pt = {}
            t_out = {}
            o_done = {}
            for step in range(66):
                if step < 64:
                    i = step
                    b, h, par = i // 4, i % 4, i % 2

                    @block.tensor
                    def _(t, i=i, b=b, h=h, par=par):
                        if i == 0:
                            t.wait_ge(sd.h, misc_ld)
                            t.wait_ge(vS.h, qloc_done[3])
                            t.wait_ge(vS.h, kloc_done[4])
                        if i >= 2:
                            t.wait_ge(vS.h, v_mask[i - 2])
                        r0 = (h % 2) * 64
                        ins = t.matmul(
                            psc[:, par, 0:256],
                            qloc[r0:r0 + 64, h // 2, b * 128:(b + 1) * 128],
                            kloc[r0:r0 + 64, h // 2, b * 128:b * 128 + 256],
                            start=True, stop=True)
                        tS.inc(ins)
                        sc_t[i] = tS.n

                if 0 <= step - 1 < 64:
                    i = step - 1
                    b, par = i // 4, i % 2
                    mrow = 0 if b == 0 else 1

                    @block.vector
                    def _(v, i=i, par=par, mrow=mrow):
                        v.wait_ge(tS.h, sc_t[i])
                        if i >= 2:
                            v.wait_ge(sS.h, s_exp[i - 2])
                        vS.inc(v.tensor_tensor(
                            probs_f[:, par, :], psc[:, par, 0:256],
                            masks[:, mrow, :], op=ALU.add))
                        v_mask[i] = vS.n

                    @block.scalar
                    def _(s, i=i, par=par):
                        s.wait_ge(vS.h, v_mask[i])
                        if i >= 2:
                            s.wait_ge(vS.h, v_pb[i - 2])
                        sS.inc(s.activation(
                            sqscr[:, par * 256:(par + 1) * 256],
                            probs_f[:, par, :], AF.Exp, scale=0.125,
                            accum_out=stats[:, 12 + par:13 + par]))
                        s_exp[i] = sS.n

                if 0 <= step - 2 < 64:
                    i = step - 2
                    b, h, par = i // 4, i % 4, i % 2

                    @block.vector
                    def _(v, i=i, par=par):
                        v.wait_ge(sS.h, s_exp[i])
                        if i >= 2:
                            v.wait_ge(tS.h, t_pt[i - 2])
                        tv = vS.inc(v.reciprocal(stats[:, 14 + par:15 + par],
                                                 stats[:, 12 + par:13 + par]))
                        v.wait_ge(vS.h, tv)
                        vS.inc(v.tensor_scalar_mul(
                            probs_b[:, par, :],
                            sqscr[:, par * 256:(par + 1) * 256],
                            stats[:, 14 + par:15 + par]))
                        v_pb[i] = vS.n

                    @block.tensor
                    def _(t, i=i, par=par):
                        t.wait_ge(vS.h, v_pb[i])
                        if i >= 2:
                            t.wait_ge(sS.h, s_pt[i - 2])
                        t.transpose(ppt[:, par, 0, :], probs_b[:, par, 0:128],
                                    ident[:, :])
                        ins = t.transpose(ppt[:, par, 1, :],
                                          probs_b[:, par, 128:256],
                                          ident[:, :])
                        tS.inc(ins)
                        t_pt[i] = tS.n

                    @block.scalar
                    def _(s, i=i, par=par):
                        s.wait_ge(tS.h, t_pt[i])
                        sS.inc(s.copy(pts[:, :, :], ppt[:, par, 0:2, :]))
                        s_pt[i] = sS.n

                    @block.tensor
                    def _(t, i=i, b=b, h=h, par=par):
                        t.wait_ge(sS.h, s_pt[i])
                        t.wait_ge(vS.h, vl_v[b + 1])
                        if i >= 2:
                            t.wait_ge(sS.h, o_done[i - 2])
                        hp = h // 2
                        t.matmul(pout[:, par, 0:128],
                                 vloc[:, b, hp * 128:(hp + 1) * 128],
                                 pts[:, 0, :], start=True, stop=False)
                        ins = t.matmul(pout[:, par, 0:128],
                                       vloc[:, b + 1, hp * 128:(hp + 1) * 128],
                                       pts[:, 1, :], start=False, stop=True)
                        tS.inc(ins)
                        t_out[i] = tS.n

                    @block.scalar
                    def _(s, i=i, b=b, h=h, par=par):
                        s.wait_ge(tS.h, t_out[i])
                        r0 = (h % 2) * 64
                        sS.inc(s.copy(
                            afm[r0:r0 + 64, h // 2, b * 128:(b + 1) * 128],
                            pout[r0:r0 + 64, par, 0:128]))
                        o_done[i] = sS.n

        tbuf_tgt = []
        carry_v = []
        cv0 = [None]

        @block.gpsimd
        def _(g):
            g.wait_ge(sS.h, pub_s[0])
            g.wait_ge(vS.h, pub_v[0])
            gw.dma(g, ag_in.ap(), tsb[:, :, :].rearrange("p j f -> p (j f)"))
            g.collective_compute(
                "AllGather", ALU.bypass,
                replica_groups=[list(range(NCORE))],
                ins=[ag_in.ap().opt()],
                outs=[ag_out.ap().opt()],
            ).then_inc(cc.h, 1)
            cc.n += 1
            g.wait_ge(cc.h, cc.n)
            vbase_c = vS.n
            for j in range(NCORE):
                if j >= 1:
                    g.wait_ge(vS.h, vbase_c + j)
                tbuf_tgt.append(gw.dma(g, tbuf[:, :, :].rearrange('p j f -> p (j f)'), ag_out.ap()[j * 128:(j + 1) * 128, :]))

        @block.vector
        def _(v):
            for j in range(NCORE):
                v.wait_ge(gw.h, tbuf_tgt[j])
                if j == 0:
                    tv = vS.inc(v.tensor_scalar_mul(
                        carry[:, :, :], tbuf[:, :, :], pref[:, 0:1]))
                else:
                    tv = vS.inc(v.scalar_tensor_tensor(
                        carry[:, :, :], tbuf[:, :, :],
                        pref[:, j:j + 1], carry[:, :, :],
                        op0=ALU.mult, op1=ALU.add))
                v.wait_ge(vS.h, tv)
                carry_v.append(tv)

        if finish_early('exch'):
            return nc

        if finish_early('p5'):
            return nc

        _phase_scope("ph_p6")
        barrier_all()
        v_ctx = {}
        t_outg = {}
        t_updA = {}
        t_updC = {}
        s_outg = {}
        s_ctx = {}
        asb = tsb
        with nc.psum_tensor("pC", [128, 8, 128], F32) as pC, \
             nc.psum_tensor("pG", [128, 8, 128], F32) as pG, \
             nc.psum_tensor("pog", [128, 2, 8, 64], F32) as pog:
            av0 = [None]

            @block.vector
            def _(v):
                v.wait_ge(vS.h, carry_v[7])
                tv = vS.inc(v.tensor_copy(asb[:, :, :], carry[:, :, :]))
                v.wait_ge(vS.h, tv)
                av0[0] = vS.n
            for u in range(P6N):
                par = u % 2
                i = u // 2
                odd = u % 2 == 1

                @block.vector
                def _(v, u=u, par=par, i=i, odd=odd):
                    if u == 0:
                        v.wait_ge(vS.h, av0[0])
                    if odd:
                        v.wait_ge(tS.h, t_updC[i])
                    if u >= 2:
                        v.wait_ge(tS.h, t_outg[u - 2])
                    if not odd and i >= 1:
                        v.wait_ge(tS.h, t_updC[i - 1])
                    if not odd:
                        vS.inc(v.tensor_copy(vstage[0:64, :], vg[0:64, i, :]))
                    if odd:
                        v.scalar_tensor_tensor(
                            ctxf[0:64, :, :], pC[0:64, 0:6, 0:64], 1.0,
                            asb[0:64, :, 0:64], op0=ALU.mult, op1=ALU.add)
                        v.scalar_tensor_tensor(
                            ctxf[64:128, :, :], pC[64:128, 0:6, 64:128], 1.0,
                            asb[64:128, :, 0:64], op0=ALU.mult, op1=ALU.add)
                        tv = vS.inc(v.scalar_tensor_tensor(
                            zrow[:, 0:6], pC[:, 6, 0:6], 1.0,
                            asb[:, :, 64:65].squeeze(2),
                            op0=ALU.mult, op1=ALU.add))
                        v.wait_ge(vS.h, tv)
                    else:
                        v.tensor_copy(ctxf[:, :, :], asb[:, :, 0:64])
                        tv = vS.inc(v.tensor_copy(zrow[:, 0:6],
                                                  asb[:, :, 64:65].squeeze(2)))
                        v.wait_ge(vS.h, tv)
                    tv = vS.inc(v.tensor_scalar_add(zrow[:, 0:6], zrow[:, 0:6],
                                                    1e-6))
                    v.wait_ge(vS.h, tv)
                    tv = vS.inc(v.reciprocal(gstat[:, 0:6], zrow[:, 0:6]))
                    v.wait_ge(vS.h, tv)
                    v.tensor_tensor(
                        ctxs2[0:64, par, :, 0:64], ctxf[0:64, :, :],
                        gstat[0:64, 0:6].unsqueeze(2).broadcast_to([64, 6, 64]),
                        op=ALU.mult)
                    tv = vS.inc(v.tensor_tensor(
                        ctxs2[64:128, par, :, 64:128], ctxf[64:128, :, :],
                        gstat[64:128, 0:6].unsqueeze(2).broadcast_to([64, 6, 64]),
                        op=ALU.mult))
                    v_ctx[u] = vS.n
                    if odd:
                        v.wait_ge(vS.h, tv)
                        v.wait_ge(tS.h, t_updA[i])
                        v.scalar_tensor_tensor(
                            asb[0:64, :, 0:64], pG[0:64, 0:6, 0:64], 1.0,
                            asb[0:64, :, 0:64], op0=ALU.mult, op1=ALU.add)
                        v.scalar_tensor_tensor(
                            asb[64:128, :, 0:64], pG[64:128, 0:6, 64:128], 1.0,
                            asb[64:128, :, 0:64], op0=ALU.mult, op1=ALU.add)
                        vS.inc(v.scalar_tensor_tensor(
                            asb[:, :, 64:65].squeeze(2), pG[:, 6, 0:6], 1.0,
                            asb[:, :, 64:65].squeeze(2),
                            op0=ALU.mult, op1=ALU.add))
                        v_ctx[u] = vS.n

                @block.tensor
                def _(t, u=u, par=par, i=i, odd=odd):
                    t.wait_ge(vS.h, v_ctx[u])
                    if u >= 2:
                        t.wait_ge(sS.h, s_outg[u - 2])
                    cs = u * 64
                    for j in range(6):
                        ins = t.matmul(pog[:, par, j, :],
                                       ctxs2[:, par, j, :],
                                       qs_fm[:, j, cs:cs + 64],
                                       start=True, stop=True)
                    tS.inc(ins)
                    t_outg[u] = tS.n
                    if not odd:
                        for j in range(6):
                            t.matmul(pC[:, j, :],
                                     kg[:, i, j * 128:(j + 1) * 128],
                                     vstage[:, j * 128:(j + 1) * 128],
                                     start=True, stop=True)
                            ins = t.matmul(pC[:, 6, j:j + 1],
                                           kg[:, i, j * 128:(j + 1) * 128],
                                           onese[:, 0:1],
                                           start=True, stop=True)
                        tS.inc(ins)
                        t_updC[i] = tS.n
                        if i >= 1:
                            t.wait_ge(vS.h, v_ctx[u - 1])
                        for j in range(6):
                            t.matmul(pG[:, j, :],
                                     kg[:, i, j * 128:(j + 1) * 128],
                                     vg[:, i, j * 128:(j + 1) * 128],
                                     start=True, stop=True)
                            ins = t.matmul(pG[:, 6, j:j + 1],
                                           kg[:, i, j * 128:(j + 1) * 128],
                                           onesc[:, 0:1],
                                           start=True, stop=True)
                        tS.inc(ins)
                        t_updA[i] = tS.n

                @block.scalar
                def _(s, u=u, par=par):
                    s.wait_ge(tS.h, t_outg[u])
                    cs = u * 64
                    for j in range(6):
                        ins = s.copy(afm[:, 2 + j, cs:cs + 64],
                                     pog[:, par, j, :])
                    sS.inc(ins)
                    s_outg[u] = sS.n

        if finish_early('p6'):
            return nc

        _phase_scope("ph_p7")
        barrier_all()
        v_x2 = {}
        with nc.psum_tensor("po", [128, 2, 2, 512], F32) as po:
            t_o = {}
            rl_tgt = {}
            slo0, slo1 = proj_loads[("o", 0)], proj_loads[("o", 1)]
            for i in range(16):
                par = i % 2

                @block.sync
                def _(sy, i=i, par=par):
                    if i < 2:
                        sy.wait_ge(vS.h, kg_v[15])
                    if i >= 2:
                        sy.wait_ge(vS.h, v_x2[i - 2])
                    rl_tgt[i] = sd.dma(sy, x_roll[:, par, :],
                                       x1sp[i * 128:(i + 1) * 128, :])

                @block.tensor
                def _(t, i=i, par=par):
                    if i == 0:
                        t.wait_ge(gw.h, max(slo0[1], slo1[1]))
                    bh_last = i * 4 + 3
                    t.wait_ge(sS.h, o_done[bh_last])
                    t.wait_ge(sS.h, s_outg[2 * i + 1])
                    if i >= 2:
                        t.wait_ge(vS.h, v_x2[i - 2])
                    to = i * 128
                    for h, slh in ((0, slo0), (1, slo1)):
                        for c in range(NCH):
                            t.matmul(po[:, par, h, :],
                                     afm[:, c, to:to + 128],
                                     wbuf[slh[0]][:, c, :],
                                     start=(c == 0), stop=False)
                        ins = t.matmul(po[:, par, h, :],
                                       onesr[0:1, :],
                                       bo_sb[0:1, 512 * h:512 * h + 512],
                                       start=False, stop=True)
                    tS.inc(ins)
                    t_o[i] = tS.n

                @block.vector
                def _(v, i=i, par=par):
                    v.wait_ge(tS.h, t_o[i])
                    v.wait_ge(sd.h, rl_tgt[i])
                    v.tensor_tensor(x2[:, i, 0:512], po[:, par, 0, :],
                                    x_roll[:, par, 0:512], op=ALU.add)
                    vS.inc(v.tensor_tensor(x2[:, i, 512:1024], po[:, par, 1, :],
                                           x_roll[:, par, 512:1024],
                                           op=ALU.add))
                    v_x2[i] = vS.n

        for h in range(2):
            proj_loads[("o", h)][2][0] = tS
            proj_loads[("o", h)][2][1] = tS.n

        if STAGE == 2:
            dbgsel = os.environ.get("MK_DBG", "")
            if dbgsel:
                @block.vector
                def _(v):
                    v.wait_ge(vS.h, vS.n)
                    if dbgsel == "kv":
                        v.tensor_copy(scr_a[:, 0, :], kg[:, 0, :])
                        vS.inc(v.tensor_copy(scr_b[:, 0, 0:256], vg[:, 0, 0:256]))
                    elif dbgsel == "qs":
                        vS.inc(v.tensor_copy(
                            scr_a[:, :, :].rearrange("p a b -> p (a b)")[:, 0:768]
                            .rearrange("p (j t) -> p j t", t=128),
                            qs_fm[:, :, 0:128]))
                    elif dbgsel == "afm":
                        vS.inc(v.tensor_copy(
                            scr_a[:, :, :].rearrange("p a b -> p (a b)")[:, 0:768]
                            .rearrange("p (c t) -> p c t", t=128),
                            afm[:, 2:8, 64:192]))
                dbg_v = vS.n

                @block.gpsimd
                def _(g):
                    g.wait_ge(vS.h, dbg_v)
                    gw.dma(g, dbg_d[:, 0:768],
                           scr_a[:, :, :].rearrange("p a b -> p (a b)")[:, 0:768])
                    if dbgsel == "kv":
                        gw.dma(g, dbg_d[:, 768:1024], scr_b[:, 0, 0:256])

            @block.sync
            def _(sy):
                for i in range(16):
                    sy.wait_ge(vS.h, v_x2[i])
                    sd.dma(sy, out_d[i * 128:(i + 1) * 128, :], x2[:, i, :])
                if os.environ.get("MK_DBG") == "spill":
                    sd.dma(sy, dbg_d[:, :], x1sp[0:128, :])
                sy.wait_ge(sd.h, sd.n)
                sy.wait_ge(gw.h, gw.em)
            return nc

        wslot_rr[0] = 0
        ffn2_loads = plan_wloads(
            [((e, h), expT_d[e, :, 512 * h:512 * h + 512]) for e in range(NE)
             for h in range(2)])
        ffn2_plan = wload_plans[-1]

        _phase_scope("ph_p89")
        barrier_all()
        with nc.psum_tensor("pT8", [128, 2, NCH, 128], BF16) as pT8, \
             nc.psum_tensor("pY8", [128, 2, 2, 512], F32) as pY8, \
             nc.psum_tensor("pL8", [128, 2, 512], F32) as pL8:
            groups16 = list(range(16))
            _, xdone2 = emit_lnffn(
                groups16, lambda g: x2[:, g, :], afm, lambda g: g * 128,
                lambda g, h: x2[:, g, 512 * h:512 * h + 512],
                ffn2_loads, pY8, pL8, pT8,
                extra_s_wait=lambda g: (vS, v_x2[g]))
            emit_wload_body(ffn2_plan)

        @block.sync
        def _(sy):
            for i in range(16):
                sy.wait_ge(vS.h, xdone2[i])
                sd.dma(sy, out_d[i * 128:(i + 1) * 128, :], x2[:, i, :])
            sy.wait_ge(sd.h, sd.n)
            sy.wait_ge(gw.h, gw.n)

        return nc


    return nc


def _host_prep(L0_attn_x, L0_attn_Wq, L0_attn_Wk, L0_attn_Wv, L0_attn_Wo,
               L0_attn_bo, L0_ffn_gate_w, L0_ffn_expert_w):
    x = np.asarray(L0_attn_x, np.float32).reshape(16384, 1024)
    xpad = np.concatenate([np.zeros((HALO, D), np.float32), x], 0)
    wq = np.ascontiguousarray(np.asarray(L0_attn_Wq, np.float32))
    wk = np.ascontiguousarray(np.asarray(L0_attn_Wk, np.float32))
    wv = np.ascontiguousarray(np.asarray(L0_attn_Wv, np.float32))
    wo = np.ascontiguousarray(np.asarray(L0_attn_Wo, np.float32))
    expT = np.ascontiguousarray(
        np.asarray(L0_ffn_expert_w, np.float32).transpose(0, 2, 1))
    gT = np.ascontiguousarray(np.asarray(L0_ffn_gate_w, np.float32).T)
    bo = np.asarray(L0_attn_bo, np.float32).reshape(1, D)
    ident = np.eye(128, dtype=np.float32)
    onesc = np.ones((128, 1), np.float32)

    ii = np.arange(128)[:, None]
    jj = np.arange(256)[None, :]
    mrest = np.where((128 + ii) < jj, MASKVAL, 0.0).astype(np.float32)
    mfirst_core0 = mrest.copy()
    mfirst_core0[:, :128] = MASKVAL

    in_maps = []
    for c in range(NCORE):
        mfirst = mfirst_core0 if c == 0 else mrest
        prefm = np.tile((np.arange(NE) < c).astype(np.float32), (128, 1))
        in_maps.append({
            "xin": np.ascontiguousarray(xpad[c * TOK:c * TOK + TTOK]),
            "wq": wq, "wk": wk, "wv": wv, "wo": wo,
            "expT": expT, "gT": gT, "bo": bo,
            "ident": ident, "onesc": onesc,
            "onesr": np.ones((1, 128), np.float32),
            "onese": np.concatenate([np.ones((64, 1), np.float32),
                                     np.zeros((64, 1), np.float32)]),
            "masks": np.ascontiguousarray(
                np.stack([mfirst, mrest]).astype(np.float32)),
            "prefm": np.ascontiguousarray(prefm),
        })
    return in_maps


def kernel(L0_attn_x, L0_attn_Wq, L0_attn_Wk, L0_attn_Wv, L0_attn_Wo,
           L0_attn_bo, L0_ffn_gate_w, L0_ffn_expert_w):
    in_maps = _host_prep(L0_attn_x, L0_attn_Wq, L0_attn_Wk, L0_attn_Wv,
                         L0_attn_Wo, L0_attn_bo, L0_ffn_gate_w, L0_ffn_expert_w)
    nc = build()
    res = run_bass_kernel_spmd(nc, in_maps, core_ids=list(range(NCORE)))
    out = np.concatenate([r["out"] for r in res.results], 0)
    return out.reshape(1, 16384, 1024)


